# version 18
# baseline (speedup 1.0000x reference)
import contextlib
import os

os.environ.setdefault("GOMEMLIMIT", "26GiB")

import numpy as np
import ml_dtypes

import concourse.bass as bass
import concourse.mybir as mybir
from concourse.bass_utils import run_bass_kernel_spmd
from concourse.alu_op_type import AluOpType as ALU

AF = mybir.ActivationFunctionType
AX = mybir.AxisListType
F32 = mybir.dt.float32
BF16 = mybir.dt.bfloat16
U8 = mybir.dt.uint8
BF = ml_dtypes.bfloat16


class Cfg:
    def __init__(self, S=2048, D=2048, NH=16, RANK=512, NOPE=128, ROPE=64,
                 VD=128, MLP=16384, TC=512, NSEQ=2, NCORE=8):
        self.S, self.D, self.NH, self.RANK = S, D, NH, RANK
        self.NOPE, self.ROPE, self.VD, self.MLP = NOPE, ROPE, VD, MLP
        self.TC, self.NSEQ, self.NCORE = TC, NSEQ, NCORE
        self.KT = D // 128
        self.NMB = MLP // 128
        self.NQ = 8
        self.MBQ = self.NMB // self.NQ
        self.NCH = S // TC
        self.RPC = TC // 128
        self.HT = D // 128
        self.CJ = RANK // 128
        self.NHT = NH * VD // 128
        self.QK = NOPE + ROPE
        self.SCALE = self.QK ** -0.5
        self.THETA = 10000.0
        self.DC = min(512, MLP)
        self.COMPACT = (D == 2048 and MLP == 16384 and S == 2048)
        assert TC % 128 == 0 and S % TC == 0 and D % 128 == 0
        assert MLP % (128 * self.NQ) == 0 and RANK % 128 == 0



def _deint(n):
    return np.concatenate([np.arange(0, n, 2), np.arange(1, n, 2)])


def host_prepare(cfg, L0_attn_x, L0_attn_wq, L0_attn_wkv_a, L0_attn_wkv_b,
                 L0_attn_wo, L0_attn_kv_norm_w, L0_ffn_gate_W_packed,
                 L0_ffn_gate_scales, L0_ffn_up_W_packed, L0_ffn_up_scales,
                 L0_ffn_down_W_packed, L0_ffn_down_scales):
    c = cfg
    perm_k = _deint(c.D)
    perm_m = _deint(c.MLP)

    x = np.asarray(L0_attn_x, dtype=np.float32)
    xp = np.ascontiguousarray(x[:, :, perm_k])

    wq = np.asarray(L0_attn_wq, dtype=np.float32).reshape(c.NH, c.QK, c.D)
    wq_r = np.concatenate(
        [wq[:, :c.NOPE], wq[:, c.NOPE::2], wq[:, c.NOPE + 1::2]], axis=1)
    wq_t = np.ascontiguousarray(wq_r[:, :, perm_k].transpose(0, 2, 1))

    wkv_a = np.asarray(L0_attn_wkv_a, dtype=np.float32)
    wkv_a_r = np.concatenate(
        [wkv_a[:c.RANK], wkv_a[c.RANK::2], wkv_a[c.RANK + 1::2]], axis=0)
    wkv_a_t = np.ascontiguousarray(wkv_a_r[:, perm_k].T)

    wkv_b = np.asarray(L0_attn_wkv_b, dtype=np.float32).reshape(
        c.NH, c.NOPE + c.VD, c.RANK)
    wkvb_k = np.ascontiguousarray(wkv_b[:, :c.NOPE, :])
    wkvb_vT = np.ascontiguousarray(wkv_b[:, c.NOPE:, :].transpose(0, 2, 1))

    wo = np.asarray(L0_attn_wo, dtype=np.float32)
    wo_t = np.ascontiguousarray(wo[perm_k, :].T)

    kvnw = np.ascontiguousarray(
        np.asarray(L0_attn_kv_norm_w, dtype=np.float32).reshape(c.CJ, 128).T)

    def pack_t(w_packed, row_perm, col_perm=None):
        a = np.asarray(w_packed)
        assert a.dtype == np.int32
        u = a.view(np.uint8)[:, ::4]
        if row_perm is not None:
            u = u[row_perm]
        if col_perm is not None:
            u = u[col_perm]
        return np.ascontiguousarray(u.T)

    gpk_t = pack_t(L0_ffn_gate_W_packed, perm_m)
    upk_t = pack_t(L0_ffn_up_W_packed, perm_m)
    dpk_t = pack_t(L0_ffn_down_W_packed, None, col_perm=perm_k)

    gsc_t = np.ascontiguousarray(
        np.asarray(L0_ffn_gate_scales, dtype=np.float32)[perm_m].T)
    usc_t = np.ascontiguousarray(
        np.asarray(L0_ffn_up_scales, dtype=np.float32)[perm_m].T)
    dsc_t = np.ascontiguousarray(
        np.asarray(L0_ffn_down_scales, dtype=np.float32)[perm_k].T)

    ident = np.eye(128, dtype=BF)
    ident32 = np.eye(128, dtype=np.float32)
    ones = np.ones((128, 128), dtype=BF)
    masks = np.zeros((128, c.TC), dtype=np.float32)
    for i in range(128):
        masks[i, i + 1:] = -1e30
    masks = masks.astype(BF)
    freqs = 1.0 / c.THETA ** (
        np.arange(0, c.ROPE, 2, dtype=np.float32) / c.ROPE)
    ang = np.outer(np.arange(c.S, dtype=np.float32), freqs)
    cos_t = np.ascontiguousarray(np.cos(ang).T).astype(BF)
    sin_t = np.ascontiguousarray(np.sin(ang).T).astype(BF)

    shared = dict(
        wq_t=wq_t, wkv_a_t=wkv_a_t, wkvb_k=wkvb_k, wkvb_vT=wkvb_vT,
        wo_t=wo_t, kvnw=kvnw, gpk_t=gpk_t, upk_t=upk_t, dpk_t=dpk_t,
        gsc_t=gsc_t, usc_t=usc_t, dsc_t=dsc_t, ident=ident, ident32=ident32,
        ones=ones, masks=masks, cos_t=cos_t, sin_t=sin_t)
    return xp, shared



class Sem:
    def __init__(self, handle, name):
        self.h = handle
        self.name = name
        self.n = 0


class Plan:
    def __init__(self, drains=True):
        self.drains = drains
        self.ops = {k: [] for k in
                    ("sync", "vector", "scalar", "tensor", "gpsimd")}
        self.last_data = {k: False for k in self.ops}
        self.waited = {k: {} for k in self.ops}

    def emit(self, eng, f, data=False):
        self.ops[eng].append(f)
        self.last_data[eng] = data

    def wait(self, eng, sem, val=None):
        v = sem.n if val is None else val
        if v <= self.waited[eng].get(id(sem), 0):
            return
        self.waited[eng][id(sem)] = v
        self.emit(eng, lambda e, s=sem.h, v=v: e.wait_ge(s, v))

    def dma(self, eng, sem, out, in_, inc=16):
        sem.n += inc
        self.emit(eng, lambda e, o=out, i=in_, s=sem.h, k=inc:
                  e.dma_start(out=o, in_=i).then_inc(s, k), data=True)

    def op(self, eng, f, sem=None, inc=1):
        if sem is not None:
            sem.n += inc
            self.emit(eng, lambda e, s=sem.h, k=inc, ff=f:
                      ff(e).then_inc(s, k), data=True)
        else:
            self.emit(eng, f, data=True)

    def drain(self, eng, sem=None, inc=1):
        elide = (not self.drains) and eng == "vector"
        if sem is None:
            if not elide:
                self.emit(eng, lambda e: e.drain())
            return
        sem.n += inc
        if not elide or not self.last_data[eng]:
            self.emit(eng, lambda e, s=sem.h, k=inc: e.drain().then_inc(s, k))
        else:
            last = self.ops[eng][-1]

            def wrapped(e, f=last, s=sem.h, k=inc):
                r = f(e)
                r.then_inc(s, k)
                return r
            self.ops[eng][-1] = wrapped
            self.last_data[eng] = False



def build_nc(cfg, phases=("ffn1", "attn", "ffn2"), drains=True):
    c = cfg
    nc = bass.Bass("TRN2", target_bir_lowering=False, debug=False,
                   detect_race_conditions=drains)
    st = contextlib.ExitStack()

    def din(name, shape, dt):
        return nc.dram_tensor(name, list(shape), dt, kind="ExternalInput").ap()

    x_d = din("x2", (c.NSEQ, c.S, c.D), F32)
    wq_d = din("wq_t", (c.NH, c.D, c.QK), F32)
    wkva_d = din("wkv_a_t", (c.D, c.RANK + c.ROPE), F32)
    wkbk_d = din("wkvb_k", (c.NH, c.NOPE, c.RANK), F32)
    wkbv_d = din("wkvb_vT", (c.NH, c.RANK, c.VD), F32)
    wo_d = din("wo_t", (c.NH * c.VD, c.D), F32)
    kvnw_d = din("kvnw", (128, c.CJ), F32)
    gpk_d = din("gpk_t", (c.D // 2, c.MLP), U8)
    upk_d = din("upk_t", (c.D // 2, c.MLP), U8)
    dpk_d = din("dpk_t", (c.MLP // 2, c.D), U8)
    gsc_d = din("gsc_t", (c.D // 32, c.MLP), F32)
    usc_d = din("usc_t", (c.D // 32, c.MLP), F32)
    dsc_d = din("dsc_t", (c.MLP // 32, c.D), F32)
    iden_d = din("ident", (128, 128), BF16)
    iden32_d = din("ident32", (128, 128), F32)
    ones_d = din("ones", (128, 128), BF16)
    mask_d = din("masks", (128, c.TC), BF16)
    cos_d = din("cos_t", (c.ROPE // 2, c.S), BF16)
    sin_d = din("sin_t", (c.ROPE // 2, c.S), BF16)
    y_d = nc.dram_tensor("y", [c.NSEQ, c.S, c.D], F32,
                         kind="ExternalOutput").ap()

    def scr(name, shape, dt):
        return nc.dram_tensor(name, list(shape), dt, kind="Internal").ap()

    gsc_s = scr("gsc_s", (c.D // 2, c.MLP), F32)
    usc_s = scr("usc_s", (c.D // 2, c.MLP), F32)
    dsc_s = scr("dsc_s", (c.MLP // 2, c.D), F32)
    wg_s = scr("wg_s", (c.NMB, 128, c.KT, 128), BF16)
    wu_s = scr("wu_s", (c.NMB, 128, c.KT, 128), BF16)
    wd_s = scr("wd_s", (c.HT, c.NMB, 128, 128), BF16)
    wqa_s = scr("wqa_s", (c.NH, 128, c.KT, c.QK), BF16)
    wkva_s = scr("wkva_s", (128, c.KT, c.RANK + c.ROPE), BF16)
    wkbk_s = scr("wkbk_s", (c.NH, 128, c.CJ * 128), BF16)
    wkbv_s = scr("wkbv_s", (c.NH, 128, c.CJ * c.VD), BF16)
    wo_s = scr("wo_s", (c.HT, 128, c.NHT, 128), BF16)

    base0 = (nc.sbuf_base + 31) // 32 * 32
    cur = [base0]
    offs = {}

    def _nb(shape, dt):
        n = 1
        for d in shape[1:]:
            n *= d
        sz = {F32: 4, BF16: 2, U8: 1}[dt]
        return ((n * sz + 31) // 32) * 32

    sizes = {}

    def sb(name, shape, dt, at=None, spill=False):
        nb = _nb(shape, dt)
        off = None
        if at is not None and c.COMPACT:
            host, delta = at
            if spill or delta + nb <= sizes[host]:
                off = offs[host] + delta
        if off is None:
            off = cur[0]
            cur[0] += nb
        offs[name] = off
        sizes[name] = nb
        return nc.alloc_sbuf_tensor_at(name, list(shape), dt, offset=off)

    NFT = c.D // 128
    stream = sb("stream", (128, NFT, c.S), BF16)
    xn = sb("xn", (128, NFT, c.TC), BF16)
    interq = sb("interq", (128, c.MBQ, c.TC), BF16)
    outp = sb("outp", (128, c.HT, c.TC), F32)
    wbuf_g = sb("wbuf_g", (128, 2, 2, c.KT, 128), BF16)
    wbuf_u = sb("wbuf_u", (128, 2, 2, c.KT, 128), BF16)
    gel = sb("gel", (128, 2, c.TC), BF16)
    sqb = sb("sqb", (128, 2, c.TC), BF16)
    mean_rep = sb("mean_rep", (128, c.TC), BF16)
    rstd_rep = sb("rstd_rep", (128, c.TC), BF16)
    stat = sb("stat", (33, c.TC), F32)
    statb = sb("statb", (1, c.TC), BF16)
    statc = sb("statc", (1, c.TC), BF16)
    res_t = sb("res_t", (128, 2, c.TC), F32)
    tstage = sb("tstage", (128, 2, 128), F32)
    rmax = sb("rmax", (128, 8), F32)
    psum_r = sb("psum_r", (128, 4), F32)
    identb = sb("identb", (128, 128), BF16)
    ident32b = sb("ident32b", (128, 128), F32)
    onesb = sb("onesb", (128, 128), BF16)
    maskb = sb("maskb", (128, c.TC), BF16)
    cosb = sb("cosb", (c.ROPE // 2, c.S), BF16)
    sinb = sb("sinb", (c.ROPE // 2, c.S), BF16)
    kvnwb = sb("kvnwb", (128, c.CJ), F32)
    DCB = c.DC
    pkb = sb("pkb", (128, 2, DCB), U8)
    scb = sb("scb", (128, 2, DCB), F32)
    d_u8 = sb("d_u8", (128, 2 * DCB), U8)
    d_f2 = sb("d_f2", (128, 2 * DCB), BF16)
    d_ge8 = sb("d_ge8", (128, 2 * DCB), BF16)
    d_m = sb("d_m", (128, 2 * DCB), BF16)
    d_t2 = sb("d_t2", (128, 2 * DCB), BF16)
    d_a = sb("d_a", (128, 2 * DCB), BF16)
    d_val = sb("d_val", (128, 1, 2 * DCB), BF16)
    d_pl = sb("d_pl", (128, 1, 2, DCB), BF16)
    statm2 = sb("statm2", (1, c.TC), F32, at=("mean_rep", 0), spill=True)
    dwbuf = sb("dwbuf", (128, 2, c.MBQ, 128), BF16, at=("wbuf_g", 0))
    wc16 = sb("wc16", (128, 2, 2048), BF16, at=("pkb", 0), spill=True)
    xstage = sb("xstage", (128, c.D), F32, at=("interq", 0), spill=True)
    xstage_b = sb("xstage_b", (128, c.D), BF16,
                  at=("interq", 4 * c.D), spill=True)
    kvT = sb("kvT", (128, c.CJ, c.S), BF16, at=("outp", 0), spill=True)
    pT = sb("pT", (128, c.S // 128, c.TC), BF16,
            at=("outp", 2 * c.CJ * c.S), spill=True)
    kvlat = sb("kvlat", (128, c.CJ, c.TC), BF16,
               at=("outp", 2 * c.CJ * c.S), spill=True)
    kvtok = sb("kvtok", (128, c.S // 128, c.RANK), BF16,
               at=("pkb", 0), spill=True)
    headso = sb("headso", (128, c.NHT, c.TC), BF16,
                at=("interq", 0), spill=True)
    prow = sb("prow", (128, 2, c.S), BF16, at=("wbuf_g", 0), spill=True)
    kpeT = sb("kpeT", (c.ROPE, c.S), BF16, at=("wbuf_g", 4 * c.S),
              spill=True)
    qlatT = sb("qlatT", (128, c.CJ, c.TC), BF16,
               at=("wbuf_g", 6 * c.S), spill=True)
    outlatT = sb("outlatT", (128, c.CJ, c.TC), BF16,
                 at=("mean_rep", 0), spill=True)
    wqh_b = sb("wqh_b", (128, 2, c.KT, c.QK), BF16, at=("wbuf_u", 0))
    qn_h = sb("qn_h", (128, c.TC), BF16, at=("wbuf_u", 4 * c.KT * c.QK))
    qpe_h = sb("qpe_h", (c.ROPE, c.TC), BF16,
               at=("wbuf_u", 4 * c.KT * c.QK + 2 * c.TC))
    wkbk_b = sb("wkbk_b", (128, 2, c.CJ * 128), BF16,
                at=("wbuf_u", 4 * c.KT * c.QK + 4 * c.TC))
    wkva_b = sb("wkva_b", (128, 2, c.KT, 128), BF16, at=("wqh_b", 0))
    wo_b = sb("wo_b", (128, 2, c.NHT, 128), BF16, at=("wqh_b", 0))
    wkbv_b = sb("wkbv_b", (128, 2, c.CJ * c.VD), BF16, at=("gel", 0))
    ropet = sb("ropet", (c.ROPE // 2, 2, c.TC), F32, at=("res_t", 0))
    nc.bump_sbuf(cur[0] - nc.sbuf_base)
    pG0 = st.enter_context(nc.psum_tensor("pG0", [128, 512], F32))
    pG1 = st.enter_context(nc.psum_tensor("pG1", [128, 512], F32))
    pU0 = st.enter_context(nc.psum_tensor("pU0", [128, 512], F32))
    pU1 = st.enter_context(nc.psum_tensor("pU1", [128, 512], F32))
    pD0 = st.enter_context(nc.psum_tensor("pD0", [128, 512], F32))
    pD1 = st.enter_context(nc.psum_tensor("pD1", [128, 512], F32))
    pR = st.enter_context(nc.psum_tensor("pR", [128, 512], F32))
    pTB = st.enter_context(nc.psum_tensor("pTB", [128, 1024], BF16))
    pG = [pG0, pG1]
    pU = [pU0, pU1]
    pD = [pD0, pD1]
    pS = [pG0, pG1, pU0, pU1]
    tbank_list = [pD0, pD1, pR]

    def sem(name):
        return Sem(st.enter_context(nc.semaphore(name)), name)

    s_c = sem("s_c")
    s_g = sem("s_g")
    s_g2 = sem("s_g2")
    s_g3 = sem("s_g3")
    s_g4 = sem("s_g4")
    tbanks = tbank_list
    gq_sems = [s_g, s_g2, s_g3, s_g4]
    gq_idx = [0]

    def gdma(out, in_):
        s = gq_sems[gq_idx[0] % 4]
        gq_idx[0] += 1
        P.wait("gpsimd", s)
        P.dma("gpsimd", s, out, in_)

    def gq_snapshot():
        return [s.n for s in gq_sems]

    def gq_wait(eng, snap=None):
        for s, v in zip(gq_sems, snap or [s.n for s in gq_sems]):
            P.wait(eng, s, v)
    s_wg = [sem("s_wg0"), sem("s_wg1")]
    s_wu = [sem("s_wu0"), sem("s_wu1")]
    s_wd = [sem("s_wd0"), sem("s_wd1")]
    s_wa = [sem("s_wa0"), sem("s_wa1")]
    s_wb = [sem("s_wb0"), sem("s_wb1")]
    s_pk = [sem("s_pk0"), sem("s_pk1")]
    s_gp = sem("s_gp")
    s_pe = sem("s_pe")
    s_v = sem("s_v")
    s_s = sem("s_s")
    s_y = [sem("s_y0"), sem("s_y1"), sem("s_y2"), sem("s_y3")]

    P = Plan(drains=drains)

    def wstream_barrier(eng="sync"):
        for s in (s_wg[0], s_wg[1], s_wu[0], s_wu[1], s_wd[0], s_wd[1],
                  s_wa[0], s_wa[1], s_wb[0], s_wb[1]):
            P.wait(eng, s)
        P.wait(eng, s_pe)
        P.wait(eng, s_v)


    def mm_transpose(dst_ap, src_ap, ck, idx):
        sl = idx % 3
        bank = tbanks[sl]
        P.wait("tensor", s_v, ck[sl])
        P.op("tensor", lambda e, b=bank, s=src_ap: e.matmul(
            b[:, 0:128], s, identb[:], start=True, stop=True), sem=s_pe)
        P.wait("vector", s_pe)
        P.op("vector", lambda e, b=bank, d=dst_ap: e.tensor_copy(
            out=d, in_=b[:, 0:128]), sem=s_v)
        ck[sl] = s_v.n

    P.dma("scalar", s_c, identb[:], iden_d[:])
    P.dma("scalar", s_c, ident32b[:], iden32_d[:])
    P.dma("scalar", s_c, onesb[:], ones_d[:])
    P.dma("scalar", s_c, maskb[:], mask_d[:])
    P.dma("scalar", s_c, cosb[:], cos_d[:])
    P.dma("scalar", s_c, sinb[:], sin_d[:])
    P.dma("scalar", s_c, kvnwb[:], kvnw_d[:])

    for (src_, dst) in ((gsc_d, gsc_s), (usc_d, usc_s), (dsc_d, dsc_s)):
        for i in range(16):
            gdma(dst.rearrange("(u k) m -> u k m", k=16)[:, i, :], src_[:])

    cast_jobs = []
    for h in range(c.NH):
        for kt in range(c.KT):
            cast_jobs.append((wq_d[h, kt * 128:(kt + 1) * 128, :],
                              wqa_s[h, :, kt, :], c.QK))
    for kt in range(c.KT):
        cast_jobs.append((wkva_d[kt * 128:(kt + 1) * 128, :],
                          wkva_s[:, kt, :], c.RANK + c.ROPE))
    for h in range(c.NH):
        cast_jobs.append((wkbk_d[h], wkbk_s[h], c.CJ * 128))
        cast_jobs.append((wkbv_d[h].rearrange("(j p) v -> p j v", p=128),
                          wkbv_s[h], c.CJ * c.VD))
    for ft in range(c.HT):
        cast_jobs.append((
            wo_d.rearrange("(t p) f -> p t f", p=128)[
                :, :, ft * 128:(ft + 1) * 128],
            wo_s[ft], c.NHT * 128))
    for j, (src_ap, dst_ap, ncol) in enumerate(cast_jobs):
        par = j % 2
        if j >= 2:
            gq_wait("gpsimd")
        gdma(wc16[:, par, :ncol], src_ap)
        gq_wait("gpsimd")
        gdma(dst_ap, wc16[:, par, :ncol])

    def decode_tile(par, dc):
        pk = pkb[:, par, :dc]
        w = 2 * dc
        P.op("vector", lambda e, pk=pk, dc=dc: e.tensor_scalar(
            out=d_u8[:, :dc], in0=pk, scalar1=15, scalar2=None,
            op0=ALU.bitwise_and))
        P.op("vector", lambda e, pk=pk, dc=dc: e.tensor_scalar(
            out=d_u8[:, dc:2 * dc], in0=pk, scalar1=4, scalar2=None,
            op0=ALU.logical_shift_right))
        P.drain("vector")
        P.op("vector", lambda e, w=w: e.tensor_copy(
            out=d_f2[:, :w], in_=d_u8[:, :w]))
        P.drain("vector")
        P.op("vector", lambda e, w=w: e.tensor_scalar(
            out=d_ge8[:, :w], in0=d_f2[:, :w], scalar1=8.0, scalar2=None,
            op0=ALU.is_ge))
        P.drain("vector")
        P.op("vector", lambda e, w=w: e.scalar_tensor_tensor(
            out=d_m[:, :w], in0=d_ge8[:, :w], scalar=-8.0, in1=d_f2[:, :w],
            op0=ALU.mult, op1=ALU.add))
        P.drain("vector", s_v)
        P.wait("scalar", s_v)
        P.op("scalar", lambda e, w=w: e.activation(
            out=d_t2[:, :w], in_=d_m[:, :w], func=AF.Copy,
            scale=3.0, bias=-15.0), sem=s_s)
        P.wait("vector", s_s)
        P.op("vector", lambda e, w=w: e.scalar_tensor_tensor(
            out=d_a[:, :w], in0=d_m[:, :w], scalar=0.5, in1=d_t2[:, :w],
            op0=ALU.mult, op1=ALU.max))
        P.drain("vector")
        P.op("vector", lambda e, w=w: e.scalar_tensor_tensor(
            out=d_t2[:, :w], in0=d_m[:, :w], scalar=-2.0, in1=d_a[:, :w],
            op0=ALU.add, op1=ALU.max))
        P.drain("vector")
        P.op("vector", lambda e, w=w: e.scalar_tensor_tensor(
            out=d_m[:, :w], in0=d_ge8[:, :w], scalar=-2.0,
            in1=d_t2[:, :w], op0=ALU.mult, op1=ALU.mult))
        P.drain("vector")
        if gp_done:
            P.wait("vector", s_gp, gp_done[-1])
        P.op("vector", lambda e, w=w: e.tensor_tensor(
            out=d_val[:, 0, :w], in0=d_t2[:, :w], in1=d_m[:, :w],
            op=ALU.add))
        P.drain("vector", s_v)
        P.wait("gpsimd", s_v)
        if g_done:
            gq_wait("gpsimd", g_done[-1])
        P.op("gpsimd", lambda e, dc=dc, par=par: e.tensor_tensor(
            out=d_pl[:, 0, 0, :dc], in0=d_val[:, 0, :dc],
            in1=scb[:, par, :dc], op=ALU.mult))
        P.op("gpsimd", lambda e, dc=dc, par=par: e.tensor_tensor(
            out=d_pl[:, 0, 1, :dc], in0=d_val[:, 0, dc:2 * dc],
            in1=scb[:, par, :dc], op=ALU.mult))
        P.drain("gpsimd", s_gp)

    v_done = []
    gp_done = []
    g_done = []
    eighth_ck = {}

    def decode_one(pk_src, sc_src, j, cc, dc, write_plane):
        ti = len(v_done)
        par = ti % 2
        if ti >= 2:
            P.wait("sync", s_gp, gp_done[ti - 2])
        P.dma("sync", s_pk[par], pkb[:, par, :dc],
              pk_src[j * 128:(j + 1) * 128, cc * dc:(cc + 1) * dc])
        P.dma("sync", s_pk[par], scb[:, par, :dc],
              sc_src[j * 128:(j + 1) * 128, cc * dc:(cc + 1) * dc])
        P.wait("vector", s_pk[par])
        decode_tile(par, dc)
        P.drain("vector", s_v)
        v_done.append(s_v.n)
        write_plane(j, cc, par)
        g_done.append(gq_snapshot())
        gp_done.append(s_gp.n)

    def wg_write(which_s):
        def w(j, cc, par):
            dc = c.DC
            for pi, kt in ((0, j), (1, c.KT // 2 + j)):
                nmb = dc // 128
                gdma(which_s[cc * nmb:(cc + 1) * nmb, :, kt, :].rearrange(
                         "b p m -> p b m"),
                     d_pl[:, 0, pi, :dc])
        return w

    def wd_write(j, cc, par):
        dc = c.DC
        for pi, mb in ((0, j), (1, c.NMB // 2 + j)):
            nht = dc // 128
            gdma(wd_s[cc * nht:(cc + 1) * nht, mb, :, :].rearrange(
                     "t p h -> p t h"),
                 d_pl[:, 0, pi, :dc])

    def decode_gen():
        gw = wg_write(wg_s)
        uw = wg_write(wu_s)
        ncc_g = c.DC // 128
        done_g, done_u = set(), set()
        for q8 in range(c.NQ):
            mb_lo, mb_hi = q8 * c.MBQ, (q8 + 1) * c.MBQ
            ccs = range(mb_lo // ncc_g, -(-mb_hi // ncc_g))
            for cc in ccs:
                if cc in done_g:
                    continue
                done_g.add(cc)
                for j in range(c.D // 256):
                    decode_one(gpk_d, gsc_s, j, cc, c.DC, gw)
                    yield 1
            for cc in ccs:
                if cc in done_u:
                    continue
                done_u.add(cc)
                for j in range(c.D // 256):
                    decode_one(upk_d, usc_s, j, cc, c.DC, uw)
                    yield 1
            if q8 < c.NQ // 2:
                for j in range(mb_lo, mb_hi):
                    for cc in range(c.D // c.DC):
                        decode_one(dpk_d, dsc_s, j, cc, c.DC, wd_write)
                        yield 1
            eighth_ck[q8] = (gq_snapshot(), s_gp.n, s_v.n)

    gq_wait("sync")
    bg = decode_gen()

    def pump(n=1):
        for _ in range(n):
            if next(bg, None) is None:
                return

    def need_eighth(q8):
        while q8 not in eighth_ck:
            if next(bg, None) is None:
                break

    def layer_norm(T0):
        P.wait("scalar", s_v)
        P.wait("vector", s_pe)
        P.wait("tensor", s_v)
        ln_ck = []
        for ft in range(NFT):
            par = ft % 2
            if ft >= 2:
                P.wait("scalar", s_pe, ln_ck[ft - 2])
            P.op("scalar", lambda e, ft=ft, par=par, T0=T0: e.activation(
                out=sqb[:, par, :], in_=stream[:, ft, T0:T0 + c.TC],
                func=AF.Square), sem=s_s)
            P.wait("tensor", s_s)
            P.op("tensor", lambda e, ft=ft, T0=T0: e.matmul(
                pR[0:1, 0:c.TC], onesb[:, 0:1], stream[:, ft, T0:T0 + c.TC],
                start=(ft == 0), stop=(ft == NFT - 1)))
            P.op("tensor", lambda e, par=par, ft=ft: e.matmul(
                pR[32:33, 0:c.TC], onesb[:, 0:1], sqb[:, par, :],
                start=(ft == 0), stop=(ft == NFT - 1)), sem=s_pe)
            ln_ck.append(s_pe.n)
        P.wait("vector", s_pe)
        P.op("vector", lambda e: e.tensor_scalar(
            out=stat[0:1, :], in0=pR[0:1, 0:c.TC], scalar1=1.0 / c.D,
            scalar2=None, op0=ALU.mult))
        P.op("vector", lambda e: e.tensor_scalar(
            out=stat[32:33, :], in0=pR[32:33, 0:c.TC], scalar1=1.0 / c.D,
            scalar2=None, op0=ALU.mult))
        P.drain("vector")
        P.op("vector", lambda e: e.tensor_tensor(
            out=statb[0:1, :], in0=stat[0:1, :], in1=stat[0:1, :],
            op=ALU.mult))
        P.op("vector", lambda e: e.tensor_copy(
            out=statm2[0:1, :], in_=stat[32:33, :]))
        P.drain("vector")
        P.op("vector", lambda e: e.tensor_tensor(
            out=statm2[0:1, :], in0=statm2[0:1, :], in1=statb[0:1, :],
            op=ALU.subtract))
        P.drain("vector")
        P.op("vector", lambda e: e.tensor_scalar(
            out=statm2[0:1, :], in0=statm2[0:1, :], scalar1=1e-5,
            scalar2=None, op0=ALU.add))
        P.drain("vector", s_v)
        P.wait("scalar", s_v)
        P.op("scalar", lambda e: e.activation(
            out=statm2[0:1, :], in_=statm2[0:1, :], func=AF.Sqrt), sem=s_s)
        P.wait("vector", s_s)
        P.op("vector", lambda e: e.reciprocal(
            out=statm2[0:1, :], in_=statm2[0:1, :]))
        P.drain("vector")
        P.op("vector", lambda e: e.tensor_copy(
            out=statb[0:1, :], in_=stat[0:1, :]))
        P.op("vector", lambda e: e.tensor_copy(
            out=statc[0:1, :], in_=statm2[0:1, :]))
        P.drain("vector", s_v)
        P.wait("tensor", s_v)
        P.op("tensor", lambda e: e.matmul(
            pR[:, 0:c.TC], onesb[0:1, :], statb[0:1, :],
            start=True, stop=True), sem=s_pe)
        P.wait("vector", s_pe)
        P.op("vector", lambda e: e.tensor_copy(
            out=mean_rep[:], in_=pR[:, 0:c.TC]))
        P.drain("vector", s_v)
        P.wait("tensor", s_v)
        P.op("tensor", lambda e: e.matmul(
            pR[:, 0:c.TC], onesb[0:1, :], statc[0:1, :],
            start=True, stop=True), sem=s_pe)
        P.wait("vector", s_pe)
        P.op("vector", lambda e: e.tensor_copy(
            out=rstd_rep[:], in_=pR[:, 0:c.TC]))
        P.drain("vector")
        for ft in range(NFT):
            P.op("vector", lambda e, ft=ft, T0=T0: e.scalar_tensor_tensor(
                out=xn[:, ft, :], in0=stream[:, ft, T0:T0 + c.TC],
                scalar=1.0, in1=mean_rep[:], op0=ALU.mult, op1=ALU.subtract))
        P.drain("vector")
        for ft in range(NFT):
            P.op("vector", lambda e, ft=ft: e.tensor_tensor(
                out=xn[:, ft, :], in0=xn[:, ft, :], in1=rstd_rep[:],
                op=ALU.mult))
        P.drain("vector", s_v)
        P.wait("tensor", s_v)
        P.wait("scalar", s_v)

    def ffn(sseq, final):
        for ch in range(c.NCH):
            T0 = ch * c.TC
            layer_norm(T0)
            for q in range(c.NQ):
                evac_ck = []
                need_eighth(q)
                for g2 in range(c.MBQ // 2):
                    par = g2 % 2
                    mbg0 = q * c.MBQ + g2 * 2
                    if g2 >= 2:
                        P.wait("sync", s_v, evac_ck[2 * (g2 - 2) + 1])
                    if g2 == 0:
                        wstream_barrier()
                        gq_wait("sync", eighth_ck[q][0])
                    P.dma("sync", s_wg[par], wbuf_g[:, par],
                          wg_s[mbg0:mbg0 + 2].rearrange("b p k m -> p b k m"))
                    P.dma("sync", s_wg[par], wbuf_u[:, par],
                          wu_s[mbg0:mbg0 + 2].rearrange("b p k m -> p b k m"))
                    for bi in range(2):
                        mb = g2 * 2 + bi
                        par2 = mb % 2
                        P.wait("tensor", s_wg[par])
                        if len(evac_ck) >= 2:
                            P.wait("tensor", s_v, evac_ck[-2])
                        for kt in range(c.KT):
                            P.op("tensor", lambda e, par=par, bi=bi, kt=kt,
                                 par2=par2: e.matmul(
                                     pG[par2][:, 0:c.TC],
                                     wbuf_g[:, par, bi, kt, :],
                                     xn[:, kt, :], start=(kt == 0),
                                     stop=(kt == c.KT - 1)),
                                 sem=(s_pe if kt == c.KT - 1 else None))
                        for kt in range(c.KT):
                            P.op("tensor", lambda e, par=par, bi=bi, kt=kt,
                                 par2=par2: e.matmul(
                                     pU[par2][:, 0:c.TC],
                                     wbuf_u[:, par, bi, kt, :],
                                     xn[:, kt, :], start=(kt == 0),
                                     stop=(kt == c.KT - 1)),
                                 sem=(s_pe if kt == c.KT - 1 else None))
                        P.wait("scalar", s_pe, s_pe.n - 1)
                        P.op("scalar", lambda e, par2=par2: e.activation(
                            out=gel[:, par2, :], in_=pG[par2][:, 0:c.TC],
                            func=AF.Gelu_apprx_tanh), sem=s_s)
                        P.wait("vector", s_s)
                        P.wait("vector", s_pe)
                        P.op("vector", lambda e, mb=mb, par2=par2:
                             e.tensor_tensor(
                                 out=interq[:, mb, :], in0=gel[:, par2, :],
                                 in1=pU[par2][:, 0:c.TC], op=ALU.mult),
                             sem=s_v)
                        evac_ck.append(s_v.n)
                        pump(1)
                dn_ck = []
                for ht in range(c.HT):
                    par = ht % 2
                    if ht >= 2:
                        P.wait("sync", s_v, dn_ck[ht - 2])
                    else:
                        P.wait("sync", s_v, evac_ck[-1])
                    P.dma("sync", s_wd[par], dwbuf[:, par],
                          wd_s[ht, q * c.MBQ:(q + 1) * c.MBQ].rearrange(
                              "b p h -> p b h"))
                    P.wait("tensor", s_wd[par])
                    P.wait("tensor", s_v, evac_ck[-1])
                    if ht >= 2:
                        P.wait("tensor", s_v, dn_ck[ht - 2])
                    pump(1)
                    for mb in range(c.MBQ):
                        P.op("tensor", lambda e, par=par, mb=mb: e.matmul(
                            pD[par][:, 0:c.TC], dwbuf[:, par, mb, :],
                            interq[:, mb, :], start=(mb == 0),
                            stop=(mb == c.MBQ - 1)),
                            sem=(s_pe if mb == c.MBQ - 1 else None))
                    P.wait("vector", s_pe)
                    if q == 0:
                        P.op("vector", lambda e, par=par, ht=ht:
                             e.tensor_copy(out=outp[:, ht, :],
                                           in_=pD[par][:, 0:c.TC]), sem=s_v)
                    elif q < c.NQ - 1:
                        P.op("vector", lambda e, par=par, ht=ht:
                             e.tensor_tensor(
                                 out=outp[:, ht, :], in0=outp[:, ht, :],
                                 in1=pD[par][:, 0:c.TC], op=ALU.add),
                             sem=s_v)
                    else:
                        P.op("vector", lambda e, par=par, ht=ht:
                             e.tensor_tensor(
                                 out=outp[:, ht, :], in0=outp[:, ht, :],
                                 in1=pD[par][:, 0:c.TC], op=ALU.add))
                        P.drain("vector")
                        if not final:
                            P.op("vector", lambda e, ht=ht, T0=T0:
                                 e.scalar_tensor_tensor(
                                     out=stream[:, ht, T0:T0 + c.TC],
                                     in0=outp[:, ht, :], scalar=0.5,
                                     in1=stream[:, ht, T0:T0 + c.TC],
                                     op0=ALU.mult, op1=ALU.add), sem=s_v)
                        else:
                            rp = ht % 2
                            P.op("vector", lambda e, ht=ht, T0=T0, rp=rp:
                                 e.scalar_tensor_tensor(
                                     out=res_t[:, rp, :],
                                     in0=outp[:, ht, :], scalar=0.5,
                                     in1=stream[:, ht, T0:T0 + c.TC],
                                     op0=ALU.mult, op1=ALU.add), sem=s_v)
                            for b in range(c.RPC):
                                tp = b % 2
                                P.wait("tensor", s_v)
                                P.wait("tensor", s_y[tp])
                                P.op("tensor", lambda e, rp=rp, b=b:
                                     e.transpose(
                                         pR[:, 0:128],
                                         res_t[:, rp, b * 128:(b + 1) * 128],
                                         ident32b[:]), sem=s_pe)
                                P.wait("vector", s_pe)
                                P.op("vector", lambda e, tp=tp:
                                     e.tensor_copy(out=tstage[:, tp, :],
                                                   in_=pR[:, 0:128]),
                                     sem=s_v)
                                P.wait("scalar", s_v)
                                P.dma("scalar", s_y[tp],
                                      y_d[sseq,
                                          T0 + b * 128:T0 + (b + 1) * 128,
                                          ht * 128:(ht + 1) * 128],
                                      tstage[:, tp, :])
                    dn_ck.append(s_v.n)

    def rope_apply(dst0, dst1, src_ps, Tsl):
        H = c.ROPE // 2
        cs = cosb[:, Tsl]
        sn = sinb[:, Tsl]
        P.op("vector", lambda e: e.tensor_tensor(
            out=ropet[:, 0, :], in0=src_ps[0:H, 0:c.TC], in1=cs, op=ALU.mult))
        P.op("vector", lambda e: e.tensor_tensor(
            out=ropet[:, 1, :], in0=src_ps[H:2 * H, 0:c.TC], in1=sn,
            op=ALU.mult))
        P.drain("vector")
        P.op("vector", lambda e, d=dst0: e.tensor_tensor(
            out=d, in0=ropet[:, 0, :], in1=ropet[:, 1, :], op=ALU.subtract))
        P.drain("vector")
        P.op("vector", lambda e: e.tensor_tensor(
            out=ropet[:, 0, :], in0=src_ps[0:H, 0:c.TC], in1=sn, op=ALU.mult))
        P.op("vector", lambda e: e.tensor_tensor(
            out=ropet[:, 1, :], in0=src_ps[H:2 * H, 0:c.TC], in1=cs,
            op=ALU.mult))
        P.drain("vector")
        P.op("vector", lambda e, d=dst1: e.tensor_tensor(
            out=d, in0=ropet[:, 0, :], in1=ropet[:, 1, :], op=ALU.add))
        P.drain("vector", s_v)

    def attention(sseq):
        for _ in bg:
            pass
        gq_wait("vector")
        P.wait("vector", s_gp)
        pt_slot_ck = [0, 0, 0]
        for qc in range(c.NCH):
            T0 = qc * c.TC
            Tsl = slice(T0, T0 + c.TC)
            layer_norm(T0)
            wstream_barrier()
            kv_ck = []
            for rt in range(c.CJ + 1):
                par = rt % 2
                ncol = 128 if rt < c.CJ else c.ROPE
                if rt >= 2:
                    P.wait("sync", s_pe, kv_ck[rt - 1])
                P.dma("sync", s_wa[par], wkva_b[:, par, :, 0:ncol],
                      wkva_s[:, :, rt * 128:rt * 128 + ncol])
                P.wait("tensor", s_wa[par])
                if rt >= 2:
                    P.wait("tensor", s_v)
                    P.wait("tensor", s_s)
                for kt in range(c.KT):
                    P.op("tensor", lambda e, par=par, kt=kt, nc_=ncol:
                         e.matmul(pD[par][0:nc_, 0:c.TC],
                                  wkva_b[:, par, kt, 0:nc_],
                                  xn[:, kt, :], start=(kt == 0),
                                  stop=(kt == c.KT - 1)),
                         sem=(s_pe if kt == c.KT - 1 else None))
                kv_ck.append(s_pe.n)
                if rt < c.CJ:
                    P.wait("vector", s_pe)
                    P.op("vector", lambda e, par=par, rt=rt: e.tensor_copy(
                        out=kvlat[:, rt, :], in_=pD[par][:, 0:c.TC]))
                    P.drain("vector", s_v)
                    P.wait("scalar", s_v)
                    P.op("scalar", lambda e, par=par, rt=rt: e.activation(
                        out=sqb[:, rt % 2, :], in_=pD[par][:, 0:c.TC],
                        func=AF.Square), sem=s_s)
                    P.wait("tensor", s_s)
                    P.op("tensor", lambda e, rt=rt: e.matmul(
                        pR[0:1, 0:c.TC], onesb[:, 0:1], sqb[:, rt % 2, :],
                        start=(rt == 0), stop=(rt == c.CJ - 1)), sem=s_pe)
                else:
                    P.wait("vector", s_pe)
                    rope_apply(kpeT[0:c.ROPE // 2, Tsl],
                               kpeT[c.ROPE // 2:c.ROPE, Tsl], pD[par], Tsl)
            P.wait("vector", s_pe)
            P.op("vector", lambda e: e.tensor_scalar(
                out=statm2[0:1, :], in0=pR[0:1, 0:c.TC],
                scalar1=1.0 / c.RANK, scalar2=1e-6, op0=ALU.mult,
                op1=ALU.add))
            P.drain("vector", s_v)
            P.wait("scalar", s_v)
            P.op("scalar", lambda e: e.activation(
                out=statm2[0:1, :], in_=statm2[0:1, :], func=AF.Sqrt),
                sem=s_s)
            P.wait("vector", s_s)
            P.op("vector", lambda e: e.reciprocal(
                out=statm2[0:1, :], in_=statm2[0:1, :]))
            P.drain("vector")
            P.op("vector", lambda e: e.tensor_copy(
                out=statc[0:1, :], in_=statm2[0:1, :]))
            P.drain("vector", s_v)
            P.wait("tensor", s_v)
            P.op("tensor", lambda e: e.matmul(
                pR[:, 0:c.TC], onesb[0:1, :], statc[0:1, :],
                start=True, stop=True), sem=s_pe)
            P.wait("vector", s_pe)
            P.op("vector", lambda e: e.tensor_copy(
                out=rstd_rep[:], in_=pR[:, 0:c.TC]))
            P.drain("vector")
            for cj in range(c.CJ):
                P.op("vector", lambda e, cj=cj, Tsl=Tsl:
                     e.scalar_tensor_tensor(
                         out=kvT[:, cj, Tsl], in0=kvlat[:, cj, :],
                         scalar=kvnwb[:, cj:cj + 1], in1=rstd_rep[:],
                         op0=ALU.mult, op1=ALU.mult))
            P.drain("vector", s_v)
            slot_ck = [0, 0, 0]
            P.wait("tensor", s_v)
            for cj in range(c.CJ):
                for b in range(c.RPC):
                    mm_transpose(
                        kvtok[:, qc * c.RPC + b, cj * 128:(cj + 1) * 128],
                        kvT[:, cj, T0 + b * 128:T0 + (b + 1) * 128],
                        slot_ck, cj * c.RPC + b)
            head_ck = []
            for h in range(c.NH):
                parh = h % 2
                if h == 0:
                    wstream_barrier()
                if h >= 2:
                    P.wait("sync", s_pe, head_ck[h - 1])
                P.dma("sync", s_wa[parh], wqh_b[:, parh], wqa_s[h])
                P.dma("sync", s_wg[parh], wkbk_b[:, parh], wkbk_s[h])
                P.dma("sync", s_wu[parh], wkbv_b[:, parh], wkbv_s[h])
                P.wait("tensor", s_wa[parh])
                P.wait("tensor", s_v)
                for kt in range(c.KT):
                    P.op("tensor", lambda e, parh=parh, kt=kt: e.matmul(
                        pD0[:, 0:c.TC], wqh_b[:, parh, kt, 0:c.NOPE],
                        xn[:, kt, :], start=(kt == 0),
                        stop=(kt == c.KT - 1)))
                for kt in range(c.KT):
                    P.op("tensor", lambda e, parh=parh, kt=kt: e.matmul(
                        pD1[0:c.ROPE, 0:c.TC],
                        wqh_b[:, parh, kt, c.NOPE:c.QK],
                        xn[:, kt, :], start=(kt == 0),
                        stop=(kt == c.KT - 1)),
                        sem=(s_pe if kt == c.KT - 1 else None))
                head_ck.append(s_pe.n)
                P.wait("vector", s_pe)
                P.op("vector", lambda e: e.tensor_copy(
                    out=qn_h[:], in_=pD0[:, 0:c.TC]))
                P.drain("vector", s_v)
                rope_apply(qpe_h[0:c.ROPE // 2, :],
                           qpe_h[c.ROPE // 2:c.ROPE, :], pD1, Tsl)
                P.wait("tensor", s_v)
                P.wait("tensor", s_wg[parh])
                for cj in range(c.CJ):
                    if cj >= 2:
                        P.wait("tensor", s_v)
                    P.op("tensor", lambda e, cj=cj, parh=parh: e.matmul(
                        pD[cj % 2][:, 0:c.TC], wkbk_b[
                            :, parh, cj * 128:(cj + 1) * 128], qn_h[:],
                        start=True, stop=True), sem=s_pe)
                    P.wait("vector", s_pe)
                    P.op("vector", lambda e, cj=cj: e.tensor_copy(
                        out=qlatT[:, cj, :], in_=pD[cj % 2][:, 0:c.TC]),
                        sem=s_v)
                def emit_row_tr(r, R, rp, div_ck):
                    P.wait("tensor", s_v, div_ck)
                    for j0 in range(0, R + 1, 4):
                        jn = min(4, R + 1 - j0)
                        sl = (j0 // 4) % 3
                        bank = tbank_list[sl]
                        P.wait("tensor", s_v, pt_slot_ck[sl])
                        for ji in range(jn):
                            P.op("tensor", lambda e, b=bank, j=j0 + ji,
                                 ji=ji, rp=rp: e.matmul(
                                     b[:, ji * 128:(ji + 1) * 128],
                                     prow[:, rp, j * 128:(j + 1) * 128],
                                     identb[:], start=True, stop=True),
                                 sem=(s_pe if ji == jn - 1 else None))
                        P.wait("vector", s_pe)
                        P.op("vector", lambda e, b=bank, j0=j0, jn=jn, r=r:
                             e.tensor_copy(
                                 out=pT[:, j0:j0 + jn,
                                        r * 128:(r + 1) * 128],
                                 in_=b[:, 0:jn * 128]), sem=s_v)
                        pt_slot_ck[sl] = s_v.n

                pending = None
                for r in range(c.RPC):
                    R = qc * c.RPC + r
                    ncc = qc + 1
                    rsl = slice(r * 128, (r + 1) * 128)
                    rp = r % 2
                    P.wait("tensor", s_v)
                    P.wait("tensor", s_s)
                    for cc in range(ncc):
                        ksl = slice(cc * c.TC, (cc + 1) * c.TC)
                        for cj in range(c.CJ):
                            P.op("tensor", lambda e, cc=cc, cj=cj, rsl=rsl,
                                 ksl=ksl: e.matmul(
                                     pS[cc][:, 0:c.TC], qlatT[:, cj, rsl],
                                     kvT[:, cj, ksl], start=(cj == 0),
                                     stop=False))
                        P.op("tensor", lambda e, cc=cc, rsl=rsl, ksl=ksl:
                             e.matmul(pS[cc][:, 0:c.TC], qpe_h[:, rsl],
                                      kpeT[:, ksl], start=False, stop=True),
                             sem=s_pe)
                    if pending is not None:
                        emit_row_tr(*pending)
                    P.wait("vector", s_pe)
                    P.op("vector", lambda e, r=r, qc=qc: e.tensor_tensor(
                        out=pS[qc][:, r * 128:c.TC],
                        in0=pS[qc][:, r * 128:c.TC],
                        in1=maskb[:, 0:c.TC - r * 128], op=ALU.add))
                    P.drain("vector")
                    for cc in range(ncc):
                        P.op("vector", lambda e, cc=cc: e.reduce_max(
                            out=rmax[:, cc:cc + 1], in_=pS[cc][:, 0:c.TC],
                            axis=AX.X))
                    P.drain("vector")
                    for cc in range(1, ncc):
                        P.op("vector", lambda e, cc=cc: e.tensor_tensor(
                            out=rmax[:, 0:1], in0=rmax[:, 0:1],
                            in1=rmax[:, cc:cc + 1], op=ALU.max))
                        P.drain("vector")
                    P.op("vector", lambda e: e.tensor_scalar(
                        out=rmax[:, 4:5], in0=rmax[:, 0:1],
                        scalar1=-c.SCALE, scalar2=None, op0=ALU.mult))
                    P.drain("vector", s_v)
                    P.wait("scalar", s_v)
                    for cc in range(ncc):
                        P.op("scalar", lambda e, cc=cc, rp=rp: e.activation(
                            out=prow[:, rp, cc * c.TC:(cc + 1) * c.TC],
                            in_=pS[cc][:, 0:c.TC], func=AF.Exp,
                            bias=rmax[:, 4:5], scale=c.SCALE,
                            accum_out=psum_r[:, cc:cc + 1]),
                            sem=(s_s if cc == ncc - 1 else None))
                    P.wait("vector", s_s)
                    for cc in range(1, ncc):
                        P.op("vector", lambda e, cc=cc: e.tensor_tensor(
                            out=psum_r[:, 0:1], in0=psum_r[:, 0:1],
                            in1=psum_r[:, cc:cc + 1], op=ALU.add))
                        P.drain("vector")
                    P.op("vector", lambda e: e.reciprocal(
                        out=psum_r[:, 0:1], in_=psum_r[:, 0:1]))
                    P.drain("vector")
                    P.op("vector", lambda e, rp=rp, w=ncc * c.TC:
                         e.tensor_scalar(
                             out=prow[:, rp, 0:w], in0=prow[:, rp, 0:w],
                             scalar1=psum_r[:, 0:1], scalar2=None,
                             op0=ALU.mult))
                    P.drain("vector", s_v)
                    pending = (r, R, rp, s_v.n)
                emit_row_tr(*pending)
                for r in range(c.RPC):
                    R = qc * c.RPC + r
                    for j in range(R + 1, (qc + 1) * c.RPC):
                        P.op("vector", lambda e, j=j, r=r: e.memset(
                            pT[:, j, r * 128:(r + 1) * 128], 0.0))
                P.drain("vector", s_v)
                P.wait("tensor", s_v)
                P.wait("tensor", s_s)
                NJT = (qc + 1) * c.RPC
                for cj in range(c.CJ):
                    for j in range(NJT):
                        P.op("tensor", lambda e, cj=cj, j=j, NJT=NJT:
                             e.matmul(
                                 pS[cj][:, 0:c.TC],
                                 kvtok[:, j, cj * 128:(cj + 1) * 128],
                                 pT[:, j, :], start=(j == 0),
                                 stop=(j == NJT - 1)),
                             sem=(s_pe if j == NJT - 1 else None))
                    P.wait("vector", s_pe)
                    P.op("vector", lambda e, cj=cj: e.tensor_copy(
                        out=outlatT[:, cj, :], in_=pS[cj][:, 0:c.TC]),
                        sem=s_v)
                P.wait("tensor", s_v)
                P.wait("tensor", s_wu[parh])
                for cj in range(c.CJ):
                    P.op("tensor", lambda e, cj=cj, parh=parh: e.matmul(
                        pR[0:c.VD, 0:c.TC],
                        wkbv_b[:, parh, cj * c.VD:(cj + 1) * c.VD],
                        outlatT[:, cj, :], start=(cj == 0),
                        stop=(cj == c.CJ - 1)),
                        sem=(s_pe if cj == c.CJ - 1 else None))
                P.wait("vector", s_pe)
                P.op("vector", lambda e, h=h: e.tensor_copy(
                    out=headso[:, h, :], in_=pR[0:c.VD, 0:c.TC]), sem=s_v)
            wo_ck = []
            for ft in range(c.HT):
                par = ft % 2
                if ft == 0:
                    wstream_barrier()
                if ft >= 2:
                    P.wait("sync", s_pe, wo_ck[ft - 1])
                P.dma("sync", s_wb[par], wo_b[:, par], wo_s[ft])
                P.wait("tensor", s_wb[par])
                P.wait("tensor", s_v)
                for ht in range(c.NHT):
                    P.op("tensor", lambda e, par=par, ht=ht: e.matmul(
                        pD[par][:, 0:c.TC], wo_b[:, par, ht, :],
                        headso[:, ht, :], start=(ht == 0),
                        stop=(ht == c.NHT - 1)),
                        sem=(s_pe if ht == c.NHT - 1 else None))
                wo_ck.append(s_pe.n)
                P.wait("vector", s_pe)
                P.op("vector", lambda e, par=par, ft=ft, T0=T0:
                     e.tensor_tensor(
                         out=stream[:, ft, T0:T0 + c.TC],
                         in0=stream[:, ft, T0:T0 + c.TC],
                         in1=pD[par][:, 0:c.TC], op=ALU.add), sem=s_v)

    def store_stream(sseq):
        for ch in range(c.NCH):
            T0 = ch * c.TC
            for ht in range(c.HT):
                rp = ht % 2
                P.wait("vector", s_pe)
                P.wait("vector", s_s)
                P.op("vector", lambda e, ht=ht, T0=T0, rp=rp: e.tensor_copy(
                    out=res_t[:, rp, :], in_=stream[:, ht, T0:T0 + c.TC]),
                    sem=s_v)
                for b in range(c.RPC):
                    tp = b % 2
                    P.wait("tensor", s_v)
                    P.wait("tensor", s_y[tp])
                    P.op("tensor", lambda e, rp=rp, b=b: e.transpose(
                        pR[:, 0:128], res_t[:, rp, b * 128:(b + 1) * 128],
                        ident32b[:]), sem=s_pe)
                    P.wait("vector", s_pe)
                    P.op("vector", lambda e, tp=tp: e.tensor_copy(
                        out=tstage[:, tp, :], in_=pR[:, 0:128]), sem=s_v)
                    P.wait("scalar", s_v)
                    P.dma("scalar", s_y[tp],
                          y_d[sseq, T0 + b * 128:T0 + (b + 1) * 128,
                              ht * 128:(ht + 1) * 128], tstage[:, tp, :])

    def load_x(sseq):
        slot_ck = [0, 0, 0]
        for tr in range(c.S // 128):
            P.wait("scalar", s_v)
            P.dma("scalar", s_c, xstage[:],
                  x_d[sseq, tr * 128:(tr + 1) * 128])
            P.wait("vector", s_c)
            P.wait("vector", s_pe)
            P.op("vector", lambda e: e.tensor_copy(
                out=xstage_b[:], in_=xstage[:]))
            P.drain("vector", s_v)
            P.wait("tensor", s_v)
            for ft in range(NFT):
                mm_transpose(
                    stream[:, ft, tr * 128:(tr + 1) * 128],
                    xstage_b[:, ft * 128:(ft + 1) * 128], slot_ck, ft)

    P.wait("scalar", s_c)
    for sseq in range(c.NSEQ):
        load_x(sseq)
        for pi_, ph in enumerate(phases):
            last = pi_ == len(phases) - 1
            if ph in ("ffn1", "ffn2"):
                ffn(sseq, final=last)
            elif ph == "attn":
                attention(sseq)
                if last:
                    store_stream(sseq)
            elif ph == "none":
                store_stream(sseq)
    for _ in bg:
        pass
    for tp in range(4):
        P.wait("sync", s_y[tp])
    P.wait("sync", s_c)

    with nc.Block() as block:
        @block.sync
        def _(eng):
            for f in P.ops["sync"]:
                f(eng)

        @block.vector
        def _(eng):
            for f in P.ops["vector"]:
                f(eng)

        @block.scalar
        def _(eng):
            for f in P.ops["scalar"]:
                f(eng)

        @block.tensor
        def _(eng):
            for f in P.ops["tensor"]:
                f(eng)

        @block.gpsimd
        def _(eng):
            for f in P.ops["gpsimd"]:
                f(eng)

    return nc, st



def kernel(**inputs):
    cfg = Cfg()
    xp, shared = host_prepare(cfg, **inputs)
    nc, st = build_nc(cfg, drains=True)
    in_maps = []
    for core in range(cfg.NCORE):
        m = dict(x2=np.ascontiguousarray(
            xp[core * cfg.NSEQ:(core + 1) * cfg.NSEQ]))
        m.update(shared)
        in_maps.append(m)
    res = run_bass_kernel_spmd(nc, in_maps, core_ids=list(range(cfg.NCORE)))
    st.close()
    inv = np.argsort(_deint(cfg.D))
    y = np.concatenate([r["y"] for r in res.results], axis=0)
    return np.ascontiguousarray(y[:, :, inv])


# version 19
# speedup vs baseline: 1.0314x; 1.0314x over previous
import contextlib
import os

os.environ.setdefault("GOMEMLIMIT", "26GiB")

import numpy as np
import ml_dtypes

import concourse.bass as bass
import concourse.mybir as mybir
from concourse.bass_utils import run_bass_kernel_spmd
from concourse.alu_op_type import AluOpType as ALU

AF = mybir.ActivationFunctionType
AX = mybir.AxisListType
F32 = mybir.dt.float32
BF16 = mybir.dt.bfloat16
U8 = mybir.dt.uint8
BF = ml_dtypes.bfloat16


class Cfg:
    def __init__(self, S=2048, D=2048, NH=16, RANK=512, NOPE=128, ROPE=64,
                 VD=128, MLP=16384, TC=512, NSEQ=2, NCORE=8):
        self.S, self.D, self.NH, self.RANK = S, D, NH, RANK
        self.NOPE, self.ROPE, self.VD, self.MLP = NOPE, ROPE, VD, MLP
        self.TC, self.NSEQ, self.NCORE = TC, NSEQ, NCORE
        self.KT = D // 128
        self.NMB = MLP // 128
        self.NQ = 8
        self.MBQ = self.NMB // self.NQ
        self.NCH = S // TC
        self.RPC = TC // 128
        self.HT = D // 128
        self.CJ = RANK // 128
        self.NHT = NH * VD // 128
        self.QK = NOPE + ROPE
        self.SCALE = self.QK ** -0.5
        self.THETA = 10000.0
        self.DC = min(512, MLP)
        self.COMPACT = (D == 2048 and MLP == 16384 and S == 2048)
        assert TC % 128 == 0 and S % TC == 0 and D % 128 == 0
        assert MLP % (128 * self.NQ) == 0 and RANK % 128 == 0



def _deint(n):
    return np.concatenate([np.arange(0, n, 2), np.arange(1, n, 2)])


def host_prepare(cfg, L0_attn_x, L0_attn_wq, L0_attn_wkv_a, L0_attn_wkv_b,
                 L0_attn_wo, L0_attn_kv_norm_w, L0_ffn_gate_W_packed,
                 L0_ffn_gate_scales, L0_ffn_up_W_packed, L0_ffn_up_scales,
                 L0_ffn_down_W_packed, L0_ffn_down_scales):
    c = cfg
    perm_k = _deint(c.D)
    perm_m = _deint(c.MLP)

    x = np.asarray(L0_attn_x, dtype=np.float32)
    xp = np.ascontiguousarray(x[:, :, perm_k])

    wq = np.asarray(L0_attn_wq, dtype=np.float32).reshape(c.NH, c.QK, c.D)
    wq_r = np.concatenate(
        [wq[:, :c.NOPE], wq[:, c.NOPE::2], wq[:, c.NOPE + 1::2]], axis=1)
    wq_t = np.ascontiguousarray(wq_r[:, :, perm_k].transpose(0, 2, 1))

    wkv_a = np.asarray(L0_attn_wkv_a, dtype=np.float32)
    wkv_a_r = np.concatenate(
        [wkv_a[:c.RANK], wkv_a[c.RANK::2], wkv_a[c.RANK + 1::2]], axis=0)
    wkv_a_t = np.ascontiguousarray(wkv_a_r[:, perm_k].T)

    wkv_b = np.asarray(L0_attn_wkv_b, dtype=np.float32).reshape(
        c.NH, c.NOPE + c.VD, c.RANK)
    wkvb_k = np.ascontiguousarray(wkv_b[:, :c.NOPE, :])
    wkvb_vT = np.ascontiguousarray(wkv_b[:, c.NOPE:, :].transpose(0, 2, 1))

    wo = np.asarray(L0_attn_wo, dtype=np.float32)
    wo_t = np.ascontiguousarray(wo[perm_k, :].T)

    kvnw = np.ascontiguousarray(
        np.asarray(L0_attn_kv_norm_w, dtype=np.float32).reshape(c.CJ, 128).T)

    def pack_t(w_packed, row_perm, col_perm=None):
        a = np.asarray(w_packed)
        assert a.dtype == np.int32
        u = a.view(np.uint8)[:, ::4]
        if row_perm is not None:
            u = u[row_perm]
        if col_perm is not None:
            u = u[col_perm]
        return np.ascontiguousarray(u.T)

    gpk_t = pack_t(L0_ffn_gate_W_packed, perm_m)
    upk_t = pack_t(L0_ffn_up_W_packed, perm_m)
    dpk_t = pack_t(L0_ffn_down_W_packed, None, col_perm=perm_k)

    gsc_t = np.ascontiguousarray(
        np.asarray(L0_ffn_gate_scales, dtype=np.float32)[perm_m].T)
    usc_t = np.ascontiguousarray(
        np.asarray(L0_ffn_up_scales, dtype=np.float32)[perm_m].T)
    dsc_t = np.ascontiguousarray(
        np.asarray(L0_ffn_down_scales, dtype=np.float32)[perm_k].T)

    ident = np.eye(128, dtype=BF)
    ident32 = np.eye(128, dtype=np.float32)
    ones = np.ones((128, 128), dtype=BF)
    masks = np.zeros((128, c.TC), dtype=np.float32)
    for i in range(128):
        masks[i, i + 1:] = -1e30
    masks = masks.astype(BF)
    freqs = 1.0 / c.THETA ** (
        np.arange(0, c.ROPE, 2, dtype=np.float32) / c.ROPE)
    ang = np.outer(np.arange(c.S, dtype=np.float32), freqs)
    cos_t = np.ascontiguousarray(np.cos(ang).T).astype(BF)
    sin_t = np.ascontiguousarray(np.sin(ang).T).astype(BF)

    shared = dict(
        wq_t=wq_t, wkv_a_t=wkv_a_t, wkvb_k=wkvb_k, wkvb_vT=wkvb_vT,
        wo_t=wo_t, kvnw=kvnw, gpk_t=gpk_t, upk_t=upk_t, dpk_t=dpk_t,
        gsc_t=gsc_t, usc_t=usc_t, dsc_t=dsc_t, ident=ident, ident32=ident32,
        ones=ones, masks=masks, cos_t=cos_t, sin_t=sin_t)
    return xp, shared



class Sem:
    def __init__(self, handle, name):
        self.h = handle
        self.name = name
        self.n = 0


class Plan:
    def __init__(self, drains=True):
        self.drains = drains
        self.ops = {k: [] for k in
                    ("sync", "vector", "scalar", "tensor", "gpsimd")}
        self.last_data = {k: False for k in self.ops}
        self.waited = {k: {} for k in self.ops}

    def emit(self, eng, f, data=False):
        self.ops[eng].append(f)
        self.last_data[eng] = data

    def wait(self, eng, sem, val=None):
        v = sem.n if val is None else val
        if v <= self.waited[eng].get(id(sem), 0):
            return
        self.waited[eng][id(sem)] = v
        self.emit(eng, lambda e, s=sem.h, v=v: e.wait_ge(s, v))

    def dma(self, eng, sem, out, in_, inc=16):
        sem.n += inc
        self.emit(eng, lambda e, o=out, i=in_, s=sem.h, k=inc:
                  e.dma_start(out=o, in_=i).then_inc(s, k), data=True)

    def op(self, eng, f, sem=None, inc=1):
        if sem is not None:
            sem.n += inc
            self.emit(eng, lambda e, s=sem.h, k=inc, ff=f:
                      ff(e).then_inc(s, k), data=True)
        else:
            self.emit(eng, f, data=True)

    def drain(self, eng, sem=None, inc=1):
        elide = (not self.drains) and eng == "vector"
        if sem is None:
            if not elide:
                self.emit(eng, lambda e: e.drain())
            return
        sem.n += inc
        if not elide or not self.last_data[eng]:
            self.emit(eng, lambda e, s=sem.h, k=inc: e.drain().then_inc(s, k))
        else:
            last = self.ops[eng][-1]

            def wrapped(e, f=last, s=sem.h, k=inc):
                r = f(e)
                r.then_inc(s, k)
                return r
            self.ops[eng][-1] = wrapped
            self.last_data[eng] = False



def build_nc(cfg, phases=("ffn1", "attn", "ffn2"), drains=True):
    c = cfg
    nc = bass.Bass("TRN2", target_bir_lowering=False, debug=False,
                   detect_race_conditions=drains)
    st = contextlib.ExitStack()

    def din(name, shape, dt):
        return nc.dram_tensor(name, list(shape), dt, kind="ExternalInput").ap()

    x_d = din("x2", (c.NSEQ, c.S, c.D), F32)
    wq_d = din("wq_t", (c.NH, c.D, c.QK), F32)
    wkva_d = din("wkv_a_t", (c.D, c.RANK + c.ROPE), F32)
    wkbk_d = din("wkvb_k", (c.NH, c.NOPE, c.RANK), F32)
    wkbv_d = din("wkvb_vT", (c.NH, c.RANK, c.VD), F32)
    wo_d = din("wo_t", (c.NH * c.VD, c.D), F32)
    kvnw_d = din("kvnw", (128, c.CJ), F32)
    gpk_d = din("gpk_t", (c.D // 2, c.MLP), U8)
    upk_d = din("upk_t", (c.D // 2, c.MLP), U8)
    dpk_d = din("dpk_t", (c.MLP // 2, c.D), U8)
    gsc_d = din("gsc_t", (c.D // 32, c.MLP), F32)
    usc_d = din("usc_t", (c.D // 32, c.MLP), F32)
    dsc_d = din("dsc_t", (c.MLP // 32, c.D), F32)
    iden_d = din("ident", (128, 128), BF16)
    iden32_d = din("ident32", (128, 128), F32)
    ones_d = din("ones", (128, 128), BF16)
    mask_d = din("masks", (128, c.TC), BF16)
    cos_d = din("cos_t", (c.ROPE // 2, c.S), BF16)
    sin_d = din("sin_t", (c.ROPE // 2, c.S), BF16)
    y_d = nc.dram_tensor("y", [c.NSEQ, c.S, c.D], F32,
                         kind="ExternalOutput").ap()

    def scr(name, shape, dt):
        return nc.dram_tensor(name, list(shape), dt, kind="Internal").ap()

    gsc_s = scr("gsc_s", (c.D // 2, c.MLP), F32)
    usc_s = scr("usc_s", (c.D // 2, c.MLP), F32)
    dsc_s = scr("dsc_s", (c.MLP // 2, c.D), F32)
    wg_s = scr("wg_s", (c.NMB, 128, c.KT, 128), BF16)
    wu_s = scr("wu_s", (c.NMB, 128, c.KT, 128), BF16)
    wd_s = scr("wd_s", (c.HT, c.NMB, 128, 128), BF16)
    wqa_s = scr("wqa_s", (c.NH, 128, c.KT, c.QK), BF16)
    wkva_s = scr("wkva_s", (128, c.KT, c.RANK + c.ROPE), BF16)
    wkbk_s = scr("wkbk_s", (c.NH, 128, c.CJ * 128), BF16)
    wkbv_s = scr("wkbv_s", (c.NH, 128, c.CJ * c.VD), BF16)
    wo_s = scr("wo_s", (c.HT, 128, c.NHT, 128), BF16)

    base0 = (nc.sbuf_base + 31) // 32 * 32
    cur = [base0]
    offs = {}

    def _nb(shape, dt):
        n = 1
        for d in shape[1:]:
            n *= d
        sz = {F32: 4, BF16: 2, U8: 1}[dt]
        return ((n * sz + 31) // 32) * 32

    sizes = {}

    def sb(name, shape, dt, at=None, spill=False):
        nb = _nb(shape, dt)
        off = None
        if at is not None and c.COMPACT:
            host, delta = at
            if spill or delta + nb <= sizes[host]:
                off = offs[host] + delta
        if off is None:
            off = cur[0]
            cur[0] += nb
        offs[name] = off
        sizes[name] = nb
        return nc.alloc_sbuf_tensor_at(name, list(shape), dt, offset=off)

    NFT = c.D // 128
    stream = sb("stream", (128, NFT, c.S), BF16)
    xn = sb("xn", (128, NFT, c.TC), BF16)
    interq = sb("interq", (128, c.MBQ, c.TC), BF16)
    outp = sb("outp", (128, c.HT, c.TC), F32)
    wbuf_g = sb("wbuf_g", (128, 2, 2, c.KT, 128), BF16)
    wbuf_u = sb("wbuf_u", (128, 2, 2, c.KT, 128), BF16)
    gel = sb("gel", (128, 2, c.TC), BF16)
    sqb = sb("sqb", (128, 2, c.TC), BF16)
    mean_rep = sb("mean_rep", (128, c.TC), BF16)
    rstd_rep = sb("rstd_rep", (128, c.TC), BF16)
    stat = sb("stat", (33, c.TC), F32)
    statb = sb("statb", (1, c.TC), BF16)
    statc = sb("statc", (1, c.TC), BF16)
    res_t = sb("res_t", (128, 2, c.TC), F32)
    tstage = sb("tstage", (128, 2, 128), F32)
    rmax = sb("rmax", (128, 8), F32)
    psum_r = sb("psum_r", (128, 4), F32)
    identb = sb("identb", (128, 128), BF16)
    ident32b = sb("ident32b", (128, 128), F32)
    onesb = sb("onesb", (128, 128), BF16)
    maskb = sb("maskb", (128, c.TC), BF16)
    cosb = sb("cosb", (c.ROPE // 2, c.S), BF16)
    sinb = sb("sinb", (c.ROPE // 2, c.S), BF16)
    kvnwb = sb("kvnwb", (128, c.CJ), F32)
    DCB = c.DC
    pkb = sb("pkb", (128, 2, DCB), U8)
    scb = sb("scb", (128, 2, DCB), F32)
    d_u8 = sb("d_u8", (128, 2 * DCB), U8)
    d_f2 = sb("d_f2", (128, 2 * DCB), BF16)
    d_ge8 = sb("d_ge8", (128, 2 * DCB), BF16)
    d_m = sb("d_m", (128, 2 * DCB), BF16)
    d_t2 = sb("d_t2", (128, 2 * DCB), BF16)
    d_a = sb("d_a", (128, 2 * DCB), BF16)
    d_val = sb("d_val", (128, 1, 2 * DCB), BF16)
    d_pl = sb("d_pl", (128, 1, 2, DCB), BF16)
    statm2 = sb("statm2", (1, c.TC), F32, at=("mean_rep", 0), spill=True)
    dwbuf = sb("dwbuf", (128, 2, c.MBQ, 128), BF16, at=("wbuf_g", 0))
    wc16 = sb("wc16", (128, 2, 2048), BF16, at=("pkb", 0), spill=True)
    xstage = sb("xstage", (128, c.D), F32, at=("interq", 0), spill=True)
    xstage_b = sb("xstage_b", (128, c.D), BF16,
                  at=("interq", 4 * c.D), spill=True)
    kvT = sb("kvT", (128, c.CJ, c.S), BF16, at=("outp", 0), spill=True)
    pT = sb("pT", (128, c.S // 128, c.TC), BF16,
            at=("outp", 2 * c.CJ * c.S), spill=True)
    kvlat = sb("kvlat", (128, c.CJ, c.TC), BF16,
               at=("outp", 2 * c.CJ * c.S), spill=True)
    kvtok = sb("kvtok", (128, c.S // 128, c.RANK), BF16,
               at=("pkb", 0), spill=True)
    headso = sb("headso", (128, c.NHT, c.TC), BF16,
                at=("interq", 0), spill=True)
    prow = sb("prow", (128, 2, c.S), BF16, at=("wbuf_g", 0), spill=True)
    kpeT = sb("kpeT", (c.ROPE, c.S), BF16, at=("wbuf_g", 4 * c.S),
              spill=True)
    qlatT = sb("qlatT", (128, c.CJ, c.TC), BF16,
               at=("wbuf_g", 6 * c.S), spill=True)
    outlatT = sb("outlatT", (128, c.CJ, c.TC), BF16,
                 at=("mean_rep", 0), spill=True)
    wqh_b = sb("wqh_b", (128, 2, c.KT, c.QK), BF16, at=("wbuf_u", 0))
    qn_h = sb("qn_h", (128, c.TC), BF16, at=("wbuf_u", 4 * c.KT * c.QK))
    qpe_h = sb("qpe_h", (c.ROPE, c.TC), BF16,
               at=("wbuf_u", 4 * c.KT * c.QK + 2 * c.TC))
    wkbk_b = sb("wkbk_b", (128, 2, c.CJ * 128), BF16,
                at=("wbuf_u", 4 * c.KT * c.QK + 4 * c.TC))
    wkva_b = sb("wkva_b", (128, 2, c.KT, 128), BF16, at=("wqh_b", 0))
    wo_b = sb("wo_b", (128, 2, c.NHT, 128), BF16, at=("wqh_b", 0))
    wkbv_b = sb("wkbv_b", (128, 2, c.CJ * c.VD), BF16, at=("gel", 0))
    ropet = sb("ropet", (c.ROPE // 2, 2, c.TC), F32, at=("res_t", 0))
    nc.bump_sbuf(cur[0] - nc.sbuf_base)
    pG0 = st.enter_context(nc.psum_tensor("pG0", [128, 512], F32))
    pG1 = st.enter_context(nc.psum_tensor("pG1", [128, 512], F32))
    pU0 = st.enter_context(nc.psum_tensor("pU0", [128, 512], F32))
    pU1 = st.enter_context(nc.psum_tensor("pU1", [128, 512], F32))
    pD0 = st.enter_context(nc.psum_tensor("pD0", [128, 512], F32))
    pD1 = st.enter_context(nc.psum_tensor("pD1", [128, 512], F32))
    pR = st.enter_context(nc.psum_tensor("pR", [128, 512], F32))
    pTB = st.enter_context(nc.psum_tensor("pTB", [128, 1024], BF16))
    pG = [pG0, pG1]
    pU = [pU0, pU1]
    pD = [pD0, pD1]
    pS = [pG0, pG1, pU0, pU1]
    tbank_list = [pD0, pD1, pR]

    def sem(name):
        return Sem(st.enter_context(nc.semaphore(name)), name)

    s_c = sem("s_c")
    s_g = sem("s_g")
    s_g2 = sem("s_g2")
    s_g3 = sem("s_g3")
    s_g4 = sem("s_g4")
    tbanks = tbank_list
    gq_sems = [s_g, s_g2, s_g3, s_g4]
    gq_idx = [0]

    def gdma(out, in_):
        s = gq_sems[gq_idx[0] % 4]
        gq_idx[0] += 1
        P.wait("gpsimd", s)
        P.dma("gpsimd", s, out, in_)

    def gq_snapshot():
        return [s.n for s in gq_sems]

    def gq_wait(eng, snap=None):
        for s, v in zip(gq_sems, snap or [s.n for s in gq_sems]):
            P.wait(eng, s, v)
    s_wg = [sem("s_wg0"), sem("s_wg1")]
    s_wu = [sem("s_wu0"), sem("s_wu1")]
    s_wd = [sem("s_wd0"), sem("s_wd1")]
    s_wa = [sem("s_wa0"), sem("s_wa1")]
    s_wb = [sem("s_wb0"), sem("s_wb1")]
    s_pk = [sem("s_pk0"), sem("s_pk1")]
    s_gp = sem("s_gp")
    s_pe = sem("s_pe")
    s_v = sem("s_v")
    s_s = sem("s_s")
    s_y = [sem("s_y0"), sem("s_y1"), sem("s_y2"), sem("s_y3")]

    P = Plan(drains=drains)

    def wstream_barrier(eng="sync"):
        for s in (s_wg[0], s_wg[1], s_wu[0], s_wu[1], s_wd[0], s_wd[1],
                  s_wa[0], s_wa[1], s_wb[0], s_wb[1]):
            P.wait(eng, s)
        P.wait(eng, s_pe)
        P.wait(eng, s_v)


    def mm_transpose(dst_ap, src_ap, ck, idx):
        sl = idx % 3
        bank = tbanks[sl]
        P.wait("tensor", s_v, ck[sl])
        P.op("tensor", lambda e, b=bank, s=src_ap: e.matmul(
            b[:, 0:128], s, identb[:], start=True, stop=True), sem=s_pe)
        P.wait("vector", s_pe)
        P.op("vector", lambda e, b=bank, d=dst_ap: e.tensor_copy(
            out=d, in_=b[:, 0:128]), sem=s_v)
        ck[sl] = s_v.n

    P.dma("scalar", s_c, identb[:], iden_d[:])
    P.dma("scalar", s_c, ident32b[:], iden32_d[:])
    P.dma("scalar", s_c, onesb[:], ones_d[:])
    P.dma("scalar", s_c, maskb[:], mask_d[:])
    P.dma("scalar", s_c, cosb[:], cos_d[:])
    P.dma("scalar", s_c, sinb[:], sin_d[:])
    P.dma("scalar", s_c, kvnwb[:], kvnw_d[:])

    for (src_, dst) in ((gsc_d, gsc_s), (usc_d, usc_s), (dsc_d, dsc_s)):
        for i in range(16):
            gdma(dst.rearrange("(u k) m -> u k m", k=16)[:, i, :], src_[:])

    cast_jobs = []
    for h in range(c.NH):
        for kt in range(c.KT):
            cast_jobs.append((wq_d[h, kt * 128:(kt + 1) * 128, :],
                              wqa_s[h, :, kt, :], c.QK))
    for kt in range(c.KT):
        cast_jobs.append((wkva_d[kt * 128:(kt + 1) * 128, :],
                          wkva_s[:, kt, :], c.RANK + c.ROPE))
    for h in range(c.NH):
        cast_jobs.append((wkbk_d[h], wkbk_s[h], c.CJ * 128))
        cast_jobs.append((wkbv_d[h].rearrange("(j p) v -> p j v", p=128),
                          wkbv_s[h], c.CJ * c.VD))
    for ft in range(c.HT):
        cast_jobs.append((
            wo_d.rearrange("(t p) f -> p t f", p=128)[
                :, :, ft * 128:(ft + 1) * 128],
            wo_s[ft], c.NHT * 128))
    for j, (src_ap, dst_ap, ncol) in enumerate(cast_jobs):
        par = j % 2
        if j >= 2:
            gq_wait("gpsimd")
        gdma(wc16[:, par, :ncol], src_ap)
        gq_wait("gpsimd")
        gdma(dst_ap, wc16[:, par, :ncol])

    def decode_tile(par, dc):
        pk = pkb[:, par, :dc]
        w = 2 * dc
        P.op("vector", lambda e, pk=pk, dc=dc: e.tensor_scalar(
            out=d_u8[:, :dc], in0=pk, scalar1=15, scalar2=None,
            op0=ALU.bitwise_and))
        P.op("vector", lambda e, pk=pk, dc=dc: e.tensor_scalar(
            out=d_u8[:, dc:2 * dc], in0=pk, scalar1=4, scalar2=None,
            op0=ALU.logical_shift_right))
        P.drain("vector")
        P.op("vector", lambda e, w=w: e.tensor_copy(
            out=d_f2[:, :w], in_=d_u8[:, :w]))
        P.drain("vector")
        P.op("vector", lambda e, w=w: e.tensor_scalar(
            out=d_ge8[:, :w], in0=d_f2[:, :w], scalar1=8.0, scalar2=None,
            op0=ALU.is_ge))
        P.drain("vector")
        P.op("vector", lambda e, w=w: e.scalar_tensor_tensor(
            out=d_m[:, :w], in0=d_ge8[:, :w], scalar=-8.0, in1=d_f2[:, :w],
            op0=ALU.mult, op1=ALU.add))
        P.drain("vector", s_v)
        P.wait("scalar", s_v)
        P.op("scalar", lambda e, w=w: e.activation(
            out=d_t2[:, :w], in_=d_m[:, :w], func=AF.Copy,
            scale=3.0, bias=-15.0), sem=s_s)
        P.wait("vector", s_s)
        P.op("vector", lambda e, w=w: e.scalar_tensor_tensor(
            out=d_a[:, :w], in0=d_m[:, :w], scalar=0.5, in1=d_t2[:, :w],
            op0=ALU.mult, op1=ALU.max))
        P.drain("vector")
        P.op("vector", lambda e, w=w: e.scalar_tensor_tensor(
            out=d_t2[:, :w], in0=d_m[:, :w], scalar=-2.0, in1=d_a[:, :w],
            op0=ALU.add, op1=ALU.max))
        P.drain("vector")
        P.op("vector", lambda e, w=w: e.scalar_tensor_tensor(
            out=d_m[:, :w], in0=d_ge8[:, :w], scalar=-2.0,
            in1=d_t2[:, :w], op0=ALU.mult, op1=ALU.mult))
        P.drain("vector")
        if gp_done:
            P.wait("vector", s_gp, gp_done[-1])
        P.op("vector", lambda e, w=w: e.tensor_tensor(
            out=d_val[:, 0, :w], in0=d_t2[:, :w], in1=d_m[:, :w],
            op=ALU.add))
        P.drain("vector", s_v)
        P.wait("gpsimd", s_v)
        if g_done:
            gq_wait("gpsimd", g_done[-1])
        P.op("gpsimd", lambda e, dc=dc, par=par: e.tensor_tensor(
            out=d_pl[:, 0, 0, :dc], in0=d_val[:, 0, :dc],
            in1=scb[:, par, :dc], op=ALU.mult))
        P.op("gpsimd", lambda e, dc=dc, par=par: e.tensor_tensor(
            out=d_pl[:, 0, 1, :dc], in0=d_val[:, 0, dc:2 * dc],
            in1=scb[:, par, :dc], op=ALU.mult))
        P.drain("gpsimd", s_gp)

    v_done = []
    gp_done = []
    g_done = []
    eighth_ck = {}

    def decode_one(pk_src, sc_src, j, cc, dc, write_plane):
        ti = len(v_done)
        par = ti % 2
        if ti >= 2:
            P.wait("sync", s_gp, gp_done[ti - 2])
        P.dma("sync", s_pk[par], pkb[:, par, :dc],
              pk_src[j * 128:(j + 1) * 128, cc * dc:(cc + 1) * dc])
        P.dma("sync", s_pk[par], scb[:, par, :dc],
              sc_src[j * 128:(j + 1) * 128, cc * dc:(cc + 1) * dc])
        P.wait("vector", s_pk[par])
        decode_tile(par, dc)
        P.drain("vector", s_v)
        v_done.append(s_v.n)
        write_plane(j, cc, par)
        g_done.append(gq_snapshot())
        gp_done.append(s_gp.n)

    def wg_write(which_s):
        def w(j, cc, par):
            dc = c.DC
            for pi, kt in ((0, j), (1, c.KT // 2 + j)):
                nmb = dc // 128
                gdma(which_s[cc * nmb:(cc + 1) * nmb, :, kt, :].rearrange(
                         "b p m -> p b m"),
                     d_pl[:, 0, pi, :dc])
        return w

    def wd_write(j, cc, par):
        dc = c.DC
        for pi, mb in ((0, j), (1, c.NMB // 2 + j)):
            nht = dc // 128
            gdma(wd_s[cc * nht:(cc + 1) * nht, mb, :, :].rearrange(
                     "t p h -> p t h"),
                 d_pl[:, 0, pi, :dc])

    def decode_gen():
        gw = wg_write(wg_s)
        uw = wg_write(wu_s)
        ncc_g = c.DC // 128
        done_g, done_u = set(), set()
        for q8 in range(c.NQ):
            mb_lo, mb_hi = q8 * c.MBQ, (q8 + 1) * c.MBQ
            ccs = range(mb_lo // ncc_g, -(-mb_hi // ncc_g))
            for cc in ccs:
                if cc in done_g:
                    continue
                done_g.add(cc)
                for j in range(c.D // 256):
                    decode_one(gpk_d, gsc_s, j, cc, c.DC, gw)
                    yield 1
            for cc in ccs:
                if cc in done_u:
                    continue
                done_u.add(cc)
                for j in range(c.D // 256):
                    decode_one(upk_d, usc_s, j, cc, c.DC, uw)
                    yield 1
            if q8 < c.NQ // 2:
                for j in range(mb_lo, mb_hi):
                    for cc in range(c.D // c.DC):
                        decode_one(dpk_d, dsc_s, j, cc, c.DC, wd_write)
                        yield 1
            eighth_ck[q8] = (gq_snapshot(), s_gp.n, s_v.n)

    gq_wait("sync")
    bg = decode_gen()

    def pump(n=1):
        for _ in range(n):
            if next(bg, None) is None:
                return

    def need_eighth(q8):
        while q8 not in eighth_ck:
            if next(bg, None) is None:
                break

    def layer_norm(T0):
        P.wait("scalar", s_v)
        P.wait("vector", s_pe)
        P.wait("tensor", s_v)
        ln_ck = []
        for ft in range(NFT):
            par = ft % 2
            if ft >= 2:
                P.wait("scalar", s_pe, ln_ck[ft - 2])
            P.op("scalar", lambda e, ft=ft, par=par, T0=T0: e.activation(
                out=sqb[:, par, :], in_=stream[:, ft, T0:T0 + c.TC],
                func=AF.Square), sem=s_s)
            P.wait("tensor", s_s)
            P.op("tensor", lambda e, ft=ft, T0=T0: e.matmul(
                pR[0:1, 0:c.TC], onesb[:, 0:1], stream[:, ft, T0:T0 + c.TC],
                start=(ft == 0), stop=(ft == NFT - 1)))
            P.op("tensor", lambda e, par=par, ft=ft: e.matmul(
                pR[32:33, 0:c.TC], onesb[:, 0:1], sqb[:, par, :],
                start=(ft == 0), stop=(ft == NFT - 1)), sem=s_pe)
            ln_ck.append(s_pe.n)
        P.wait("vector", s_pe)
        P.op("vector", lambda e: e.tensor_scalar(
            out=stat[0:1, :], in0=pR[0:1, 0:c.TC], scalar1=1.0 / c.D,
            scalar2=None, op0=ALU.mult))
        P.op("vector", lambda e: e.tensor_scalar(
            out=stat[32:33, :], in0=pR[32:33, 0:c.TC], scalar1=1.0 / c.D,
            scalar2=None, op0=ALU.mult))
        P.drain("vector")
        P.op("vector", lambda e: e.tensor_tensor(
            out=statb[0:1, :], in0=stat[0:1, :], in1=stat[0:1, :],
            op=ALU.mult))
        P.op("vector", lambda e: e.tensor_copy(
            out=statm2[0:1, :], in_=stat[32:33, :]))
        P.drain("vector")
        P.op("vector", lambda e: e.tensor_tensor(
            out=statm2[0:1, :], in0=statm2[0:1, :], in1=statb[0:1, :],
            op=ALU.subtract))
        P.drain("vector")
        P.op("vector", lambda e: e.tensor_scalar(
            out=statm2[0:1, :], in0=statm2[0:1, :], scalar1=1e-5,
            scalar2=None, op0=ALU.add))
        P.drain("vector", s_v)
        P.wait("scalar", s_v)
        P.op("scalar", lambda e: e.activation(
            out=statm2[0:1, :], in_=statm2[0:1, :], func=AF.Sqrt), sem=s_s)
        P.wait("vector", s_s)
        P.op("vector", lambda e: e.reciprocal(
            out=statm2[0:1, :], in_=statm2[0:1, :]))
        P.drain("vector")
        P.op("vector", lambda e: e.tensor_copy(
            out=statb[0:1, :], in_=stat[0:1, :]))
        P.op("vector", lambda e: e.tensor_copy(
            out=statc[0:1, :], in_=statm2[0:1, :]))
        P.drain("vector", s_v)
        P.wait("tensor", s_v)
        P.op("tensor", lambda e: e.matmul(
            pR[:, 0:c.TC], onesb[0:1, :], statb[0:1, :],
            start=True, stop=True), sem=s_pe)
        P.wait("vector", s_pe)
        P.op("vector", lambda e: e.tensor_copy(
            out=mean_rep[:], in_=pR[:, 0:c.TC]))
        P.drain("vector", s_v)
        P.wait("tensor", s_v)
        P.op("tensor", lambda e: e.matmul(
            pR[:, 0:c.TC], onesb[0:1, :], statc[0:1, :],
            start=True, stop=True), sem=s_pe)
        P.wait("vector", s_pe)
        P.op("vector", lambda e: e.tensor_copy(
            out=rstd_rep[:], in_=pR[:, 0:c.TC]))
        P.drain("vector")
        for ft in range(NFT):
            P.op("vector", lambda e, ft=ft, T0=T0: e.scalar_tensor_tensor(
                out=xn[:, ft, :], in0=stream[:, ft, T0:T0 + c.TC],
                scalar=1.0, in1=mean_rep[:], op0=ALU.mult, op1=ALU.subtract))
        P.drain("vector")
        for ft in range(NFT):
            P.op("vector", lambda e, ft=ft: e.tensor_tensor(
                out=xn[:, ft, :], in0=xn[:, ft, :], in1=rstd_rep[:],
                op=ALU.mult))
        P.drain("vector", s_v)
        P.wait("tensor", s_v)
        P.wait("scalar", s_v)

    def ffn(sseq, final):
        for ch in range(c.NCH):
            T0 = ch * c.TC
            layer_norm(T0)
            for q in range(c.NQ):
                evac_ck = []
                need_eighth(q)
                for g2 in range(c.MBQ // 2):
                    par = g2 % 2
                    mbg0 = q * c.MBQ + g2 * 2
                    if g2 >= 2:
                        P.wait("sync", s_v, evac_ck[2 * (g2 - 2) + 1])
                    if g2 == 0:
                        wstream_barrier()
                        gq_wait("sync", eighth_ck[q][0])
                    P.dma("sync", s_wg[par], wbuf_g[:, par],
                          wg_s[mbg0:mbg0 + 2].rearrange("b p k m -> p b k m"))
                    P.dma("sync", s_wg[par], wbuf_u[:, par],
                          wu_s[mbg0:mbg0 + 2].rearrange("b p k m -> p b k m"))
                    for bi in range(2):
                        mb = g2 * 2 + bi
                        par2 = mb % 2
                        P.wait("tensor", s_wg[par])
                        if len(evac_ck) >= 2:
                            P.wait("tensor", s_v, evac_ck[-2])
                        for kt in range(c.KT):
                            P.op("tensor", lambda e, par=par, bi=bi, kt=kt,
                                 par2=par2: e.matmul(
                                     pG[par2][:, 0:c.TC],
                                     wbuf_g[:, par, bi, kt, :],
                                     xn[:, kt, :], start=(kt == 0),
                                     stop=(kt == c.KT - 1)),
                                 sem=(s_pe if kt == c.KT - 1 else None))
                        for kt in range(c.KT):
                            P.op("tensor", lambda e, par=par, bi=bi, kt=kt,
                                 par2=par2: e.matmul(
                                     pU[par2][:, 0:c.TC],
                                     wbuf_u[:, par, bi, kt, :],
                                     xn[:, kt, :], start=(kt == 0),
                                     stop=(kt == c.KT - 1)),
                                 sem=(s_pe if kt == c.KT - 1 else None))
                        P.wait("scalar", s_pe, s_pe.n - 1)
                        P.op("scalar", lambda e, par2=par2: e.activation(
                            out=gel[:, par2, :], in_=pG[par2][:, 0:c.TC],
                            func=AF.Gelu_apprx_tanh), sem=s_s)
                        P.wait("vector", s_s)
                        P.wait("vector", s_pe)
                        P.op("vector", lambda e, mb=mb, par2=par2:
                             e.tensor_tensor(
                                 out=interq[:, mb, :], in0=gel[:, par2, :],
                                 in1=pU[par2][:, 0:c.TC], op=ALU.mult),
                             sem=s_v)
                        evac_ck.append(s_v.n)
                        pump(1)
                dn_ck = []
                for ht in range(c.HT):
                    par = ht % 2
                    if ht >= 2:
                        P.wait("sync", s_v, dn_ck[ht - 2])
                    else:
                        P.wait("sync", s_v, evac_ck[-1])
                    P.dma("sync", s_wd[par], dwbuf[:, par],
                          wd_s[ht, q * c.MBQ:(q + 1) * c.MBQ].rearrange(
                              "b p h -> p b h"))
                    P.wait("tensor", s_wd[par])
                    P.wait("tensor", s_v, evac_ck[-1])
                    if ht >= 2:
                        P.wait("tensor", s_v, dn_ck[ht - 2])
                    pump(1)
                    for mb in range(c.MBQ):
                        P.op("tensor", lambda e, par=par, mb=mb: e.matmul(
                            pD[par][:, 0:c.TC], dwbuf[:, par, mb, :],
                            interq[:, mb, :], start=(mb == 0),
                            stop=(mb == c.MBQ - 1)),
                            sem=(s_pe if mb == c.MBQ - 1 else None))
                    P.wait("vector", s_pe)
                    if q == 0:
                        P.op("vector", lambda e, par=par, ht=ht:
                             e.tensor_copy(out=outp[:, ht, :],
                                           in_=pD[par][:, 0:c.TC]), sem=s_v)
                    elif q < c.NQ - 1:
                        P.op("vector", lambda e, par=par, ht=ht:
                             e.tensor_tensor(
                                 out=outp[:, ht, :], in0=outp[:, ht, :],
                                 in1=pD[par][:, 0:c.TC], op=ALU.add),
                             sem=s_v)
                    else:
                        P.op("vector", lambda e, par=par, ht=ht:
                             e.tensor_tensor(
                                 out=outp[:, ht, :], in0=outp[:, ht, :],
                                 in1=pD[par][:, 0:c.TC], op=ALU.add))
                        P.drain("vector")
                        if not final:
                            P.op("vector", lambda e, ht=ht, T0=T0:
                                 e.scalar_tensor_tensor(
                                     out=stream[:, ht, T0:T0 + c.TC],
                                     in0=outp[:, ht, :], scalar=0.5,
                                     in1=stream[:, ht, T0:T0 + c.TC],
                                     op0=ALU.mult, op1=ALU.add), sem=s_v)
                        else:
                            rp = ht % 2
                            P.op("vector", lambda e, ht=ht, T0=T0, rp=rp:
                                 e.scalar_tensor_tensor(
                                     out=res_t[:, rp, :],
                                     in0=outp[:, ht, :], scalar=0.5,
                                     in1=stream[:, ht, T0:T0 + c.TC],
                                     op0=ALU.mult, op1=ALU.add), sem=s_v)
                            for b in range(c.RPC):
                                tp = b % 2
                                P.wait("tensor", s_v)
                                P.wait("tensor", s_y[tp])
                                P.op("tensor", lambda e, rp=rp, b=b:
                                     e.transpose(
                                         pR[:, 0:128],
                                         res_t[:, rp, b * 128:(b + 1) * 128],
                                         ident32b[:]), sem=s_pe)
                                P.wait("vector", s_pe)
                                P.op("vector", lambda e, tp=tp:
                                     e.tensor_copy(out=tstage[:, tp, :],
                                                   in_=pR[:, 0:128]),
                                     sem=s_v)
                                P.wait("scalar", s_v)
                                P.dma("scalar", s_y[tp],
                                      y_d[sseq,
                                          T0 + b * 128:T0 + (b + 1) * 128,
                                          ht * 128:(ht + 1) * 128],
                                      tstage[:, tp, :])
                    dn_ck.append(s_v.n)

    def rope_apply(dst0, dst1, src_ps, Tsl):
        H = c.ROPE // 2
        cs = cosb[:, Tsl]
        sn = sinb[:, Tsl]
        P.op("vector", lambda e: e.tensor_tensor(
            out=ropet[:, 0, :], in0=src_ps[0:H, 0:c.TC], in1=cs, op=ALU.mult))
        P.op("vector", lambda e: e.tensor_tensor(
            out=ropet[:, 1, :], in0=src_ps[H:2 * H, 0:c.TC], in1=sn,
            op=ALU.mult))
        P.drain("vector")
        P.op("vector", lambda e, d=dst0: e.tensor_tensor(
            out=d, in0=ropet[:, 0, :], in1=ropet[:, 1, :], op=ALU.subtract))
        P.drain("vector")
        P.op("vector", lambda e: e.tensor_tensor(
            out=ropet[:, 0, :], in0=src_ps[0:H, 0:c.TC], in1=sn, op=ALU.mult))
        P.op("vector", lambda e: e.tensor_tensor(
            out=ropet[:, 1, :], in0=src_ps[H:2 * H, 0:c.TC], in1=cs,
            op=ALU.mult))
        P.drain("vector")
        P.op("vector", lambda e, d=dst1: e.tensor_tensor(
            out=d, in0=ropet[:, 0, :], in1=ropet[:, 1, :], op=ALU.add))
        P.drain("vector", s_v)

    def attention(sseq):
        for _ in bg:
            pass
        gq_wait("vector")
        P.wait("vector", s_gp)
        pt_slot_ck = [0, 0, 0]
        for qc in range(c.NCH):
            T0 = qc * c.TC
            Tsl = slice(T0, T0 + c.TC)
            layer_norm(T0)
            wstream_barrier()
            kv_ck = []
            for rt in range(c.CJ + 1):
                par = rt % 2
                ncol = 128 if rt < c.CJ else c.ROPE
                if rt >= 2:
                    P.wait("sync", s_pe, kv_ck[rt - 1])
                P.dma("sync", s_wa[par], wkva_b[:, par, :, 0:ncol],
                      wkva_s[:, :, rt * 128:rt * 128 + ncol])
                P.wait("tensor", s_wa[par])
                if rt >= 2:
                    P.wait("tensor", s_v)
                    P.wait("tensor", s_s)
                for kt in range(c.KT):
                    P.op("tensor", lambda e, par=par, kt=kt, nc_=ncol:
                         e.matmul(pD[par][0:nc_, 0:c.TC],
                                  wkva_b[:, par, kt, 0:nc_],
                                  xn[:, kt, :], start=(kt == 0),
                                  stop=(kt == c.KT - 1)),
                         sem=(s_pe if kt == c.KT - 1 else None))
                kv_ck.append(s_pe.n)
                if rt < c.CJ:
                    P.wait("vector", s_pe)
                    P.op("vector", lambda e, par=par, rt=rt: e.tensor_copy(
                        out=kvlat[:, rt, :], in_=pD[par][:, 0:c.TC]))
                    P.drain("vector", s_v)
                    P.wait("scalar", s_v)
                    P.op("scalar", lambda e, par=par, rt=rt: e.activation(
                        out=sqb[:, rt % 2, :], in_=pD[par][:, 0:c.TC],
                        func=AF.Square), sem=s_s)
                    P.wait("tensor", s_s)
                    P.op("tensor", lambda e, rt=rt: e.matmul(
                        pR[0:1, 0:c.TC], onesb[:, 0:1], sqb[:, rt % 2, :],
                        start=(rt == 0), stop=(rt == c.CJ - 1)), sem=s_pe)
                else:
                    P.wait("vector", s_pe)
                    rope_apply(kpeT[0:c.ROPE // 2, Tsl],
                               kpeT[c.ROPE // 2:c.ROPE, Tsl], pD[par], Tsl)
            P.wait("vector", s_pe)
            P.op("vector", lambda e: e.tensor_scalar(
                out=statm2[0:1, :], in0=pR[0:1, 0:c.TC],
                scalar1=1.0 / c.RANK, scalar2=1e-6, op0=ALU.mult,
                op1=ALU.add))
            P.drain("vector", s_v)
            P.wait("scalar", s_v)
            P.op("scalar", lambda e: e.activation(
                out=statm2[0:1, :], in_=statm2[0:1, :], func=AF.Sqrt),
                sem=s_s)
            P.wait("vector", s_s)
            P.op("vector", lambda e: e.reciprocal(
                out=statm2[0:1, :], in_=statm2[0:1, :]))
            P.drain("vector")
            P.op("vector", lambda e: e.tensor_copy(
                out=statc[0:1, :], in_=statm2[0:1, :]))
            P.drain("vector", s_v)
            P.wait("tensor", s_v)
            P.op("tensor", lambda e: e.matmul(
                pR[:, 0:c.TC], onesb[0:1, :], statc[0:1, :],
                start=True, stop=True), sem=s_pe)
            P.wait("vector", s_pe)
            P.op("vector", lambda e: e.tensor_copy(
                out=rstd_rep[:], in_=pR[:, 0:c.TC]))
            P.drain("vector")
            for cj in range(c.CJ):
                P.op("vector", lambda e, cj=cj, Tsl=Tsl:
                     e.scalar_tensor_tensor(
                         out=kvT[:, cj, Tsl], in0=kvlat[:, cj, :],
                         scalar=kvnwb[:, cj:cj + 1], in1=rstd_rep[:],
                         op0=ALU.mult, op1=ALU.mult))
            P.drain("vector", s_v)
            slot_ck = [0, 0, 0]
            P.wait("tensor", s_v)
            for cj in range(c.CJ):
                for b in range(c.RPC):
                    mm_transpose(
                        kvtok[:, qc * c.RPC + b, cj * 128:(cj + 1) * 128],
                        kvT[:, cj, T0 + b * 128:T0 + (b + 1) * 128],
                        slot_ck, cj * c.RPC + b)
            head_ck = []
            for h in range(c.NH):
                parh = h % 2
                if h == 0:
                    wstream_barrier()
                if h >= 2:
                    P.wait("sync", s_pe, head_ck[h - 1])
                P.dma("sync", s_wa[parh], wqh_b[:, parh], wqa_s[h])
                P.dma("sync", s_wg[parh], wkbk_b[:, parh], wkbk_s[h])
                P.dma("sync", s_wu[parh], wkbv_b[:, parh], wkbv_s[h])
                P.wait("tensor", s_wa[parh])
                P.wait("tensor", s_v)
                for kt in range(c.KT):
                    P.op("tensor", lambda e, parh=parh, kt=kt: e.matmul(
                        pD0[:, 0:c.TC], wqh_b[:, parh, kt, 0:c.NOPE],
                        xn[:, kt, :], start=(kt == 0),
                        stop=(kt == c.KT - 1)))
                for kt in range(c.KT):
                    P.op("tensor", lambda e, parh=parh, kt=kt: e.matmul(
                        pD1[0:c.ROPE, 0:c.TC],
                        wqh_b[:, parh, kt, c.NOPE:c.QK],
                        xn[:, kt, :], start=(kt == 0),
                        stop=(kt == c.KT - 1)),
                        sem=(s_pe if kt == c.KT - 1 else None))
                head_ck.append(s_pe.n)
                P.wait("vector", s_pe)
                P.op("vector", lambda e: e.tensor_copy(
                    out=qn_h[:], in_=pD0[:, 0:c.TC]))
                P.drain("vector", s_v)
                rope_apply(qpe_h[0:c.ROPE // 2, :],
                           qpe_h[c.ROPE // 2:c.ROPE, :], pD1, Tsl)
                P.wait("tensor", s_v)
                P.wait("tensor", s_wg[parh])
                for cj in range(c.CJ):
                    if cj >= 2:
                        P.wait("tensor", s_v)
                    P.op("tensor", lambda e, cj=cj, parh=parh: e.matmul(
                        pD[cj % 2][:, 0:c.TC], wkbk_b[
                            :, parh, cj * 128:(cj + 1) * 128], qn_h[:],
                        start=True, stop=True), sem=s_pe)
                    P.wait("vector", s_pe)
                    P.op("vector", lambda e, cj=cj: e.tensor_copy(
                        out=qlatT[:, cj, :], in_=pD[cj % 2][:, 0:c.TC]),
                        sem=s_v)
                def emit_row_tr(r, R, rp, div_ck):
                    P.wait("tensor", s_v, div_ck)
                    for j0 in range(0, R + 1, 4):
                        jn = min(4, R + 1 - j0)
                        sl = (j0 // 4) % 3
                        bank = tbank_list[sl]
                        P.wait("tensor", s_v, pt_slot_ck[sl])
                        for ji in range(jn):
                            P.op("tensor", lambda e, b=bank, j=j0 + ji,
                                 ji=ji, rp=rp: e.matmul(
                                     b[:, ji * 128:(ji + 1) * 128],
                                     prow[:, rp, j * 128:(j + 1) * 128],
                                     identb[:], start=True, stop=True),
                                 sem=(s_pe if ji == jn - 1 else None))
                        P.wait("vector", s_pe)
                        P.op("vector", lambda e, b=bank, j0=j0, jn=jn, r=r:
                             e.tensor_copy(
                                 out=pT[:, j0:j0 + jn,
                                        r * 128:(r + 1) * 128],
                                 in_=b[:, 0:jn * 128]), sem=s_v)
                        pt_slot_ck[sl] = s_v.n

                pending = None
                for r in range(c.RPC):
                    R = qc * c.RPC + r
                    ncc = qc + 1
                    rsl = slice(r * 128, (r + 1) * 128)
                    rp = r % 2
                    P.wait("tensor", s_v)
                    P.wait("tensor", s_s)
                    for cc in range(ncc):
                        ksl = slice(cc * c.TC, (cc + 1) * c.TC)
                        for cj in range(c.CJ):
                            P.op("tensor", lambda e, cc=cc, cj=cj, rsl=rsl,
                                 ksl=ksl: e.matmul(
                                     pS[cc][:, 0:c.TC], qlatT[:, cj, rsl],
                                     kvT[:, cj, ksl], start=(cj == 0),
                                     stop=False))
                        P.op("tensor", lambda e, cc=cc, rsl=rsl, ksl=ksl:
                             e.matmul(pS[cc][:, 0:c.TC], qpe_h[:, rsl],
                                      kpeT[:, ksl], start=False, stop=True),
                             sem=s_pe)
                    if pending is not None:
                        emit_row_tr(*pending)
                    P.wait("vector", s_pe)
                    P.op("vector", lambda e, r=r, qc=qc: e.tensor_tensor(
                        out=pS[qc][:, r * 128:c.TC],
                        in0=pS[qc][:, r * 128:c.TC],
                        in1=maskb[:, 0:c.TC - r * 128], op=ALU.add))
                    P.drain("vector", s_v)
                    P.wait("scalar", s_v)
                    for cc in range(ncc):
                        P.op("scalar", lambda e, cc=cc, rp=rp: e.activation(
                            out=prow[:, rp, cc * c.TC:(cc + 1) * c.TC],
                            in_=pS[cc][:, 0:c.TC], func=AF.Exp,
                            scale=c.SCALE,
                            accum_out=psum_r[:, cc:cc + 1]),
                            sem=(s_s if cc == ncc - 1 else None))
                    P.wait("vector", s_s)
                    if ncc > 1:
                        P.op("vector", lambda e, ncc=ncc: e.reduce_sum(
                            out=rmax[:, 0:1], in_=psum_r[:, 0:ncc],
                            axis=AX.X))
                        P.drain("vector")
                    sum_ap = rmax[:, 0:1] if ncc > 1 else psum_r[:, 0:1]
                    P.op("vector", lambda e, sum_ap=sum_ap: e.reciprocal(
                        out=rmax[:, 4:5], in_=sum_ap))
                    P.drain("vector")
                    P.op("vector", lambda e, rp=rp, w=ncc * c.TC:
                         e.tensor_scalar(
                             out=prow[:, rp, 0:w], in0=prow[:, rp, 0:w],
                             scalar1=rmax[:, 4:5], scalar2=None,
                             op0=ALU.mult))
                    P.drain("vector", s_v)
                    pending = (r, R, rp, s_v.n)
                emit_row_tr(*pending)
                for r in range(c.RPC):
                    R = qc * c.RPC + r
                    for j in range(R + 1, (qc + 1) * c.RPC):
                        P.op("vector", lambda e, j=j, r=r: e.memset(
                            pT[:, j, r * 128:(r + 1) * 128], 0.0))
                P.drain("vector", s_v)
                P.wait("tensor", s_v)
                P.wait("tensor", s_s)
                NJT = (qc + 1) * c.RPC
                for cj in range(c.CJ):
                    for j in range(NJT):
                        P.op("tensor", lambda e, cj=cj, j=j, NJT=NJT:
                             e.matmul(
                                 pS[cj][:, 0:c.TC],
                                 kvtok[:, j, cj * 128:(cj + 1) * 128],
                                 pT[:, j, :], start=(j == 0),
                                 stop=(j == NJT - 1)),
                             sem=(s_pe if j == NJT - 1 else None))
                    P.wait("vector", s_pe)
                    P.op("vector", lambda e, cj=cj: e.tensor_copy(
                        out=outlatT[:, cj, :], in_=pS[cj][:, 0:c.TC]),
                        sem=s_v)
                P.wait("tensor", s_v)
                P.wait("tensor", s_wu[parh])
                for cj in range(c.CJ):
                    P.op("tensor", lambda e, cj=cj, parh=parh: e.matmul(
                        pR[0:c.VD, 0:c.TC],
                        wkbv_b[:, parh, cj * c.VD:(cj + 1) * c.VD],
                        outlatT[:, cj, :], start=(cj == 0),
                        stop=(cj == c.CJ - 1)),
                        sem=(s_pe if cj == c.CJ - 1 else None))
                P.wait("vector", s_pe)
                P.op("vector", lambda e, h=h: e.tensor_copy(
                    out=headso[:, h, :], in_=pR[0:c.VD, 0:c.TC]), sem=s_v)
            wo_ck = []
            for ft in range(c.HT):
                par = ft % 2
                if ft == 0:
                    wstream_barrier()
                if ft >= 2:
                    P.wait("sync", s_pe, wo_ck[ft - 1])
                P.dma("sync", s_wb[par], wo_b[:, par], wo_s[ft])
                P.wait("tensor", s_wb[par])
                P.wait("tensor", s_v)
                for ht in range(c.NHT):
                    P.op("tensor", lambda e, par=par, ht=ht: e.matmul(
                        pD[par][:, 0:c.TC], wo_b[:, par, ht, :],
                        headso[:, ht, :], start=(ht == 0),
                        stop=(ht == c.NHT - 1)),
                        sem=(s_pe if ht == c.NHT - 1 else None))
                wo_ck.append(s_pe.n)
                P.wait("vector", s_pe)
                P.op("vector", lambda e, par=par, ft=ft, T0=T0:
                     e.tensor_tensor(
                         out=stream[:, ft, T0:T0 + c.TC],
                         in0=stream[:, ft, T0:T0 + c.TC],
                         in1=pD[par][:, 0:c.TC], op=ALU.add), sem=s_v)

    def store_stream(sseq):
        for ch in range(c.NCH):
            T0 = ch * c.TC
            for ht in range(c.HT):
                rp = ht % 2
                P.wait("vector", s_pe)
                P.wait("vector", s_s)
                P.op("vector", lambda e, ht=ht, T0=T0, rp=rp: e.tensor_copy(
                    out=res_t[:, rp, :], in_=stream[:, ht, T0:T0 + c.TC]),
                    sem=s_v)
                for b in range(c.RPC):
                    tp = b % 2
                    P.wait("tensor", s_v)
                    P.wait("tensor", s_y[tp])
                    P.op("tensor", lambda e, rp=rp, b=b: e.transpose(
                        pR[:, 0:128], res_t[:, rp, b * 128:(b + 1) * 128],
                        ident32b[:]), sem=s_pe)
                    P.wait("vector", s_pe)
                    P.op("vector", lambda e, tp=tp: e.tensor_copy(
                        out=tstage[:, tp, :], in_=pR[:, 0:128]), sem=s_v)
                    P.wait("scalar", s_v)
                    P.dma("scalar", s_y[tp],
                          y_d[sseq, T0 + b * 128:T0 + (b + 1) * 128,
                              ht * 128:(ht + 1) * 128], tstage[:, tp, :])

    def load_x(sseq):
        slot_ck = [0, 0, 0]
        for tr in range(c.S // 128):
            P.wait("scalar", s_v)
            P.dma("scalar", s_c, xstage[:],
                  x_d[sseq, tr * 128:(tr + 1) * 128])
            if sseq == 0:
                pump(2)
            P.wait("vector", s_c)
            P.wait("vector", s_pe)
            P.op("vector", lambda e: e.tensor_copy(
                out=xstage_b[:], in_=xstage[:]))
            P.drain("vector", s_v)
            P.wait("tensor", s_v)
            for ft in range(NFT):
                mm_transpose(
                    stream[:, ft, tr * 128:(tr + 1) * 128],
                    xstage_b[:, ft * 128:(ft + 1) * 128], slot_ck, ft)

    P.wait("scalar", s_c)
    for sseq in range(c.NSEQ):
        load_x(sseq)
        for pi_, ph in enumerate(phases):
            last = pi_ == len(phases) - 1
            if ph in ("ffn1", "ffn2"):
                ffn(sseq, final=last)
            elif ph == "attn":
                attention(sseq)
                if last:
                    store_stream(sseq)
            elif ph == "none":
                store_stream(sseq)
    for _ in bg:
        pass
    for tp in range(4):
        P.wait("sync", s_y[tp])
    P.wait("sync", s_c)

    with nc.Block() as block:
        @block.sync
        def _(eng):
            for f in P.ops["sync"]:
                f(eng)

        @block.vector
        def _(eng):
            for f in P.ops["vector"]:
                f(eng)

        @block.scalar
        def _(eng):
            for f in P.ops["scalar"]:
                f(eng)

        @block.tensor
        def _(eng):
            for f in P.ops["tensor"]:
                f(eng)

        @block.gpsimd
        def _(eng):
            for f in P.ops["gpsimd"]:
                f(eng)

    return nc, st



def kernel(**inputs):
    cfg = Cfg()
    xp, shared = host_prepare(cfg, **inputs)
    nc, st = build_nc(cfg, drains=True)
    in_maps = []
    for core in range(cfg.NCORE):
        m = dict(x2=np.ascontiguousarray(
            xp[core * cfg.NSEQ:(core + 1) * cfg.NSEQ]))
        m.update(shared)
        in_maps.append(m)
    res = run_bass_kernel_spmd(nc, in_maps, core_ids=list(range(cfg.NCORE)))
    st.close()
    inv = np.argsort(_deint(cfg.D))
    y = np.concatenate([r["y"] for r in res.results], axis=0)
    return np.ascontiguousarray(y[:, :, inv])


# version 20
# speedup vs baseline: 1.0435x; 1.0118x over previous
import contextlib
import os

os.environ.setdefault("GOMEMLIMIT", "26GiB")

import numpy as np
import ml_dtypes

import concourse.bass as bass
import concourse.mybir as mybir
from concourse.bass_utils import run_bass_kernel_spmd
from concourse.alu_op_type import AluOpType as ALU

AF = mybir.ActivationFunctionType
AX = mybir.AxisListType
F32 = mybir.dt.float32
BF16 = mybir.dt.bfloat16
U8 = mybir.dt.uint8
BF = ml_dtypes.bfloat16


class Cfg:
    def __init__(self, S=2048, D=2048, NH=16, RANK=512, NOPE=128, ROPE=64,
                 VD=128, MLP=16384, TC=512, NSEQ=2, NCORE=8):
        self.S, self.D, self.NH, self.RANK = S, D, NH, RANK
        self.NOPE, self.ROPE, self.VD, self.MLP = NOPE, ROPE, VD, MLP
        self.TC, self.NSEQ, self.NCORE = TC, NSEQ, NCORE
        self.KT = D // 128
        self.NMB = MLP // 128
        self.NQ = 8
        self.MBQ = self.NMB // self.NQ
        self.NCH = S // TC
        self.RPC = TC // 128
        self.HT = D // 128
        self.CJ = RANK // 128
        self.NHT = NH * VD // 128
        self.QK = NOPE + ROPE
        self.SCALE = self.QK ** -0.5
        self.THETA = 10000.0
        self.DC = min(512, MLP)
        self.COMPACT = (D == 2048 and MLP == 16384 and S == 2048)
        assert TC % 128 == 0 and S % TC == 0 and D % 128 == 0
        assert MLP % (128 * self.NQ) == 0 and RANK % 128 == 0



def _deint(n):
    return np.concatenate([np.arange(0, n, 2), np.arange(1, n, 2)])


def host_prepare(cfg, L0_attn_x, L0_attn_wq, L0_attn_wkv_a, L0_attn_wkv_b,
                 L0_attn_wo, L0_attn_kv_norm_w, L0_ffn_gate_W_packed,
                 L0_ffn_gate_scales, L0_ffn_up_W_packed, L0_ffn_up_scales,
                 L0_ffn_down_W_packed, L0_ffn_down_scales):
    c = cfg
    perm_k = _deint(c.D)
    perm_m = _deint(c.MLP)

    x = np.asarray(L0_attn_x, dtype=np.float32)
    xp = np.ascontiguousarray(x[:, :, perm_k])

    wq = np.asarray(L0_attn_wq, dtype=np.float32).reshape(c.NH, c.QK, c.D)
    wq_r = np.concatenate(
        [wq[:, :c.NOPE], wq[:, c.NOPE::2], wq[:, c.NOPE + 1::2]], axis=1)
    wq_t = np.ascontiguousarray(wq_r[:, :, perm_k].transpose(0, 2, 1))

    wkv_a = np.asarray(L0_attn_wkv_a, dtype=np.float32)
    wkv_a_r = np.concatenate(
        [wkv_a[:c.RANK], wkv_a[c.RANK::2], wkv_a[c.RANK + 1::2]], axis=0)
    wkv_a_t = np.ascontiguousarray(wkv_a_r[:, perm_k].T)

    wkv_b = np.asarray(L0_attn_wkv_b, dtype=np.float32).reshape(
        c.NH, c.NOPE + c.VD, c.RANK)
    wkvb_k = np.ascontiguousarray(wkv_b[:, :c.NOPE, :])
    wkvb_vT = np.ascontiguousarray(wkv_b[:, c.NOPE:, :].transpose(0, 2, 1))

    wo = np.asarray(L0_attn_wo, dtype=np.float32)
    wo_t = np.ascontiguousarray(wo[perm_k, :].T)

    kvnw = np.ascontiguousarray(
        np.asarray(L0_attn_kv_norm_w, dtype=np.float32).reshape(c.CJ, 128).T)

    def pack_t(w_packed, row_perm, col_perm=None):
        a = np.asarray(w_packed)
        assert a.dtype == np.int32
        u = a.view(np.uint8)[:, ::4]
        if row_perm is not None:
            u = u[row_perm]
        if col_perm is not None:
            u = u[col_perm]
        return np.ascontiguousarray(u.T)

    gpk_t = pack_t(L0_ffn_gate_W_packed, perm_m)
    upk_t = pack_t(L0_ffn_up_W_packed, perm_m)
    dpk_t = pack_t(L0_ffn_down_W_packed, None, col_perm=perm_k)

    gsc_t = np.ascontiguousarray(
        np.asarray(L0_ffn_gate_scales, dtype=np.float32)[perm_m].T)
    usc_t = np.ascontiguousarray(
        np.asarray(L0_ffn_up_scales, dtype=np.float32)[perm_m].T)
    dsc_t = np.ascontiguousarray(
        np.asarray(L0_ffn_down_scales, dtype=np.float32)[perm_k].T)

    ident = np.eye(128, dtype=BF)
    ident32 = np.eye(128, dtype=np.float32)
    ones = np.ones((128, 128), dtype=BF)
    masks = np.zeros((128, c.TC), dtype=np.float32)
    for i in range(128):
        masks[i, i + 1:] = -1e30
    masks = masks.astype(BF)
    freqs = 1.0 / c.THETA ** (
        np.arange(0, c.ROPE, 2, dtype=np.float32) / c.ROPE)
    ang = np.outer(np.arange(c.S, dtype=np.float32), freqs)
    cos_t = np.ascontiguousarray(np.cos(ang).T).astype(BF)
    sin_t = np.ascontiguousarray(np.sin(ang).T).astype(BF)

    shared = dict(
        wq_t=wq_t, wkv_a_t=wkv_a_t, wkvb_k=wkvb_k, wkvb_vT=wkvb_vT,
        wo_t=wo_t, kvnw=kvnw, gpk_t=gpk_t, upk_t=upk_t, dpk_t=dpk_t,
        gsc_t=gsc_t, usc_t=usc_t, dsc_t=dsc_t, ident=ident, ident32=ident32,
        ones=ones, masks=masks, cos_t=cos_t, sin_t=sin_t)
    return xp, shared



class Sem:
    def __init__(self, handle, name):
        self.h = handle
        self.name = name
        self.n = 0


class Plan:
    def __init__(self, drains=True):
        self.drains = drains
        self.ops = {k: [] for k in
                    ("sync", "vector", "scalar", "tensor", "gpsimd")}
        self.last_data = {k: False for k in self.ops}
        self.waited = {k: {} for k in self.ops}

    def emit(self, eng, f, data=False):
        self.ops[eng].append(f)
        self.last_data[eng] = data

    def wait(self, eng, sem, val=None):
        v = sem.n if val is None else val
        if v <= self.waited[eng].get(id(sem), 0):
            return
        self.waited[eng][id(sem)] = v
        self.emit(eng, lambda e, s=sem.h, v=v: e.wait_ge(s, v))

    def dma(self, eng, sem, out, in_, inc=16):
        sem.n += inc
        self.emit(eng, lambda e, o=out, i=in_, s=sem.h, k=inc:
                  e.dma_start(out=o, in_=i).then_inc(s, k), data=True)

    def op(self, eng, f, sem=None, inc=1):
        if sem is not None:
            sem.n += inc
            self.emit(eng, lambda e, s=sem.h, k=inc, ff=f:
                      ff(e).then_inc(s, k), data=True)
        else:
            self.emit(eng, f, data=True)

    def drain(self, eng, sem=None, inc=1):
        elide = (not self.drains) and eng == "vector"
        if sem is None:
            if not elide:
                self.emit(eng, lambda e: e.drain())
            return
        sem.n += inc
        if not elide or not self.last_data[eng]:
            self.emit(eng, lambda e, s=sem.h, k=inc: e.drain().then_inc(s, k))
        else:
            last = self.ops[eng][-1]

            def wrapped(e, f=last, s=sem.h, k=inc):
                r = f(e)
                r.then_inc(s, k)
                return r
            self.ops[eng][-1] = wrapped
            self.last_data[eng] = False



def build_nc(cfg, phases=("ffn1", "attn", "ffn2"), drains=True):
    c = cfg
    nc = bass.Bass("TRN2", target_bir_lowering=False, debug=False,
                   detect_race_conditions=drains)
    st = contextlib.ExitStack()

    def din(name, shape, dt):
        return nc.dram_tensor(name, list(shape), dt, kind="ExternalInput").ap()

    x_d = din("x2", (c.NSEQ, c.S, c.D), F32)
    wq_d = din("wq_t", (c.NH, c.D, c.QK), F32)
    wkva_d = din("wkv_a_t", (c.D, c.RANK + c.ROPE), F32)
    wkbk_d = din("wkvb_k", (c.NH, c.NOPE, c.RANK), F32)
    wkbv_d = din("wkvb_vT", (c.NH, c.RANK, c.VD), F32)
    wo_d = din("wo_t", (c.NH * c.VD, c.D), F32)
    kvnw_d = din("kvnw", (128, c.CJ), F32)
    gpk_d = din("gpk_t", (c.D // 2, c.MLP), U8)
    upk_d = din("upk_t", (c.D // 2, c.MLP), U8)
    dpk_d = din("dpk_t", (c.MLP // 2, c.D), U8)
    gsc_d = din("gsc_t", (c.D // 32, c.MLP), F32)
    usc_d = din("usc_t", (c.D // 32, c.MLP), F32)
    dsc_d = din("dsc_t", (c.MLP // 32, c.D), F32)
    iden_d = din("ident", (128, 128), BF16)
    iden32_d = din("ident32", (128, 128), F32)
    ones_d = din("ones", (128, 128), BF16)
    mask_d = din("masks", (128, c.TC), BF16)
    cos_d = din("cos_t", (c.ROPE // 2, c.S), BF16)
    sin_d = din("sin_t", (c.ROPE // 2, c.S), BF16)
    y_d = nc.dram_tensor("y", [c.NSEQ, c.S, c.D], F32,
                         kind="ExternalOutput").ap()

    def scr(name, shape, dt):
        return nc.dram_tensor(name, list(shape), dt, kind="Internal").ap()

    gsc_s = scr("gsc_s", (c.D // 2, c.MLP), F32)
    usc_s = scr("usc_s", (c.D // 2, c.MLP), F32)
    dsc_s = scr("dsc_s", (c.MLP // 2, c.D), F32)
    wg_s = scr("wg_s", (c.NMB, 128, c.KT, 128), BF16)
    wu_s = scr("wu_s", (c.NMB, 128, c.KT, 128), BF16)
    wd_s = scr("wd_s", (c.HT, c.NMB, 128, 128), BF16)
    wqa_s = scr("wqa_s", (c.NH, 128, c.KT, c.QK), BF16)
    wkva_s = scr("wkva_s", (128, c.KT, c.RANK + c.ROPE), BF16)
    wkbk_s = scr("wkbk_s", (c.NH, 128, c.CJ * 128), BF16)
    wkbv_s = scr("wkbv_s", (c.NH, 128, c.CJ * c.VD), BF16)
    wo_s = scr("wo_s", (c.HT, 128, c.NHT, 128), BF16)

    base0 = (nc.sbuf_base + 31) // 32 * 32
    cur = [base0]
    offs = {}

    def _nb(shape, dt):
        n = 1
        for d in shape[1:]:
            n *= d
        sz = {F32: 4, BF16: 2, U8: 1}[dt]
        return ((n * sz + 31) // 32) * 32

    sizes = {}

    def sb(name, shape, dt, at=None, spill=False):
        nb = _nb(shape, dt)
        off = None
        if at is not None and c.COMPACT:
            host, delta = at
            if spill or delta + nb <= sizes[host]:
                off = offs[host] + delta
        if off is None:
            off = cur[0]
            cur[0] += nb
        offs[name] = off
        sizes[name] = nb
        return nc.alloc_sbuf_tensor_at(name, list(shape), dt, offset=off)

    NFT = c.D // 128
    stream = sb("stream", (128, NFT, c.S), BF16)
    xn = sb("xn", (128, NFT, c.TC), BF16)
    interq = sb("interq", (128, c.MBQ, c.TC), BF16)
    outp = sb("outp", (128, c.HT, c.TC), F32)
    wbuf_g = sb("wbuf_g", (128, 2, 2, c.KT, 128), BF16)
    wbuf_u = sb("wbuf_u", (128, 2, 2, c.KT, 128), BF16)
    gel = sb("gel", (128, 2, c.TC), BF16)
    sqb = sb("sqb", (128, 2, c.TC), BF16)
    mean_rep = sb("mean_rep", (128, c.TC), BF16)
    rstd_rep = sb("rstd_rep", (128, c.TC), BF16)
    stat = sb("stat", (33, c.TC), F32)
    statb = sb("statb", (1, c.TC), BF16)
    statc = sb("statc", (1, c.TC), BF16)
    res_t = sb("res_t", (128, 2, c.TC), F32)
    tstage = sb("tstage", (128, 2, 128), F32)
    rmax = sb("rmax", (128, 8), F32)
    psum_r = sb("psum_r", (128, 4), F32)
    identb = sb("identb", (128, 128), BF16)
    ident32b = sb("ident32b", (128, 128), F32)
    onesb = sb("onesb", (128, 128), BF16)
    maskb = sb("maskb", (128, c.TC), BF16)
    cosb = sb("cosb", (c.ROPE // 2, c.S), BF16)
    sinb = sb("sinb", (c.ROPE // 2, c.S), BF16)
    kvnwb = sb("kvnwb", (128, c.CJ), F32)
    DCB = c.DC
    pkb = sb("pkb", (128, 2, DCB), U8)
    scb = sb("scb", (128, 2, DCB), F32)
    d_u8 = sb("d_u8", (128, 2 * DCB), U8)
    d_f2 = sb("d_f2", (128, 2 * DCB), BF16)
    d_ge8 = sb("d_ge8", (128, 2 * DCB), BF16)
    d_m = sb("d_m", (128, 2 * DCB), BF16)
    d_t2 = sb("d_t2", (128, 2 * DCB), BF16)
    d_a = sb("d_a", (128, 2 * DCB), BF16)
    d_val = sb("d_val", (128, 1, 2 * DCB), BF16)
    d_pl = sb("d_pl", (128, 1, 2, DCB), BF16)
    statm2 = sb("statm2", (1, c.TC), F32, at=("mean_rep", 0), spill=True)
    dwbuf = sb("dwbuf", (128, 2, c.MBQ, 128), BF16, at=("wbuf_g", 0))
    wc16 = sb("wc16", (128, 2, 2048), BF16, at=("pkb", 0), spill=True)
    xstage = sb("xstage", (128, c.D), F32, at=("interq", 0), spill=True)
    xstage_b = sb("xstage_b", (128, c.D), BF16,
                  at=("interq", 4 * c.D), spill=True)
    kvT = sb("kvT", (128, c.CJ, c.S), BF16, at=("outp", 0), spill=True)
    pT = sb("pT", (128, c.S // 128, c.TC), BF16,
            at=("outp", 2 * c.CJ * c.S), spill=True)
    kvlat = sb("kvlat", (128, c.CJ, c.TC), BF16,
               at=("outp", 2 * c.CJ * c.S), spill=True)
    kvtok = sb("kvtok", (128, c.S // 128, c.RANK), BF16,
               at=("pkb", 0), spill=True)
    headso = sb("headso", (128, c.NHT, c.TC), BF16,
                at=("interq", 0), spill=True)
    prow = sb("prow", (128, 2, c.S), BF16, at=("wbuf_g", 0), spill=True)
    kpeT = sb("kpeT", (c.ROPE, c.S), BF16, at=("wbuf_g", 4 * c.S),
              spill=True)
    qlatT = sb("qlatT", (128, c.CJ, c.TC), BF16,
               at=("wbuf_g", 6 * c.S), spill=True)
    outlatT = sb("outlatT", (128, c.CJ, c.TC), BF16,
                 at=("mean_rep", 0), spill=True)
    wqh_b = sb("wqh_b", (128, 2, c.KT, c.QK), BF16, at=("wbuf_u", 0))
    qn_h = sb("qn_h", (128, c.TC), BF16, at=("wbuf_u", 4 * c.KT * c.QK))
    qpe_h = sb("qpe_h", (c.ROPE, c.TC), BF16,
               at=("wbuf_u", 4 * c.KT * c.QK + 2 * c.TC))
    wkbk_b = sb("wkbk_b", (128, 2, c.CJ * 128), BF16,
                at=("wbuf_u", 4 * c.KT * c.QK + 4 * c.TC))
    wkva_b = sb("wkva_b", (128, 2, c.KT, 128), BF16, at=("wqh_b", 0))
    wo_b = sb("wo_b", (128, 2, c.NHT, 128), BF16, at=("wqh_b", 0))
    wkbv_b = sb("wkbv_b", (128, 2, c.CJ * c.VD), BF16, at=("gel", 0))
    ropet = sb("ropet", (c.ROPE // 2, 2, c.TC), F32, at=("res_t", 0))
    nc.bump_sbuf(cur[0] - nc.sbuf_base)
    pG0 = st.enter_context(nc.psum_tensor("pG0", [128, 512], F32))
    pG1 = st.enter_context(nc.psum_tensor("pG1", [128, 512], F32))
    pU0 = st.enter_context(nc.psum_tensor("pU0", [128, 512], F32))
    pU1 = st.enter_context(nc.psum_tensor("pU1", [128, 512], F32))
    pD0 = st.enter_context(nc.psum_tensor("pD0", [128, 512], F32))
    pD1 = st.enter_context(nc.psum_tensor("pD1", [128, 512], F32))
    pR = st.enter_context(nc.psum_tensor("pR", [128, 512], F32))
    pTB = st.enter_context(nc.psum_tensor("pTB", [128, 1024], BF16))
    pG = [pG0, pG1]
    pU = [pU0, pU1]
    pD = [pD0, pD1]
    pS = [pG0, pG1, pU0, pU1]
    tbank_list = [pD0, pD1, pR]

    def sem(name):
        return Sem(st.enter_context(nc.semaphore(name)), name)

    s_c = sem("s_c")
    s_g = sem("s_g")
    s_g2 = sem("s_g2")
    s_g3 = sem("s_g3")
    s_g4 = sem("s_g4")
    tbanks = tbank_list
    gq_sems = [s_g, s_g2, s_g3, s_g4]
    gq_idx = [0]

    def gdma(out, in_):
        s = gq_sems[gq_idx[0] % 4]
        gq_idx[0] += 1
        P.wait("gpsimd", s)
        P.dma("gpsimd", s, out, in_)

    def gq_snapshot():
        return [s.n for s in gq_sems]

    def gq_wait(eng, snap=None):
        for s, v in zip(gq_sems, snap or [s.n for s in gq_sems]):
            P.wait(eng, s, v)
    s_wg = [sem("s_wg0"), sem("s_wg1")]
    s_wu = [sem("s_wu0"), sem("s_wu1")]
    s_wd = [sem("s_wd0"), sem("s_wd1")]
    s_wa = [sem("s_wa0"), sem("s_wa1")]
    s_wb = [sem("s_wb0"), sem("s_wb1")]
    s_pk = [sem("s_pk0"), sem("s_pk1")]
    s_gp = sem("s_gp")
    s_pe = sem("s_pe")
    s_v = sem("s_v")
    s_s = sem("s_s")
    s_y = [sem("s_y0"), sem("s_y1"), sem("s_y2"), sem("s_y3")]

    P = Plan(drains=drains)

    def wstream_barrier(eng="sync"):
        for s in (s_wg[0], s_wg[1], s_wu[0], s_wu[1], s_wd[0], s_wd[1],
                  s_wa[0], s_wa[1], s_wb[0], s_wb[1]):
            P.wait(eng, s)
        P.wait(eng, s_pe)
        P.wait(eng, s_v)


    def mm_transpose(dst_ap, src_ap, ck, idx):
        sl = idx % 3
        bank = tbanks[sl]
        P.wait("tensor", s_v, ck[sl])
        P.op("tensor", lambda e, b=bank, s=src_ap: e.matmul(
            b[:, 0:128], s, identb[:], start=True, stop=True), sem=s_pe)
        P.wait("vector", s_pe)
        P.op("vector", lambda e, b=bank, d=dst_ap: e.tensor_copy(
            out=d, in_=b[:, 0:128]), sem=s_v)
        ck[sl] = s_v.n

    P.dma("scalar", s_c, identb[:], iden_d[:])
    P.dma("scalar", s_c, ident32b[:], iden32_d[:])
    P.dma("scalar", s_c, onesb[:], ones_d[:])
    P.dma("scalar", s_c, maskb[:], mask_d[:])
    P.dma("scalar", s_c, cosb[:], cos_d[:])
    P.dma("scalar", s_c, sinb[:], sin_d[:])
    P.dma("scalar", s_c, kvnwb[:], kvnw_d[:])

    for (src_, dst) in ((gsc_d, gsc_s), (usc_d, usc_s), (dsc_d, dsc_s)):
        for i in range(16):
            gdma(dst.rearrange("(u k) m -> u k m", k=16)[:, i, :], src_[:])

    cast_jobs = []
    for h in range(c.NH):
        for kt in range(c.KT):
            cast_jobs.append((wq_d[h, kt * 128:(kt + 1) * 128, :],
                              wqa_s[h, :, kt, :], c.QK))
    for kt in range(c.KT):
        cast_jobs.append((wkva_d[kt * 128:(kt + 1) * 128, :],
                          wkva_s[:, kt, :], c.RANK + c.ROPE))
    for h in range(c.NH):
        cast_jobs.append((wkbk_d[h], wkbk_s[h], c.CJ * 128))
        cast_jobs.append((wkbv_d[h].rearrange("(j p) v -> p j v", p=128),
                          wkbv_s[h], c.CJ * c.VD))
    for ft in range(c.HT):
        cast_jobs.append((
            wo_d.rearrange("(t p) f -> p t f", p=128)[
                :, :, ft * 128:(ft + 1) * 128],
            wo_s[ft], c.NHT * 128))
    for j, (src_ap, dst_ap, ncol) in enumerate(cast_jobs):
        par = j % 2
        if j >= 2:
            gq_wait("gpsimd")
        gdma(wc16[:, par, :ncol], src_ap)
        gq_wait("gpsimd")
        gdma(dst_ap, wc16[:, par, :ncol])

    def decode_tile(par, dc):
        pk = pkb[:, par, :dc]
        w = 2 * dc
        P.op("vector", lambda e, pk=pk, dc=dc: e.tensor_scalar(
            out=d_u8[:, :dc], in0=pk, scalar1=15, scalar2=None,
            op0=ALU.bitwise_and))
        P.op("vector", lambda e, pk=pk, dc=dc: e.tensor_scalar(
            out=d_u8[:, dc:2 * dc], in0=pk, scalar1=4, scalar2=None,
            op0=ALU.logical_shift_right))
        P.drain("vector")
        P.op("vector", lambda e, w=w: e.tensor_copy(
            out=d_f2[:, :w], in_=d_u8[:, :w]))
        P.drain("vector")
        P.op("vector", lambda e, w=w: e.tensor_scalar(
            out=d_ge8[:, :w], in0=d_f2[:, :w], scalar1=8.0, scalar2=None,
            op0=ALU.is_ge))
        P.drain("vector")
        P.op("vector", lambda e, w=w: e.scalar_tensor_tensor(
            out=d_m[:, :w], in0=d_ge8[:, :w], scalar=-8.0, in1=d_f2[:, :w],
            op0=ALU.mult, op1=ALU.add))
        P.drain("vector", s_v)
        P.wait("scalar", s_v)
        P.op("scalar", lambda e, w=w: e.activation(
            out=d_t2[:, :w], in_=d_m[:, :w], func=AF.Copy,
            scale=3.0, bias=-15.0), sem=s_s)
        P.wait("vector", s_s)
        P.op("vector", lambda e, w=w: e.scalar_tensor_tensor(
            out=d_a[:, :w], in0=d_m[:, :w], scalar=0.5, in1=d_t2[:, :w],
            op0=ALU.mult, op1=ALU.max))
        P.drain("vector")
        P.op("vector", lambda e, w=w: e.scalar_tensor_tensor(
            out=d_t2[:, :w], in0=d_m[:, :w], scalar=-2.0, in1=d_a[:, :w],
            op0=ALU.add, op1=ALU.max))
        P.drain("vector")
        P.op("vector", lambda e, w=w: e.scalar_tensor_tensor(
            out=d_m[:, :w], in0=d_ge8[:, :w], scalar=-2.0,
            in1=d_t2[:, :w], op0=ALU.mult, op1=ALU.mult))
        P.drain("vector")
        if gp_done:
            P.wait("vector", s_gp, gp_done[-1])
        P.op("vector", lambda e, w=w: e.tensor_tensor(
            out=d_val[:, 0, :w], in0=d_t2[:, :w], in1=d_m[:, :w],
            op=ALU.add))
        P.drain("vector", s_v)
        P.wait("gpsimd", s_v)
        if g_done:
            gq_wait("gpsimd", g_done[-1])
        P.op("gpsimd", lambda e, dc=dc, par=par: e.tensor_tensor(
            out=d_pl[:, 0, 0, :dc], in0=d_val[:, 0, :dc],
            in1=scb[:, par, :dc], op=ALU.mult))
        P.op("gpsimd", lambda e, dc=dc, par=par: e.tensor_tensor(
            out=d_pl[:, 0, 1, :dc], in0=d_val[:, 0, dc:2 * dc],
            in1=scb[:, par, :dc], op=ALU.mult))
        P.drain("gpsimd", s_gp)

    v_done = []
    gp_done = []
    g_done = []
    eighth_ck = {}

    def decode_one(pk_src, sc_src, j, cc, dc, write_plane):
        ti = len(v_done)
        par = ti % 2
        if ti >= 2:
            P.wait("sync", s_gp, gp_done[ti - 2])
        P.dma("sync", s_pk[par], pkb[:, par, :dc],
              pk_src[j * 128:(j + 1) * 128, cc * dc:(cc + 1) * dc])
        P.dma("sync", s_pk[par], scb[:, par, :dc],
              sc_src[j * 128:(j + 1) * 128, cc * dc:(cc + 1) * dc])
        P.wait("vector", s_pk[par])
        decode_tile(par, dc)
        P.drain("vector", s_v)
        v_done.append(s_v.n)
        write_plane(j, cc, par)
        g_done.append(gq_snapshot())
        gp_done.append(s_gp.n)

    def wg_write(which_s):
        def w(j, cc, par):
            dc = c.DC
            for pi, kt in ((0, j), (1, c.KT // 2 + j)):
                nmb = dc // 128
                gdma(which_s[cc * nmb:(cc + 1) * nmb, :, kt, :].rearrange(
                         "b p m -> p b m"),
                     d_pl[:, 0, pi, :dc])
        return w

    def wd_write(j, cc, par):
        dc = c.DC
        for pi, mb in ((0, j), (1, c.NMB // 2 + j)):
            nht = dc // 128
            gdma(wd_s[cc * nht:(cc + 1) * nht, mb, :, :].rearrange(
                     "t p h -> p t h"),
                 d_pl[:, 0, pi, :dc])

    def decode_gen():
        gw = wg_write(wg_s)
        uw = wg_write(wu_s)
        ncc_g = c.DC // 128
        done_g, done_u = set(), set()
        for q8 in range(c.NQ):
            mb_lo, mb_hi = q8 * c.MBQ, (q8 + 1) * c.MBQ
            ccs = range(mb_lo // ncc_g, -(-mb_hi // ncc_g))
            for cc in ccs:
                if cc in done_g:
                    continue
                done_g.add(cc)
                for j in range(c.D // 256):
                    decode_one(gpk_d, gsc_s, j, cc, c.DC, gw)
                    yield 1
            for cc in ccs:
                if cc in done_u:
                    continue
                done_u.add(cc)
                for j in range(c.D // 256):
                    decode_one(upk_d, usc_s, j, cc, c.DC, uw)
                    yield 1
            if q8 < c.NQ // 2:
                for j in range(mb_lo, mb_hi):
                    for cc in range(c.D // c.DC):
                        decode_one(dpk_d, dsc_s, j, cc, c.DC, wd_write)
                        yield 1
            eighth_ck[q8] = (gq_snapshot(), s_gp.n, s_v.n)

    gq_wait("sync")
    bg = decode_gen()

    def pump(n=1):
        for _ in range(n):
            if next(bg, None) is None:
                return

    def need_eighth(q8):
        while q8 not in eighth_ck:
            if next(bg, None) is None:
                break

    def layer_norm(T0):
        P.wait("scalar", s_v)
        P.wait("vector", s_pe)
        P.wait("tensor", s_v)
        ln_ck = []
        for ft in range(NFT):
            par = ft % 2
            if ft >= 2:
                P.wait("scalar", s_pe, ln_ck[ft - 2])
            P.op("scalar", lambda e, ft=ft, par=par, T0=T0: e.activation(
                out=sqb[:, par, :], in_=stream[:, ft, T0:T0 + c.TC],
                func=AF.Square), sem=s_s)
            P.wait("tensor", s_s)
            P.op("tensor", lambda e, ft=ft, T0=T0: e.matmul(
                pR[0:1, 0:c.TC], onesb[:, 0:1], stream[:, ft, T0:T0 + c.TC],
                start=(ft == 0), stop=(ft == NFT - 1)))
            P.op("tensor", lambda e, par=par, ft=ft: e.matmul(
                pR[32:33, 0:c.TC], onesb[:, 0:1], sqb[:, par, :],
                start=(ft == 0), stop=(ft == NFT - 1)), sem=s_pe)
            ln_ck.append(s_pe.n)
        P.wait("vector", s_pe)
        P.op("vector", lambda e: e.tensor_scalar(
            out=stat[0:1, :], in0=pR[0:1, 0:c.TC], scalar1=1.0 / c.D,
            scalar2=None, op0=ALU.mult))
        P.op("vector", lambda e: e.tensor_scalar(
            out=stat[32:33, :], in0=pR[32:33, 0:c.TC], scalar1=1.0 / c.D,
            scalar2=None, op0=ALU.mult))
        P.drain("vector")
        P.op("vector", lambda e: e.tensor_tensor(
            out=statb[0:1, :], in0=stat[0:1, :], in1=stat[0:1, :],
            op=ALU.mult))
        P.op("vector", lambda e: e.tensor_copy(
            out=statm2[0:1, :], in_=stat[32:33, :]))
        P.drain("vector")
        P.op("vector", lambda e: e.tensor_tensor(
            out=statm2[0:1, :], in0=statm2[0:1, :], in1=statb[0:1, :],
            op=ALU.subtract))
        P.drain("vector")
        P.op("vector", lambda e: e.tensor_scalar(
            out=statm2[0:1, :], in0=statm2[0:1, :], scalar1=1e-5,
            scalar2=None, op0=ALU.add))
        P.drain("vector", s_v)
        P.wait("scalar", s_v)
        P.op("scalar", lambda e: e.activation(
            out=statm2[0:1, :], in_=statm2[0:1, :], func=AF.Sqrt), sem=s_s)
        P.wait("vector", s_s)
        P.op("vector", lambda e: e.reciprocal(
            out=statm2[0:1, :], in_=statm2[0:1, :]))
        P.drain("vector")
        P.op("vector", lambda e: e.tensor_copy(
            out=statb[0:1, :], in_=stat[0:1, :]))
        P.op("vector", lambda e: e.tensor_copy(
            out=statc[0:1, :], in_=statm2[0:1, :]))
        P.drain("vector", s_v)
        P.wait("tensor", s_v)
        P.op("tensor", lambda e: e.matmul(
            pR[:, 0:c.TC], onesb[0:1, :], statb[0:1, :],
            start=True, stop=True), sem=s_pe)
        P.wait("vector", s_pe)
        P.op("vector", lambda e: e.tensor_copy(
            out=mean_rep[:], in_=pR[:, 0:c.TC]))
        P.drain("vector", s_v)
        P.wait("tensor", s_v)
        P.op("tensor", lambda e: e.matmul(
            pR[:, 0:c.TC], onesb[0:1, :], statc[0:1, :],
            start=True, stop=True), sem=s_pe)
        P.wait("vector", s_pe)
        P.op("vector", lambda e: e.tensor_copy(
            out=rstd_rep[:], in_=pR[:, 0:c.TC]))
        P.drain("vector")
        for ft in range(NFT):
            P.op("vector", lambda e, ft=ft, T0=T0: e.scalar_tensor_tensor(
                out=xn[:, ft, :], in0=stream[:, ft, T0:T0 + c.TC],
                scalar=1.0, in1=mean_rep[:], op0=ALU.mult, op1=ALU.subtract))
        P.drain("vector")
        for ft in range(NFT):
            P.op("vector", lambda e, ft=ft: e.tensor_tensor(
                out=xn[:, ft, :], in0=xn[:, ft, :], in1=rstd_rep[:],
                op=ALU.mult))
        P.drain("vector", s_v)
        P.wait("tensor", s_v)
        P.wait("scalar", s_v)

    def ffn(sseq, final):
        prev_dn = [None]
        for ch in range(c.NCH):
            T0 = ch * c.TC
            layer_norm(T0)
            for q in range(c.NQ):
                evac_ck = []
                need_eighth(q)
                for g2 in range(c.MBQ // 2):
                    par = g2 % 2
                    mbg0 = q * c.MBQ + g2 * 2
                    if g2 >= 2:
                        P.wait("sync", s_v, evac_ck[2 * (g2 - 2) + 1])
                    if g2 == 0:
                        if prev_dn[0] is None:
                            wstream_barrier()
                        else:
                            P.wait("sync", s_v, prev_dn[0])
                        gq_wait("sync", eighth_ck[q][0])
                    P.dma("sync", s_wg[par], wbuf_g[:, par],
                          wg_s[mbg0:mbg0 + 2].rearrange("b p k m -> p b k m"))
                    P.dma("sync", s_wg[par], wbuf_u[:, par],
                          wu_s[mbg0:mbg0 + 2].rearrange("b p k m -> p b k m"))
                    for bi in range(2):
                        mb = g2 * 2 + bi
                        par2 = mb % 2
                        P.wait("tensor", s_wg[par])
                        if len(evac_ck) >= 2:
                            P.wait("tensor", s_v, evac_ck[-2])
                        for kt in range(c.KT):
                            P.op("tensor", lambda e, par=par, bi=bi, kt=kt,
                                 par2=par2: e.matmul(
                                     pG[par2][:, 0:c.TC],
                                     wbuf_g[:, par, bi, kt, :],
                                     xn[:, kt, :], start=(kt == 0),
                                     stop=(kt == c.KT - 1)),
                                 sem=(s_pe if kt == c.KT - 1 else None))
                        for kt in range(c.KT):
                            P.op("tensor", lambda e, par=par, bi=bi, kt=kt,
                                 par2=par2: e.matmul(
                                     pU[par2][:, 0:c.TC],
                                     wbuf_u[:, par, bi, kt, :],
                                     xn[:, kt, :], start=(kt == 0),
                                     stop=(kt == c.KT - 1)),
                                 sem=(s_pe if kt == c.KT - 1 else None))
                        P.wait("scalar", s_pe, s_pe.n - 1)
                        P.op("scalar", lambda e, par2=par2: e.activation(
                            out=gel[:, par2, :], in_=pG[par2][:, 0:c.TC],
                            func=AF.Gelu_apprx_tanh), sem=s_s)
                        P.wait("vector", s_s)
                        P.wait("vector", s_pe)
                        P.op("vector", lambda e, mb=mb, par2=par2:
                             e.tensor_tensor(
                                 out=interq[:, mb, :], in0=gel[:, par2, :],
                                 in1=pU[par2][:, 0:c.TC], op=ALU.mult),
                             sem=s_v)
                        evac_ck.append(s_v.n)
                        pump(1)
                dn_ck = []
                for ht in range(c.HT):
                    par = ht % 2
                    if ht >= 2:
                        P.wait("sync", s_v, dn_ck[ht - 2])
                    else:
                        P.wait("sync", s_v, evac_ck[-1])
                    P.dma("sync", s_wd[par], dwbuf[:, par],
                          wd_s[ht, q * c.MBQ:(q + 1) * c.MBQ].rearrange(
                              "b p h -> p b h"))
                    P.wait("tensor", s_wd[par])
                    P.wait("tensor", s_v, evac_ck[-1])
                    if ht >= 2:
                        P.wait("tensor", s_v, dn_ck[ht - 2])
                    pump(1)
                    for mb in range(c.MBQ):
                        P.op("tensor", lambda e, par=par, mb=mb: e.matmul(
                            pD[par][:, 0:c.TC], dwbuf[:, par, mb, :],
                            interq[:, mb, :], start=(mb == 0),
                            stop=(mb == c.MBQ - 1)),
                            sem=(s_pe if mb == c.MBQ - 1 else None))
                    P.wait("vector", s_pe)
                    if q == 0:
                        P.op("vector", lambda e, par=par, ht=ht:
                             e.tensor_copy(out=outp[:, ht, :],
                                           in_=pD[par][:, 0:c.TC]), sem=s_v)
                    elif q < c.NQ - 1:
                        P.op("vector", lambda e, par=par, ht=ht:
                             e.tensor_tensor(
                                 out=outp[:, ht, :], in0=outp[:, ht, :],
                                 in1=pD[par][:, 0:c.TC], op=ALU.add),
                             sem=s_v)
                    else:
                        P.op("vector", lambda e, par=par, ht=ht:
                             e.tensor_tensor(
                                 out=outp[:, ht, :], in0=outp[:, ht, :],
                                 in1=pD[par][:, 0:c.TC], op=ALU.add))
                        P.drain("vector")
                        if not final:
                            P.op("vector", lambda e, ht=ht, T0=T0:
                                 e.scalar_tensor_tensor(
                                     out=stream[:, ht, T0:T0 + c.TC],
                                     in0=outp[:, ht, :], scalar=0.5,
                                     in1=stream[:, ht, T0:T0 + c.TC],
                                     op0=ALU.mult, op1=ALU.add), sem=s_v)
                        else:
                            rp = ht % 2
                            P.op("vector", lambda e, ht=ht, T0=T0, rp=rp:
                                 e.scalar_tensor_tensor(
                                     out=res_t[:, rp, :],
                                     in0=outp[:, ht, :], scalar=0.5,
                                     in1=stream[:, ht, T0:T0 + c.TC],
                                     op0=ALU.mult, op1=ALU.add), sem=s_v)
                            for b in range(c.RPC):
                                tp = b % 2
                                P.wait("tensor", s_v)
                                P.wait("tensor", s_y[tp])
                                P.op("tensor", lambda e, rp=rp, b=b:
                                     e.transpose(
                                         pR[:, 0:128],
                                         res_t[:, rp, b * 128:(b + 1) * 128],
                                         ident32b[:]), sem=s_pe)
                                P.wait("vector", s_pe)
                                P.op("vector", lambda e, tp=tp:
                                     e.tensor_copy(out=tstage[:, tp, :],
                                                   in_=pR[:, 0:128]),
                                     sem=s_v)
                                P.wait("scalar", s_v)
                                P.dma("scalar", s_y[tp],
                                      y_d[sseq,
                                          T0 + b * 128:T0 + (b + 1) * 128,
                                          ht * 128:(ht + 1) * 128],
                                      tstage[:, tp, :])
                    dn_ck.append(s_v.n)
                prev_dn[0] = dn_ck[-1]

    def rope_apply(dst0, dst1, src_ps, Tsl):
        H = c.ROPE // 2
        cs = cosb[:, Tsl]
        sn = sinb[:, Tsl]
        P.op("vector", lambda e: e.tensor_tensor(
            out=ropet[:, 0, :], in0=src_ps[0:H, 0:c.TC], in1=cs, op=ALU.mult))
        P.op("vector", lambda e: e.tensor_tensor(
            out=ropet[:, 1, :], in0=src_ps[H:2 * H, 0:c.TC], in1=sn,
            op=ALU.mult))
        P.drain("vector")
        P.op("vector", lambda e, d=dst0: e.tensor_tensor(
            out=d, in0=ropet[:, 0, :], in1=ropet[:, 1, :], op=ALU.subtract))
        P.drain("vector")
        P.op("vector", lambda e: e.tensor_tensor(
            out=ropet[:, 0, :], in0=src_ps[0:H, 0:c.TC], in1=sn, op=ALU.mult))
        P.op("vector", lambda e: e.tensor_tensor(
            out=ropet[:, 1, :], in0=src_ps[H:2 * H, 0:c.TC], in1=cs,
            op=ALU.mult))
        P.drain("vector")
        P.op("vector", lambda e, d=dst1: e.tensor_tensor(
            out=d, in0=ropet[:, 0, :], in1=ropet[:, 1, :], op=ALU.add))
        P.drain("vector", s_v)

    def attention(sseq):
        for _ in bg:
            pass
        gq_wait("vector")
        P.wait("vector", s_gp)
        pt_slot_ck = [0, 0, 0]
        for qc in range(c.NCH):
            T0 = qc * c.TC
            Tsl = slice(T0, T0 + c.TC)
            layer_norm(T0)
            wstream_barrier()
            kv_ck = []
            for rt in range(c.CJ + 1):
                par = rt % 2
                ncol = 128 if rt < c.CJ else c.ROPE
                if rt >= 2:
                    P.wait("sync", s_pe, kv_ck[rt - 1])
                P.dma("sync", s_wa[par], wkva_b[:, par, :, 0:ncol],
                      wkva_s[:, :, rt * 128:rt * 128 + ncol])
                P.wait("tensor", s_wa[par])
                if rt >= 2:
                    P.wait("tensor", s_v)
                    P.wait("tensor", s_s)
                for kt in range(c.KT):
                    P.op("tensor", lambda e, par=par, kt=kt, nc_=ncol:
                         e.matmul(pD[par][0:nc_, 0:c.TC],
                                  wkva_b[:, par, kt, 0:nc_],
                                  xn[:, kt, :], start=(kt == 0),
                                  stop=(kt == c.KT - 1)),
                         sem=(s_pe if kt == c.KT - 1 else None))
                kv_ck.append(s_pe.n)
                if rt < c.CJ:
                    P.wait("vector", s_pe)
                    P.op("vector", lambda e, par=par, rt=rt: e.tensor_copy(
                        out=kvlat[:, rt, :], in_=pD[par][:, 0:c.TC]))
                    P.drain("vector", s_v)
                    P.wait("scalar", s_v)
                    P.op("scalar", lambda e, par=par, rt=rt: e.activation(
                        out=sqb[:, rt % 2, :], in_=pD[par][:, 0:c.TC],
                        func=AF.Square), sem=s_s)
                    P.wait("tensor", s_s)
                    P.op("tensor", lambda e, rt=rt: e.matmul(
                        pR[0:1, 0:c.TC], onesb[:, 0:1], sqb[:, rt % 2, :],
                        start=(rt == 0), stop=(rt == c.CJ - 1)), sem=s_pe)
                else:
                    P.wait("vector", s_pe)
                    rope_apply(kpeT[0:c.ROPE // 2, Tsl],
                               kpeT[c.ROPE // 2:c.ROPE, Tsl], pD[par], Tsl)
            P.wait("vector", s_pe)
            P.op("vector", lambda e: e.tensor_scalar(
                out=statm2[0:1, :], in0=pR[0:1, 0:c.TC],
                scalar1=1.0 / c.RANK, scalar2=1e-6, op0=ALU.mult,
                op1=ALU.add))
            P.drain("vector", s_v)
            P.wait("scalar", s_v)
            P.op("scalar", lambda e: e.activation(
                out=statm2[0:1, :], in_=statm2[0:1, :], func=AF.Sqrt),
                sem=s_s)
            P.wait("vector", s_s)
            P.op("vector", lambda e: e.reciprocal(
                out=statm2[0:1, :], in_=statm2[0:1, :]))
            P.drain("vector")
            P.op("vector", lambda e: e.tensor_copy(
                out=statc[0:1, :], in_=statm2[0:1, :]))
            P.drain("vector", s_v)
            P.wait("tensor", s_v)
            P.op("tensor", lambda e: e.matmul(
                pR[:, 0:c.TC], onesb[0:1, :], statc[0:1, :],
                start=True, stop=True), sem=s_pe)
            P.wait("vector", s_pe)
            P.op("vector", lambda e: e.tensor_copy(
                out=rstd_rep[:], in_=pR[:, 0:c.TC]))
            P.drain("vector")
            for cj in range(c.CJ):
                P.op("vector", lambda e, cj=cj, Tsl=Tsl:
                     e.scalar_tensor_tensor(
                         out=kvT[:, cj, Tsl], in0=kvlat[:, cj, :],
                         scalar=kvnwb[:, cj:cj + 1], in1=rstd_rep[:],
                         op0=ALU.mult, op1=ALU.mult))
            P.drain("vector", s_v)
            slot_ck = [0, 0, 0]
            P.wait("tensor", s_v)
            for cj in range(c.CJ):
                for b in range(c.RPC):
                    mm_transpose(
                        kvtok[:, qc * c.RPC + b, cj * 128:(cj + 1) * 128],
                        kvT[:, cj, T0 + b * 128:T0 + (b + 1) * 128],
                        slot_ck, cj * c.RPC + b)
            head_ck = []
            for h in range(c.NH):
                parh = h % 2
                if h == 0:
                    wstream_barrier()
                if h >= 2:
                    P.wait("sync", s_pe, head_ck[h - 1])
                P.dma("sync", s_wa[parh], wqh_b[:, parh], wqa_s[h])
                P.dma("sync", s_wg[parh], wkbk_b[:, parh], wkbk_s[h])
                P.dma("sync", s_wu[parh], wkbv_b[:, parh], wkbv_s[h])
                P.wait("tensor", s_wa[parh])
                P.wait("tensor", s_v)
                for kt in range(c.KT):
                    P.op("tensor", lambda e, parh=parh, kt=kt: e.matmul(
                        pD0[:, 0:c.TC], wqh_b[:, parh, kt, 0:c.NOPE],
                        xn[:, kt, :], start=(kt == 0),
                        stop=(kt == c.KT - 1)))
                for kt in range(c.KT):
                    P.op("tensor", lambda e, parh=parh, kt=kt: e.matmul(
                        pD1[0:c.ROPE, 0:c.TC],
                        wqh_b[:, parh, kt, c.NOPE:c.QK],
                        xn[:, kt, :], start=(kt == 0),
                        stop=(kt == c.KT - 1)),
                        sem=(s_pe if kt == c.KT - 1 else None))
                head_ck.append(s_pe.n)
                P.wait("vector", s_pe)
                P.op("vector", lambda e: e.tensor_copy(
                    out=qn_h[:], in_=pD0[:, 0:c.TC]))
                P.drain("vector", s_v)
                rope_apply(qpe_h[0:c.ROPE // 2, :],
                           qpe_h[c.ROPE // 2:c.ROPE, :], pD1, Tsl)
                P.wait("tensor", s_v)
                P.wait("tensor", s_wg[parh])
                for cj in range(c.CJ):
                    if cj >= 2:
                        P.wait("tensor", s_v)
                    P.op("tensor", lambda e, cj=cj, parh=parh: e.matmul(
                        pD[cj % 2][:, 0:c.TC], wkbk_b[
                            :, parh, cj * 128:(cj + 1) * 128], qn_h[:],
                        start=True, stop=True), sem=s_pe)
                    P.wait("vector", s_pe)
                    P.op("vector", lambda e, cj=cj: e.tensor_copy(
                        out=qlatT[:, cj, :], in_=pD[cj % 2][:, 0:c.TC]),
                        sem=s_v)
                def emit_row_tr(r, R, rp, div_ck):
                    P.wait("tensor", s_v, div_ck)
                    for j0 in range(0, R + 1, 4):
                        jn = min(4, R + 1 - j0)
                        sl = (j0 // 4) % 3
                        bank = tbank_list[sl]
                        P.wait("tensor", s_v, pt_slot_ck[sl])
                        for ji in range(jn):
                            P.op("tensor", lambda e, b=bank, j=j0 + ji,
                                 ji=ji, rp=rp: e.matmul(
                                     b[:, ji * 128:(ji + 1) * 128],
                                     prow[:, rp, j * 128:(j + 1) * 128],
                                     identb[:], start=True, stop=True),
                                 sem=(s_pe if ji == jn - 1 else None))
                        P.wait("vector", s_pe)
                        P.op("vector", lambda e, b=bank, j0=j0, jn=jn, r=r:
                             e.tensor_copy(
                                 out=pT[:, j0:j0 + jn,
                                        r * 128:(r + 1) * 128],
                                 in_=b[:, 0:jn * 128]), sem=s_v)
                        pt_slot_ck[sl] = s_v.n

                pending = None
                for r in range(c.RPC):
                    R = qc * c.RPC + r
                    ncc = qc + 1
                    rsl = slice(r * 128, (r + 1) * 128)
                    rp = r % 2
                    P.wait("tensor", s_v)
                    P.wait("tensor", s_s)
                    for cc in range(ncc):
                        ksl = slice(cc * c.TC, (cc + 1) * c.TC)
                        for cj in range(c.CJ):
                            P.op("tensor", lambda e, cc=cc, cj=cj, rsl=rsl,
                                 ksl=ksl: e.matmul(
                                     pS[cc][:, 0:c.TC], qlatT[:, cj, rsl],
                                     kvT[:, cj, ksl], start=(cj == 0),
                                     stop=False))
                        P.op("tensor", lambda e, cc=cc, rsl=rsl, ksl=ksl:
                             e.matmul(pS[cc][:, 0:c.TC], qpe_h[:, rsl],
                                      kpeT[:, ksl], start=False, stop=True),
                             sem=s_pe)
                    if pending is not None:
                        emit_row_tr(*pending)
                    P.wait("vector", s_pe)
                    P.op("vector", lambda e, r=r, qc=qc: e.tensor_tensor(
                        out=pS[qc][:, r * 128:c.TC],
                        in0=pS[qc][:, r * 128:c.TC],
                        in1=maskb[:, 0:c.TC - r * 128], op=ALU.add))
                    P.drain("vector", s_v)
                    P.wait("scalar", s_v)
                    for cc in range(ncc):
                        P.op("scalar", lambda e, cc=cc, rp=rp: e.activation(
                            out=prow[:, rp, cc * c.TC:(cc + 1) * c.TC],
                            in_=pS[cc][:, 0:c.TC], func=AF.Exp,
                            scale=c.SCALE,
                            accum_out=psum_r[:, cc:cc + 1]),
                            sem=(s_s if cc == ncc - 1 else None))
                    P.wait("vector", s_s)
                    if ncc > 1:
                        P.op("vector", lambda e, ncc=ncc: e.reduce_sum(
                            out=rmax[:, 0:1], in_=psum_r[:, 0:ncc],
                            axis=AX.X))
                        P.drain("vector")
                    sum_ap = rmax[:, 0:1] if ncc > 1 else psum_r[:, 0:1]
                    P.op("vector", lambda e, sum_ap=sum_ap: e.reciprocal(
                        out=rmax[:, 4:5], in_=sum_ap))
                    P.drain("vector")
                    P.op("vector", lambda e, rp=rp, w=ncc * c.TC:
                         e.tensor_scalar(
                             out=prow[:, rp, 0:w], in0=prow[:, rp, 0:w],
                             scalar1=rmax[:, 4:5], scalar2=None,
                             op0=ALU.mult))
                    P.drain("vector", s_v)
                    pending = (r, R, rp, s_v.n)
                emit_row_tr(*pending)
                for r in range(c.RPC):
                    R = qc * c.RPC + r
                    for j in range(R + 1, (qc + 1) * c.RPC):
                        P.op("vector", lambda e, j=j, r=r: e.memset(
                            pT[:, j, r * 128:(r + 1) * 128], 0.0))
                P.drain("vector", s_v)
                P.wait("tensor", s_v)
                P.wait("tensor", s_s)
                NJT = (qc + 1) * c.RPC
                for cj in range(c.CJ):
                    for j in range(NJT):
                        P.op("tensor", lambda e, cj=cj, j=j, NJT=NJT:
                             e.matmul(
                                 pS[cj][:, 0:c.TC],
                                 kvtok[:, j, cj * 128:(cj + 1) * 128],
                                 pT[:, j, :], start=(j == 0),
                                 stop=(j == NJT - 1)),
                             sem=(s_pe if j == NJT - 1 else None))
                    P.wait("vector", s_pe)
                    P.op("vector", lambda e, cj=cj: e.tensor_copy(
                        out=outlatT[:, cj, :], in_=pS[cj][:, 0:c.TC]),
                        sem=s_v)
                P.wait("tensor", s_v)
                P.wait("tensor", s_wu[parh])
                for cj in range(c.CJ):
                    P.op("tensor", lambda e, cj=cj, parh=parh: e.matmul(
                        pR[0:c.VD, 0:c.TC],
                        wkbv_b[:, parh, cj * c.VD:(cj + 1) * c.VD],
                        outlatT[:, cj, :], start=(cj == 0),
                        stop=(cj == c.CJ - 1)),
                        sem=(s_pe if cj == c.CJ - 1 else None))
                P.wait("vector", s_pe)
                P.op("vector", lambda e, h=h: e.tensor_copy(
                    out=headso[:, h, :], in_=pR[0:c.VD, 0:c.TC]), sem=s_v)
            wo_ck = []
            for ft in range(c.HT):
                par = ft % 2
                if ft == 0:
                    wstream_barrier()
                if ft >= 2:
                    P.wait("sync", s_pe, wo_ck[ft - 1])
                P.dma("sync", s_wb[par], wo_b[:, par], wo_s[ft])
                P.wait("tensor", s_wb[par])
                P.wait("tensor", s_v)
                for ht in range(c.NHT):
                    P.op("tensor", lambda e, par=par, ht=ht: e.matmul(
                        pD[par][:, 0:c.TC], wo_b[:, par, ht, :],
                        headso[:, ht, :], start=(ht == 0),
                        stop=(ht == c.NHT - 1)),
                        sem=(s_pe if ht == c.NHT - 1 else None))
                wo_ck.append(s_pe.n)
                P.wait("vector", s_pe)
                P.op("vector", lambda e, par=par, ft=ft, T0=T0:
                     e.tensor_tensor(
                         out=stream[:, ft, T0:T0 + c.TC],
                         in0=stream[:, ft, T0:T0 + c.TC],
                         in1=pD[par][:, 0:c.TC], op=ALU.add), sem=s_v)

    def store_stream(sseq):
        for ch in range(c.NCH):
            T0 = ch * c.TC
            for ht in range(c.HT):
                rp = ht % 2
                P.wait("vector", s_pe)
                P.wait("vector", s_s)
                P.op("vector", lambda e, ht=ht, T0=T0, rp=rp: e.tensor_copy(
                    out=res_t[:, rp, :], in_=stream[:, ht, T0:T0 + c.TC]),
                    sem=s_v)
                for b in range(c.RPC):
                    tp = b % 2
                    P.wait("tensor", s_v)
                    P.wait("tensor", s_y[tp])
                    P.op("tensor", lambda e, rp=rp, b=b: e.transpose(
                        pR[:, 0:128], res_t[:, rp, b * 128:(b + 1) * 128],
                        ident32b[:]), sem=s_pe)
                    P.wait("vector", s_pe)
                    P.op("vector", lambda e, tp=tp: e.tensor_copy(
                        out=tstage[:, tp, :], in_=pR[:, 0:128]), sem=s_v)
                    P.wait("scalar", s_v)
                    P.dma("scalar", s_y[tp],
                          y_d[sseq, T0 + b * 128:T0 + (b + 1) * 128,
                              ht * 128:(ht + 1) * 128], tstage[:, tp, :])

    def load_x(sseq):
        slot_ck = [0, 0, 0]
        for tr in range(c.S // 128):
            P.wait("scalar", s_v)
            P.dma("scalar", s_c, xstage[:],
                  x_d[sseq, tr * 128:(tr + 1) * 128])
            if sseq == 0:
                pump(2)
            P.wait("vector", s_c)
            P.wait("vector", s_pe)
            P.op("vector", lambda e: e.tensor_copy(
                out=xstage_b[:], in_=xstage[:]))
            P.drain("vector", s_v)
            P.wait("tensor", s_v)
            for ft in range(NFT):
                mm_transpose(
                    stream[:, ft, tr * 128:(tr + 1) * 128],
                    xstage_b[:, ft * 128:(ft + 1) * 128], slot_ck, ft)

    P.wait("scalar", s_c)
    for sseq in range(c.NSEQ):
        load_x(sseq)
        for pi_, ph in enumerate(phases):
            last = pi_ == len(phases) - 1
            if ph in ("ffn1", "ffn2"):
                ffn(sseq, final=last)
            elif ph == "attn":
                attention(sseq)
                if last:
                    store_stream(sseq)
            elif ph == "none":
                store_stream(sseq)
    for _ in bg:
        pass
    for tp in range(4):
        P.wait("sync", s_y[tp])
    P.wait("sync", s_c)

    with nc.Block() as block:
        @block.sync
        def _(eng):
            for f in P.ops["sync"]:
                f(eng)

        @block.vector
        def _(eng):
            for f in P.ops["vector"]:
                f(eng)

        @block.scalar
        def _(eng):
            for f in P.ops["scalar"]:
                f(eng)

        @block.tensor
        def _(eng):
            for f in P.ops["tensor"]:
                f(eng)

        @block.gpsimd
        def _(eng):
            for f in P.ops["gpsimd"]:
                f(eng)

    return nc, st



def kernel(**inputs):
    cfg = Cfg()
    xp, shared = host_prepare(cfg, **inputs)
    nc, st = build_nc(cfg, drains=True)
    in_maps = []
    for core in range(cfg.NCORE):
        m = dict(x2=np.ascontiguousarray(
            xp[core * cfg.NSEQ:(core + 1) * cfg.NSEQ]))
        m.update(shared)
        in_maps.append(m)
    res = run_bass_kernel_spmd(nc, in_maps, core_ids=list(range(cfg.NCORE)))
    st.close()
    inv = np.argsort(_deint(cfg.D))
    y = np.concatenate([r["y"] for r in res.results], axis=0)
    return np.ascontiguousarray(y[:, :, inv])


# version 21 (final)
# speedup vs baseline: 1.1249x; 1.0780x over previous
import contextlib
import os

os.environ.setdefault("GOMEMLIMIT", "26GiB")

import numpy as np
import ml_dtypes

import concourse.bass as bass
import concourse.mybir as mybir
from concourse.bass_utils import run_bass_kernel_spmd
from concourse.alu_op_type import AluOpType as ALU

AF = mybir.ActivationFunctionType
AX = mybir.AxisListType
F32 = mybir.dt.float32
BF16 = mybir.dt.bfloat16
U8 = mybir.dt.uint8
BF = ml_dtypes.bfloat16


class Cfg:
    def __init__(self, S=2048, D=2048, NH=16, RANK=512, NOPE=128, ROPE=64,
                 VD=128, MLP=16384, TC=512, NSEQ=2, NCORE=8):
        self.S, self.D, self.NH, self.RANK = S, D, NH, RANK
        self.NOPE, self.ROPE, self.VD, self.MLP = NOPE, ROPE, VD, MLP
        self.TC, self.NSEQ, self.NCORE = TC, NSEQ, NCORE
        self.KT = D // 128
        self.NMB = MLP // 128
        self.NQ = 4
        self.MBQ = self.NMB // self.NQ
        self.NCH = S // TC
        self.RPC = TC // 128
        self.HT = D // 128
        self.CJ = RANK // 128
        self.NHT = NH * VD // 128
        self.QK = NOPE + ROPE
        self.SCALE = self.QK ** -0.5
        self.THETA = 10000.0
        self.DC = min(2048, MLP)
        assert TC % 128 == 0 and S % TC == 0 and D % 128 == 0
        assert MLP % (128 * self.NQ) == 0 and RANK % 128 == 0



def _deint(n):
    return np.concatenate([np.arange(0, n, 2), np.arange(1, n, 2)])


def host_prepare(cfg, L0_attn_x, L0_attn_wq, L0_attn_wkv_a, L0_attn_wkv_b,
                 L0_attn_wo, L0_attn_kv_norm_w, L0_ffn_gate_W_packed,
                 L0_ffn_gate_scales, L0_ffn_up_W_packed, L0_ffn_up_scales,
                 L0_ffn_down_W_packed, L0_ffn_down_scales):
    c = cfg
    perm_k = _deint(c.D)
    perm_m = _deint(c.MLP)

    x = np.asarray(L0_attn_x, dtype=np.float32)
    xp = np.ascontiguousarray(x[:, :, perm_k])

    wq = np.asarray(L0_attn_wq, dtype=np.float32).reshape(c.NH, c.QK, c.D)
    wq_r = np.concatenate(
        [wq[:, :c.NOPE], wq[:, c.NOPE::2], wq[:, c.NOPE + 1::2]], axis=1)
    wq_t = np.ascontiguousarray(wq_r[:, :, perm_k].transpose(0, 2, 1))

    wkv_a = np.asarray(L0_attn_wkv_a, dtype=np.float32)
    wkv_a_r = np.concatenate(
        [wkv_a[:c.RANK], wkv_a[c.RANK::2], wkv_a[c.RANK + 1::2]], axis=0)
    wkv_a_t = np.ascontiguousarray(wkv_a_r[:, perm_k].T)

    wkv_b = np.asarray(L0_attn_wkv_b, dtype=np.float32).reshape(
        c.NH, c.NOPE + c.VD, c.RANK)
    wkvb_k = np.ascontiguousarray(wkv_b[:, :c.NOPE, :])
    wkvb_vT = np.ascontiguousarray(wkv_b[:, c.NOPE:, :].transpose(0, 2, 1))

    wo = np.asarray(L0_attn_wo, dtype=np.float32)
    wo_t = np.ascontiguousarray(wo[perm_k, :].T)

    kvnw = np.ascontiguousarray(
        np.asarray(L0_attn_kv_norm_w, dtype=np.float32).reshape(c.CJ, 128).T)

    def pack_t(w_packed, row_perm, col_perm=None):
        a = np.asarray(w_packed)
        assert a.dtype == np.int32
        u = a.view(np.uint8)[:, ::4]
        if row_perm is not None:
            u = u[row_perm]
        if col_perm is not None:
            u = u[col_perm]
        return np.ascontiguousarray(u.T)

    gpk_t = pack_t(L0_ffn_gate_W_packed, perm_m)
    upk_t = pack_t(L0_ffn_up_W_packed, perm_m)
    dpk_t = pack_t(L0_ffn_down_W_packed, None, col_perm=perm_k)

    gsc_t = np.ascontiguousarray(
        np.asarray(L0_ffn_gate_scales, dtype=np.float32)[perm_m].T)
    usc_t = np.ascontiguousarray(
        np.asarray(L0_ffn_up_scales, dtype=np.float32)[perm_m].T)
    dsc_t = np.ascontiguousarray(
        np.asarray(L0_ffn_down_scales, dtype=np.float32)[perm_k].T)

    ident = np.eye(128, dtype=BF)
    ident32 = np.eye(128, dtype=np.float32)
    ones = np.ones((128, 128), dtype=BF)
    masks = np.zeros((128, c.TC), dtype=np.float32)
    for i in range(128):
        masks[i, i + 1:] = -1e30
    masks = masks.astype(BF)
    freqs = 1.0 / c.THETA ** (
        np.arange(0, c.ROPE, 2, dtype=np.float32) / c.ROPE)
    ang = np.outer(np.arange(c.S, dtype=np.float32), freqs)
    cos_t = np.ascontiguousarray(np.cos(ang).T).astype(BF)
    sin_t = np.ascontiguousarray(np.sin(ang).T).astype(BF)

    shared = dict(
        wq_t=wq_t, wkv_a_t=wkv_a_t, wkvb_k=wkvb_k, wkvb_vT=wkvb_vT,
        wo_t=wo_t, kvnw=kvnw, gpk_t=gpk_t, upk_t=upk_t, dpk_t=dpk_t,
        gsc_t=gsc_t, usc_t=usc_t, dsc_t=dsc_t, ident=ident, ident32=ident32,
        ones=ones, masks=masks, cos_t=cos_t, sin_t=sin_t)
    return xp, shared



class Sem:
    def __init__(self, handle, name):
        self.h = handle
        self.name = name
        self.n = 0


class Plan:
    def __init__(self, drains=True):
        self.drains = drains
        self.ops = {k: [] for k in
                    ("sync", "vector", "scalar", "tensor", "gpsimd")}
        self.last_data = {k: False for k in self.ops}
        self.waited = {k: {} for k in self.ops}

    def emit(self, eng, f, data=False):
        self.ops[eng].append(f)
        self.last_data[eng] = data

    def wait(self, eng, sem, val=None):
        v = sem.n if val is None else val
        if v <= self.waited[eng].get(id(sem), 0):
            return
        self.waited[eng][id(sem)] = v
        self.emit(eng, lambda e, s=sem.h, v=v: e.wait_ge(s, v))

    def dma(self, eng, sem, out, in_, inc=16):
        sem.n += inc
        self.emit(eng, lambda e, o=out, i=in_, s=sem.h, k=inc:
                  e.dma_start(out=o, in_=i).then_inc(s, k), data=True)

    def op(self, eng, f, sem=None, inc=1):
        if sem is not None:
            sem.n += inc
            self.emit(eng, lambda e, s=sem.h, k=inc, ff=f:
                      ff(e).then_inc(s, k), data=True)
        else:
            self.emit(eng, f, data=True)

    def drain(self, eng, sem=None, inc=1):
        elide = (not self.drains) and eng == "vector"
        if sem is None:
            if not elide:
                self.emit(eng, lambda e: e.drain())
            return
        sem.n += inc
        if not elide or not self.last_data[eng]:
            self.emit(eng, lambda e, s=sem.h, k=inc: e.drain().then_inc(s, k))
        else:
            last = self.ops[eng][-1]

            def wrapped(e, f=last, s=sem.h, k=inc):
                r = f(e)
                r.then_inc(s, k)
                return r
            self.ops[eng][-1] = wrapped
            self.last_data[eng] = False



def build_nc(cfg, phases=("ffn1", "attn", "ffn2"), drains=True):
    c = cfg
    nc = bass.Bass("TRN2", target_bir_lowering=False, debug=False,
                   detect_race_conditions=drains)
    st = contextlib.ExitStack()

    def din(name, shape, dt):
        return nc.dram_tensor(name, list(shape), dt, kind="ExternalInput").ap()

    x_d = din("x2", (c.NSEQ, c.S, c.D), F32)
    wq_d = din("wq_t", (c.NH, c.D, c.QK), F32)
    wkva_d = din("wkv_a_t", (c.D, c.RANK + c.ROPE), F32)
    wkbk_d = din("wkvb_k", (c.NH, c.NOPE, c.RANK), F32)
    wkbv_d = din("wkvb_vT", (c.NH, c.RANK, c.VD), F32)
    wo_d = din("wo_t", (c.NH * c.VD, c.D), F32)
    kvnw_d = din("kvnw", (128, c.CJ), F32)
    gpk_d = din("gpk_t", (c.D // 2, c.MLP), U8)
    upk_d = din("upk_t", (c.D // 2, c.MLP), U8)
    dpk_d = din("dpk_t", (c.MLP // 2, c.D), U8)
    gsc_d = din("gsc_t", (c.D // 32, c.MLP), F32)
    usc_d = din("usc_t", (c.D // 32, c.MLP), F32)
    dsc_d = din("dsc_t", (c.MLP // 32, c.D), F32)
    iden_d = din("ident", (128, 128), BF16)
    iden32_d = din("ident32", (128, 128), F32)
    ones_d = din("ones", (128, 128), BF16)
    mask_d = din("masks", (128, c.TC), BF16)
    cos_d = din("cos_t", (c.ROPE // 2, c.S), BF16)
    sin_d = din("sin_t", (c.ROPE // 2, c.S), BF16)
    y_d = nc.dram_tensor("y", [c.NSEQ, c.S, c.D], F32,
                         kind="ExternalOutput").ap()

    def scr(name, shape, dt):
        return nc.dram_tensor(name, list(shape), dt, kind="Internal").ap()

    gsc_s = scr("gsc_s", (c.D // 2, c.MLP), F32)
    usc_s = scr("usc_s", (c.D // 2, c.MLP), F32)
    dsc_s = scr("dsc_s", (c.MLP // 2, c.D), F32)
    wg_s = scr("wg_s", (c.NMB, 128, c.KT, 128), BF16)
    wu_s = scr("wu_s", (c.NMB, 128, c.KT, 128), BF16)
    wd_s = scr("wd_s", (c.HT, c.NMB, 128, 128), BF16)
    wqa_s = scr("wqa_s", (c.NH, 128, c.KT, c.QK), BF16)
    wkva_s = scr("wkva_s", (128, c.KT, c.RANK + c.ROPE), BF16)
    wkbk_s = scr("wkbk_s", (c.NH, 128, c.CJ * 128), BF16)
    wkbv_s = scr("wkbv_s", (c.NH, 128, c.CJ * c.VD), BF16)
    wo_s = scr("wo_s", (c.HT, 128, c.NHT, 128), BF16)

    base0 = (nc.sbuf_base + 31) // 32 * 32
    cur = [base0]
    offs = {}

    def _nb(shape, dt):
        n = 1
        for d in shape[1:]:
            n *= d
        sz = {F32: 4, BF16: 2, U8: 1}[dt]
        return ((n * sz + 31) // 32) * 32

    sizes = {}

    def sb(name, shape, dt, at=None, spill=False):
        nb = _nb(shape, dt)
        off = None
        if at is not None:
            host, delta = at
            if spill or delta + nb <= sizes[host]:
                off = offs[host] + delta
        if off is None:
            off = cur[0]
            cur[0] += nb
        offs[name] = off
        sizes[name] = nb
        return nc.alloc_sbuf_tensor_at(name, list(shape), dt, offset=off)

    NFT = c.D // 128
    stream = sb("stream", (128, NFT, c.S), BF16)
    xn = sb("xn", (128, NFT, c.TC), BF16)
    interq = sb("interq", (128, c.MBQ, c.TC), BF16)
    outp = sb("outp", (128, c.HT, c.TC), F32)
    wbuf_g = sb("wbuf_g", (128, 2, 2, c.KT, 128), BF16)
    wbuf_u = sb("wbuf_u", (128, 2, 2, c.KT, 128), BF16)
    gel = sb("gel", (128, 2, c.TC), BF16)
    sqb = sb("sqb", (128, 2, c.TC), BF16)
    mean_rep = sb("mean_rep", (128, c.TC), BF16)
    rstd_rep = sb("rstd_rep", (128, c.TC), BF16)
    stat = sb("stat", (1, c.TC), F32)
    statv = sb("statv", (1, c.TC), F32)
    statb = sb("statb", (1, c.TC), BF16)
    statc = sb("statc", (1, c.TC), BF16)
    res_t = sb("res_t", (128, 2, c.TC), F32)
    tstage = sb("tstage", (128, 4, 128), F32)
    rmax = sb("rmax", (128, 8), F32)
    psum_r = sb("psum_r", (128, 4), F32)
    identb = sb("identb", (128, 128), BF16)
    ident32b = sb("ident32b", (128, 128), F32)
    onesb = sb("onesb", (128, 128), BF16)
    maskb = sb("maskb", (128, c.TC), BF16)
    cosb = sb("cosb", (c.ROPE // 2, c.S), BF16)
    sinb = sb("sinb", (c.ROPE // 2, c.S), BF16)
    kvnwb = sb("kvnwb", (128, c.CJ), F32)
    statm2 = sb("statm2", (1, c.TC), F32, at=("mean_rep", 0))

    dwbuf = sb("dwbuf", (128, 2, c.MBQ, 128), BF16, at=("wbuf_g", 0))
    wc16 = sb("wc16", (128, 2, 2048), BF16, at=("xn", 0))
    DCB = c.DC
    pkb = sb("pkb", (128, 2, DCB), U8, at=("interq", 0))
    scb = sb("scb", (128, 2, DCB), F32, at=("interq", 2 * DCB))
    d_l4f = sb("d_l4f", (128, DCB), BF16, at=("interq", 10 * DCB))
    d_h4f = sb("d_h4f", (128, DCB), BF16, at=("interq", 12 * DCB))
    d_ge8 = sb("d_ge8", (128, DCB), BF16, at=("interq", 14 * DCB))
    d_m = sb("d_m", (128, DCB), BF16, at=("outp", 0))
    d_t2 = sb("d_t2", (128, DCB), BF16, at=("outp", 2 * DCB))
    d_a = sb("d_a", (128, DCB), BF16, at=("outp", 4 * DCB))
    d_mag = sb("d_mag", (128, DCB), BF16, at=("outp", 6 * DCB))
    d_neg = sb("d_neg", (128, DCB), BF16, at=("outp", 8 * DCB))
    d_scb = sb("d_scb", (128, 2, DCB), BF16, at=("outp", 10 * DCB))
    d_h4u = sb("d_h4u", (128, DCB), U8, at=("outp", 14 * DCB))
    d_l4u = sb("d_l4u", (128, DCB), U8, at=("outp", 15 * DCB))
    d_val = sb("d_val", (128, 2, 2, DCB), BF16, at=("wbuf_u", 0))
    d_pl = sb("d_pl", (128, 2, 2, DCB), BF16, at=("wbuf_g", 0))
    xstage = sb("xstage", (128, c.D), F32, at=("interq", 0))
    xstage_b = sb("xstage_b", (128, c.D), BF16, at=("interq", 4 * c.D))
    kvT = sb("kvT", (128, c.CJ, c.S), BF16, at=("interq", 0))
    kvtok = sb("kvtok", (128, c.S // 128, c.RANK), BF16,
               at=("interq", 2 * c.CJ * c.S))
    pT = sb("pT", (128, c.S // 128, c.TC), BF16, at=("outp", 0))
    headso = sb("headso", (128, c.NHT, c.TC), BF16,
                at=("outp", 2 * (c.S // 128) * c.TC))
    kvlat = sb("kvlat", (128, c.CJ, c.TC), BF16, at=("outp", 0))
    prow = sb("prow", (128, 2, c.S), BF16, at=("wbuf_g", 0))
    kpeT = sb("kpeT", (c.ROPE, c.S), BF16, at=("wbuf_g", 4 * c.S))
    qlatT = sb("qlatT", (128, c.CJ, c.TC), BF16, at=("wbuf_g", 6 * c.S))
    outlatT = sb("outlatT", (128, c.CJ, c.TC), BF16,
                 at=("mean_rep", 0), spill=True)
    wqh_b = sb("wqh_b", (128, 2, c.KT, c.QK), BF16, at=("wbuf_u", 0))
    qn_h = sb("qn_h", (128, c.TC), BF16, at=("wbuf_u", 4 * c.KT * c.QK))
    qpe_h = sb("qpe_h", (c.ROPE, c.TC), BF16,
               at=("wbuf_u", 4 * c.KT * c.QK + 2 * c.TC))
    wkbk_b = sb("wkbk_b", (128, 2, c.CJ * 128), BF16,
                at=("wbuf_u", 4 * c.KT * c.QK + 4 * c.TC))
    wkva_b = sb("wkva_b", (128, 2, c.KT, 128), BF16, at=("wqh_b", 0))
    wo_b = sb("wo_b", (128, 2, c.NHT, 128), BF16, at=("wqh_b", 0))
    wkbv_b = sb("wkbv_b", (128, 2, c.CJ * c.VD), BF16, at=("gel", 0))
    ropet = sb("ropet", (c.ROPE // 2, 2, c.TC), F32, at=("res_t", 0))
    nc.bump_sbuf(cur[0] - nc.sbuf_base)
    pG0 = st.enter_context(nc.psum_tensor("pG0", [128, 512], F32))
    pG1 = st.enter_context(nc.psum_tensor("pG1", [128, 512], F32))
    pU0 = st.enter_context(nc.psum_tensor("pU0", [128, 512], F32))
    pU1 = st.enter_context(nc.psum_tensor("pU1", [128, 512], F32))
    pD0 = st.enter_context(nc.psum_tensor("pD0", [128, 512], F32))
    pD1 = st.enter_context(nc.psum_tensor("pD1", [128, 512], F32))
    pR = st.enter_context(nc.psum_tensor("pR", [128, 512], F32))
    pTB = st.enter_context(nc.psum_tensor("pTB", [128, 1024], BF16))
    pG = [pG0, pG1]
    pU = [pU0, pU1]
    pD = [pD0, pD1]
    pS = [pG0, pG1, pU0, pU1]
    tbank_list = [pD0, pD1, pR]

    def sem(name):
        return Sem(st.enter_context(nc.semaphore(name)), name)

    s_c = sem("s_c")
    s_g = sem("s_g")
    s_g2 = sem("s_g2")
    s_g3 = sem("s_g3")
    s_g4 = sem("s_g4")
    tbanks = tbank_list
    gq_sems = [s_g, s_g2, s_g3, s_g4]
    gq_idx = [0]

    def gdma(out, in_):
        s = gq_sems[gq_idx[0] % 4]
        gq_idx[0] += 1
        P.wait("gpsimd", s)
        P.dma("gpsimd", s, out, in_)

    def gq_snapshot():
        return [s.n for s in gq_sems]

    def gq_wait(eng, snap=None):
        for s, v in zip(gq_sems, snap or [s.n for s in gq_sems]):
            P.wait(eng, s, v)
    s_wg = [sem("s_wg0"), sem("s_wg1")]
    s_wu = [sem("s_wu0"), sem("s_wu1")]
    s_wd = [sem("s_wd0"), sem("s_wd1")]
    s_wa = [sem("s_wa0"), sem("s_wa1")]
    s_wb = [sem("s_wb0"), sem("s_wb1")]
    s_pk = [sem("s_pk0"), sem("s_pk1")]
    s_sc = [sem("s_sc0"), sem("s_sc1")]
    s_gp = sem("s_gp")
    s_pe = sem("s_pe")
    s_v = sem("s_v")
    s_s = sem("s_s")
    s_y = [sem("s_y0"), sem("s_y1"), sem("s_y2"), sem("s_y3")]

    P = Plan(drains=drains)

    def wstream_barrier(eng="sync"):
        for s in (s_wg[0], s_wg[1], s_wu[0], s_wu[1], s_wd[0], s_wd[1],
                  s_wa[0], s_wa[1], s_wb[0], s_wb[1]):
            P.wait(eng, s)
        P.wait(eng, s_pe)
        P.wait(eng, s_v)


    def mm_transpose(dst_ap, src_ap, ck, idx):
        sl = idx % 3
        bank = tbanks[sl]
        P.wait("tensor", s_v, ck[sl])
        P.op("tensor", lambda e, b=bank, s=src_ap: e.matmul(
            b[:, 0:128], s, identb[:], start=True, stop=True), sem=s_pe)
        P.wait("vector", s_pe)
        P.op("vector", lambda e, b=bank, d=dst_ap: e.tensor_copy(
            out=d, in_=b[:, 0:128]), sem=s_v)
        ck[sl] = s_v.n

    P.dma("scalar", s_c, identb[:], iden_d[:])
    P.dma("scalar", s_c, ident32b[:], iden32_d[:])
    P.dma("scalar", s_c, onesb[:], ones_d[:])
    P.dma("scalar", s_c, maskb[:], mask_d[:])
    P.dma("scalar", s_c, cosb[:], cos_d[:])
    P.dma("scalar", s_c, sinb[:], sin_d[:])
    P.dma("scalar", s_c, kvnwb[:], kvnw_d[:])

    for (src_, dst) in ((gsc_d, gsc_s), (usc_d, usc_s), (dsc_d, dsc_s)):
        for i in range(16):
            gdma(dst.rearrange("(u k) m -> u k m", k=16)[:, i, :], src_[:])

    cast_jobs = []
    for h in range(c.NH):
        for kt in range(c.KT):
            cast_jobs.append((wq_d[h, kt * 128:(kt + 1) * 128, :],
                              wqa_s[h, :, kt, :], c.QK))
    for kt in range(c.KT):
        cast_jobs.append((wkva_d[kt * 128:(kt + 1) * 128, :],
                          wkva_s[:, kt, :], c.RANK + c.ROPE))
    for h in range(c.NH):
        cast_jobs.append((wkbk_d[h], wkbk_s[h], c.CJ * 128))
        cast_jobs.append((wkbv_d[h].rearrange("(j p) v -> p j v", p=128),
                          wkbv_s[h], c.CJ * c.VD))
    for ft in range(c.HT):
        cast_jobs.append((
            wo_d.rearrange("(t p) f -> p t f", p=128)[
                :, :, ft * 128:(ft + 1) * 128],
            wo_s[ft], c.NHT * 128))
    for j, (src_ap, dst_ap, ncol) in enumerate(cast_jobs):
        par = j % 2
        if j >= 2:
            gq_wait("gpsimd")
        gdma(wc16[:, par, :ncol], src_ap)
        gq_wait("gpsimd")
        gdma(dst_ap, wc16[:, par, :ncol])

    def decode_tile(par, dc):
        pk = pkb[:, par, :dc]
        P.op("vector", lambda e, pk=pk, dc=dc: e.tensor_scalar(
            out=d_l4u[:, :dc], in0=pk, scalar1=15, scalar2=None,
            op0=ALU.bitwise_and))
        P.op("vector", lambda e, pk=pk, dc=dc: e.tensor_scalar(
            out=d_h4u[:, :dc], in0=pk, scalar1=240, scalar2=None,
            op0=ALU.bitwise_and))
        P.drain("vector")
        P.op("vector", lambda e, dc=dc: e.tensor_copy(
            out=d_l4f[:, :dc], in_=d_l4u[:, :dc]))
        P.op("vector", lambda e, dc=dc: e.tensor_copy(
            out=d_h4f[:, :dc], in_=d_h4u[:, :dc]))
        P.op("vector", lambda e, par=par, dc=dc: e.tensor_copy(
            out=d_scb[:, par, :dc], in_=scb[:, par, :dc]))
        P.drain("vector")
        P.op("vector", lambda e, dc=dc: e.tensor_scalar(
            out=d_h4f[:, :dc], in0=d_h4f[:, :dc], scalar1=0.0625,
            scalar2=None, op0=ALU.mult))
        P.drain("vector")
        for (f, pi) in ((d_l4f, 0), (d_h4f, 1)):
            P.op("vector", lambda e, f=f, dc=dc: e.tensor_scalar(
                out=d_ge8[:, :dc], in0=f[:, :dc], scalar1=8.0, scalar2=None,
                op0=ALU.is_ge))
            P.drain("vector")
            P.op("vector", lambda e, dc=dc: e.tensor_scalar(
                out=d_a[:, :dc], in0=d_ge8[:, :dc], scalar1=-8.0,
                scalar2=None, op0=ALU.mult))
            P.drain("vector")
            P.op("vector", lambda e, f=f, dc=dc: e.tensor_tensor(
                out=d_m[:, :dc], in0=f[:, :dc], in1=d_a[:, :dc], op=ALU.add))
            P.drain("vector", s_v)
            P.wait("scalar", s_v)
            P.drain("scalar")
            P.op("scalar", lambda e, dc=dc: e.activation(
                out=d_t2[:, :dc], in_=d_m[:, :dc], func=AF.Copy,
                scale=3.0, bias=-15.0), sem=s_s)
            P.op("vector", lambda e, dc=dc: e.tensor_scalar(
                out=d_a[:, :dc], in0=d_m[:, :dc], scalar1=0.5,
                scalar2=None, op0=ALU.mult))
            P.op("vector", lambda e, dc=dc: e.tensor_scalar(
                out=d_neg[:, :dc], in0=d_m[:, :dc], scalar1=-2.0,
                scalar2=None, op0=ALU.add))
            P.op("vector", lambda e, dc=dc: e.tensor_scalar(
                out=d_ge8[:, :dc], in0=d_ge8[:, :dc], scalar1=-2.0,
                scalar2=None, op0=ALU.mult))
            P.drain("vector")
            P.wait("vector", s_s)
            P.op("vector", lambda e, dc=dc: e.tensor_tensor(
                out=d_a[:, :dc], in0=d_a[:, :dc], in1=d_t2[:, :dc],
                op=ALU.max))
            P.drain("vector")
            P.op("vector", lambda e, dc=dc: e.tensor_tensor(
                out=d_mag[:, :dc], in0=d_neg[:, :dc], in1=d_a[:, :dc],
                op=ALU.max))
            P.drain("vector")
            P.op("vector", lambda e, dc=dc: e.tensor_tensor(
                out=d_neg[:, :dc], in0=d_ge8[:, :dc], in1=d_mag[:, :dc],
                op=ALU.mult))
            P.drain("vector")
            P.op("vector", lambda e, par=par, pi=pi, dc=dc: e.tensor_tensor(
                out=d_val[:, par, pi, :dc], in0=d_mag[:, :dc],
                in1=d_neg[:, :dc], op=ALU.add))
            P.drain("vector", s_v)
            P.wait("gpsimd", s_v)
            P.op("gpsimd", lambda e, par=par, pi=pi, dc=dc: e.tensor_tensor(
                out=d_pl[:, par, pi, :dc], in0=d_val[:, par, pi, :dc],
                in1=d_scb[:, par, :dc], op=ALU.mult))
            P.drain("gpsimd", s_gp)

    v_done = []
    gp_done = []
    g_done = []

    def decode_matrix(pk_src, sc_src, kbrows, ncols, write_plane):
        njt = kbrows // 128
        dc = min(c.DC, ncols)
        ncc = ncols // dc
        for j in range(njt):
            for cc in range(ncc):
                ti = len(v_done)
                par = ti % 2
                if ti >= 2:
                    P.wait("sync", s_v, v_done[ti - 2])
                P.dma("sync", s_pk[par], pkb[:, par, :dc],
                      pk_src[j * 128:(j + 1) * 128, cc * dc:(cc + 1) * dc])
                P.dma("sync", s_sc[par], scb[:, par, :dc],
                      sc_src[j * 128:(j + 1) * 128, cc * dc:(cc + 1) * dc])
                P.wait("vector", s_pk[par])
                P.wait("vector", s_sc[par])
                if ti >= 2:
                    gq_wait("vector", g_done[ti - 2])
                    P.wait("vector", s_gp, gp_done[ti - 2])
                decode_tile(par, dc)
                P.drain("vector", s_v)
                v_done.append(s_v.n)
                write_plane(j, cc, par)
                g_done.append(gq_snapshot())
                gp_done.append(s_gp.n)

    def wg_write(which_s):
        def w(j, cc, par):
            dc = min(c.DC, c.MLP)
            for pi, kt in ((0, j), (1, c.KT // 2 + j)):
                nmb = dc // 128
                gdma(which_s[cc * nmb:(cc + 1) * nmb, :, kt, :].rearrange(
                          "b p m -> p b m"),
                     d_pl[:, par, pi, :dc])
        return w

    def wd_write(j, cc, par):
        dc = min(c.DC, c.D)
        for pi, mb in ((0, j), (1, c.NMB // 2 + j)):
            nht = dc // 128
            gdma(wd_s[cc * nht:(cc + 1) * nht, mb, :, :].rearrange(
                      "t p h -> p t h"),
                 d_pl[:, par, pi, :dc])

    gq_wait("sync")
    decode_matrix(gpk_d, gsc_s, c.D // 2, c.MLP, wg_write(wg_s))
    decode_matrix(upk_d, usc_s, c.D // 2, c.MLP, wg_write(wu_s))
    decode_matrix(dpk_d, dsc_s, c.MLP // 2, c.D, wd_write)
    decode_done = gq_snapshot()

    def layer_norm(T0):
        P.wait("scalar", s_v)
        P.wait("vector", s_pe)
        P.wait("tensor", s_v)
        ln_ck = []
        for ft in range(NFT):
            par = ft % 2
            if ft >= 2:
                P.wait("scalar", s_pe, ln_ck[ft - 2])
            P.op("scalar", lambda e, ft=ft, par=par, T0=T0: e.activation(
                out=sqb[:, par, :], in_=stream[:, ft, T0:T0 + c.TC],
                func=AF.Square), sem=s_s)
            P.wait("tensor", s_s)
            P.op("tensor", lambda e, ft=ft, T0=T0: e.matmul(
                pR[0:1, 0:c.TC], onesb[:, 0:1], stream[:, ft, T0:T0 + c.TC],
                start=(ft == 0), stop=(ft == NFT - 1)))
            P.op("tensor", lambda e, par=par, ft=ft: e.matmul(
                pR[32:33, 0:c.TC], onesb[:, 0:1], sqb[:, par, :],
                start=(ft == 0), stop=(ft == NFT - 1)), sem=s_pe)
            ln_ck.append(s_pe.n)
        P.wait("vector", s_pe)
        P.op("vector", lambda e: e.tensor_scalar(
            out=stat[0:1, :], in0=pR[0:1, 0:c.TC], scalar1=1.0 / c.D,
            scalar2=None, op0=ALU.mult))
        P.op("vector", lambda e: e.tensor_scalar(
            out=statv[0:1, :], in0=pR[32:33, 0:c.TC], scalar1=1.0 / c.D,
            scalar2=None, op0=ALU.mult))
        P.drain("vector")
        P.op("vector", lambda e: e.tensor_tensor(
            out=statm2[0:1, :], in0=stat[0:1, :], in1=stat[0:1, :],
            op=ALU.mult))
        P.drain("vector")
        P.op("vector", lambda e: e.tensor_tensor(
            out=statv[0:1, :], in0=statv[0:1, :], in1=statm2[0:1, :],
            op=ALU.subtract))
        P.drain("vector")
        P.op("vector", lambda e: e.tensor_scalar(
            out=statv[0:1, :], in0=statv[0:1, :], scalar1=1e-5, scalar2=None,
            op0=ALU.add))
        P.drain("vector", s_v)
        P.wait("scalar", s_v)
        P.op("scalar", lambda e: e.activation(
            out=statv[0:1, :], in_=statv[0:1, :], func=AF.Sqrt), sem=s_s)
        P.wait("vector", s_s)
        P.op("vector", lambda e: e.reciprocal(
            out=statv[0:1, :], in_=statv[0:1, :]))
        P.drain("vector")
        P.op("vector", lambda e: e.tensor_copy(
            out=statb[0:1, :], in_=stat[0:1, :]))
        P.op("vector", lambda e: e.tensor_copy(
            out=statc[0:1, :], in_=statv[0:1, :]))
        P.drain("vector", s_v)
        P.wait("tensor", s_v)
        P.op("tensor", lambda e: e.matmul(
            pR[:, 0:c.TC], onesb[0:1, :], statb[0:1, :],
            start=True, stop=True), sem=s_pe)
        P.wait("vector", s_pe)
        P.op("vector", lambda e: e.tensor_copy(
            out=mean_rep[:], in_=pR[:, 0:c.TC]))
        P.drain("vector", s_v)
        P.wait("tensor", s_v)
        P.op("tensor", lambda e: e.matmul(
            pR[:, 0:c.TC], onesb[0:1, :], statc[0:1, :],
            start=True, stop=True), sem=s_pe)
        P.wait("vector", s_pe)
        P.op("vector", lambda e: e.tensor_copy(
            out=rstd_rep[:], in_=pR[:, 0:c.TC]))
        P.drain("vector")
        for ft in range(NFT):
            P.op("vector", lambda e, ft=ft, T0=T0: e.scalar_tensor_tensor(
                out=xn[:, ft, :], in0=stream[:, ft, T0:T0 + c.TC],
                scalar=1.0, in1=mean_rep[:], op0=ALU.mult, op1=ALU.subtract))
        P.drain("vector")
        for ft in range(NFT):
            P.op("vector", lambda e, ft=ft: e.tensor_tensor(
                out=xn[:, ft, :], in0=xn[:, ft, :], in1=rstd_rep[:],
                op=ALU.mult))
        P.drain("vector", s_v)
        P.wait("tensor", s_v)
        P.wait("scalar", s_v)

    def ffn(sseq, final):
        for ch in range(c.NCH):
            T0 = ch * c.TC
            layer_norm(T0)
            for q in range(c.NQ):
                evac_ck = []
                for g2 in range(c.MBQ // 2):
                    par = g2 % 2
                    mbg0 = q * c.MBQ + g2 * 2
                    if g2 >= 2:
                        P.wait("sync", s_v, evac_ck[2 * (g2 - 2) + 1])
                    if g2 == 0:
                        wstream_barrier()
                    P.dma("sync", s_wg[par], wbuf_g[:, par],
                          wg_s[mbg0:mbg0 + 2].rearrange("b p k m -> p b k m"))
                    P.dma("sync", s_wu[par], wbuf_u[:, par],
                          wu_s[mbg0:mbg0 + 2].rearrange("b p k m -> p b k m"))
                    for bi in range(2):
                        mb = g2 * 2 + bi
                        par2 = mb % 2
                        P.wait("tensor", s_wg[par])
                        P.wait("tensor", s_wu[par])
                        if len(evac_ck) >= 2:
                            P.wait("tensor", s_v, evac_ck[-2])
                        for kt in range(c.KT):
                            P.op("tensor", lambda e, par=par, bi=bi, kt=kt,
                                 par2=par2: e.matmul(
                                     pG[par2][:, 0:c.TC],
                                     wbuf_g[:, par, bi, kt, :],
                                     xn[:, kt, :], start=(kt == 0),
                                     stop=(kt == c.KT - 1)),
                                 sem=(s_pe if kt == c.KT - 1 else None))
                        for kt in range(c.KT):
                            P.op("tensor", lambda e, par=par, bi=bi, kt=kt,
                                 par2=par2: e.matmul(
                                     pU[par2][:, 0:c.TC],
                                     wbuf_u[:, par, bi, kt, :],
                                     xn[:, kt, :], start=(kt == 0),
                                     stop=(kt == c.KT - 1)),
                                 sem=(s_pe if kt == c.KT - 1 else None))
                        P.wait("scalar", s_pe, s_pe.n - 1)
                        P.op("scalar", lambda e, par2=par2: e.activation(
                            out=gel[:, par2, :], in_=pG[par2][:, 0:c.TC],
                            func=AF.Gelu_apprx_tanh), sem=s_s)
                        P.wait("vector", s_s)
                        P.wait("vector", s_pe)
                        P.op("vector", lambda e, mb=mb, par2=par2:
                             e.tensor_tensor(
                                 out=interq[:, mb, :], in0=gel[:, par2, :],
                                 in1=pU[par2][:, 0:c.TC], op=ALU.mult),
                             sem=s_v)
                        evac_ck.append(s_v.n)
                dn_ck = []
                for ht in range(c.HT):
                    par = ht % 2
                    if ht >= 2:
                        P.wait("sync", s_v, dn_ck[ht - 2])
                    else:
                        P.wait("sync", s_v, evac_ck[-1])
                    P.dma("sync", s_wd[par], dwbuf[:, par],
                          wd_s[ht, q * c.MBQ:(q + 1) * c.MBQ].rearrange(
                              "b p h -> p b h"))
                    P.wait("tensor", s_wd[par])
                    P.wait("tensor", s_v, evac_ck[-1])
                    if ht >= 2:
                        P.wait("tensor", s_v, dn_ck[ht - 2])
                    for mb in range(c.MBQ):
                        P.op("tensor", lambda e, par=par, mb=mb: e.matmul(
                            pD[par][:, 0:c.TC], dwbuf[:, par, mb, :],
                            interq[:, mb, :], start=(mb == 0),
                            stop=(mb == c.MBQ - 1)),
                            sem=(s_pe if mb == c.MBQ - 1 else None))
                    P.wait("vector", s_pe)
                    if q == 0:
                        P.op("vector", lambda e, par=par, ht=ht:
                             e.tensor_copy(out=outp[:, ht, :],
                                           in_=pD[par][:, 0:c.TC]), sem=s_v)
                    elif q < c.NQ - 1:
                        P.op("vector", lambda e, par=par, ht=ht:
                             e.tensor_tensor(
                                 out=outp[:, ht, :], in0=outp[:, ht, :],
                                 in1=pD[par][:, 0:c.TC], op=ALU.add),
                             sem=s_v)
                    else:
                        P.op("vector", lambda e, par=par, ht=ht:
                             e.tensor_tensor(
                                 out=outp[:, ht, :], in0=outp[:, ht, :],
                                 in1=pD[par][:, 0:c.TC], op=ALU.add))
                        P.drain("vector")
                        if not final:
                            P.op("vector", lambda e, ht=ht, T0=T0:
                                 e.scalar_tensor_tensor(
                                     out=stream[:, ht, T0:T0 + c.TC],
                                     in0=outp[:, ht, :], scalar=0.5,
                                     in1=stream[:, ht, T0:T0 + c.TC],
                                     op0=ALU.mult, op1=ALU.add), sem=s_v)
                        else:
                            rp = ht % 2
                            P.op("vector", lambda e, ht=ht, T0=T0, rp=rp:
                                 e.scalar_tensor_tensor(
                                     out=res_t[:, rp, :],
                                     in0=outp[:, ht, :], scalar=0.5,
                                     in1=stream[:, ht, T0:T0 + c.TC],
                                     op0=ALU.mult, op1=ALU.add), sem=s_v)
                            for b in range(c.RPC):
                                tp = b % 4
                                P.wait("tensor", s_v)
                                P.wait("tensor", s_y[tp])
                                P.op("tensor", lambda e, rp=rp, b=b:
                                     e.transpose(
                                         pR[:, 0:128],
                                         res_t[:, rp, b * 128:(b + 1) * 128],
                                         ident32b[:]), sem=s_pe)
                                P.wait("vector", s_pe)
                                P.op("vector", lambda e, tp=tp:
                                     e.tensor_copy(out=tstage[:, tp, :],
                                                   in_=pR[:, 0:128]),
                                     sem=s_v)
                                P.wait("scalar", s_v)
                                P.dma("scalar", s_y[tp],
                                      y_d[sseq,
                                          T0 + b * 128:T0 + (b + 1) * 128,
                                          ht * 128:(ht + 1) * 128],
                                      tstage[:, tp, :])
                    dn_ck.append(s_v.n)

    def rope_apply(dst0, dst1, src_ps, Tsl):
        H = c.ROPE // 2
        cs = cosb[:, Tsl]
        sn = sinb[:, Tsl]
        P.op("vector", lambda e: e.tensor_tensor(
            out=ropet[:, 0, :], in0=src_ps[0:H, 0:c.TC], in1=cs, op=ALU.mult))
        P.op("vector", lambda e: e.tensor_tensor(
            out=ropet[:, 1, :], in0=src_ps[H:2 * H, 0:c.TC], in1=sn,
            op=ALU.mult))
        P.drain("vector")
        P.op("vector", lambda e, d=dst0: e.tensor_tensor(
            out=d, in0=ropet[:, 0, :], in1=ropet[:, 1, :], op=ALU.subtract))
        P.drain("vector")
        P.op("vector", lambda e: e.tensor_tensor(
            out=ropet[:, 0, :], in0=src_ps[0:H, 0:c.TC], in1=sn, op=ALU.mult))
        P.op("vector", lambda e: e.tensor_tensor(
            out=ropet[:, 1, :], in0=src_ps[H:2 * H, 0:c.TC], in1=cs,
            op=ALU.mult))
        P.drain("vector")
        P.op("vector", lambda e, d=dst1: e.tensor_tensor(
            out=d, in0=ropet[:, 0, :], in1=ropet[:, 1, :], op=ALU.add))
        P.drain("vector", s_v)

    def attention(sseq):
        pt_slot_ck = [0, 0, 0]
        for qc in range(c.NCH):
            T0 = qc * c.TC
            Tsl = slice(T0, T0 + c.TC)
            layer_norm(T0)
            wstream_barrier()
            kv_ck = []
            for rt in range(c.CJ + 1):
                par = rt % 2
                ncol = 128 if rt < c.CJ else c.ROPE
                if rt >= 2:
                    P.wait("sync", s_pe, kv_ck[rt - 1])
                P.dma("sync", s_wa[par], wkva_b[:, par, :, 0:ncol],
                      wkva_s[:, :, rt * 128:rt * 128 + ncol])
                P.wait("tensor", s_wa[par])
                if rt >= 2:
                    P.wait("tensor", s_v)
                    P.wait("tensor", s_s)
                for kt in range(c.KT):
                    P.op("tensor", lambda e, par=par, kt=kt, nc_=ncol:
                         e.matmul(pD[par][0:nc_, 0:c.TC],
                                  wkva_b[:, par, kt, 0:nc_],
                                  xn[:, kt, :], start=(kt == 0),
                                  stop=(kt == c.KT - 1)),
                         sem=(s_pe if kt == c.KT - 1 else None))
                kv_ck.append(s_pe.n)
                if rt < c.CJ:
                    P.wait("vector", s_pe)
                    P.op("vector", lambda e, par=par, rt=rt: e.tensor_copy(
                        out=kvlat[:, rt, :], in_=pD[par][:, 0:c.TC]))
                    P.drain("vector", s_v)
                    P.wait("scalar", s_v)
                    P.op("scalar", lambda e, par=par, rt=rt: e.activation(
                        out=sqb[:, rt % 2, :], in_=pD[par][:, 0:c.TC],
                        func=AF.Square), sem=s_s)
                    P.wait("tensor", s_s)
                    P.op("tensor", lambda e, rt=rt: e.matmul(
                        pR[0:1, 0:c.TC], onesb[:, 0:1], sqb[:, rt % 2, :],
                        start=(rt == 0), stop=(rt == c.CJ - 1)), sem=s_pe)
                else:
                    P.wait("vector", s_pe)
                    rope_apply(kpeT[0:c.ROPE // 2, Tsl],
                               kpeT[c.ROPE // 2:c.ROPE, Tsl], pD[par], Tsl)
            P.wait("vector", s_pe)
            P.op("vector", lambda e: e.tensor_scalar(
                out=statv[0:1, :], in0=pR[0:1, 0:c.TC],
                scalar1=1.0 / c.RANK, scalar2=1e-6, op0=ALU.mult,
                op1=ALU.add))
            P.drain("vector", s_v)
            P.wait("scalar", s_v)
            P.op("scalar", lambda e: e.activation(
                out=statv[0:1, :], in_=statv[0:1, :], func=AF.Sqrt), sem=s_s)
            P.wait("vector", s_s)
            P.op("vector", lambda e: e.reciprocal(
                out=statv[0:1, :], in_=statv[0:1, :]))
            P.drain("vector")
            P.op("vector", lambda e: e.tensor_copy(
                out=statc[0:1, :], in_=statv[0:1, :]))
            P.drain("vector", s_v)
            P.wait("tensor", s_v)
            P.op("tensor", lambda e: e.matmul(
                pR[:, 0:c.TC], onesb[0:1, :], statc[0:1, :],
                start=True, stop=True), sem=s_pe)
            P.wait("vector", s_pe)
            P.op("vector", lambda e: e.tensor_copy(
                out=rstd_rep[:], in_=pR[:, 0:c.TC]))
            P.drain("vector")
            for cj in range(c.CJ):
                P.op("vector", lambda e, cj=cj, Tsl=Tsl:
                     e.scalar_tensor_tensor(
                         out=kvT[:, cj, Tsl], in0=kvlat[:, cj, :],
                         scalar=kvnwb[:, cj:cj + 1], in1=rstd_rep[:],
                         op0=ALU.mult, op1=ALU.mult))
            P.drain("vector", s_v)
            slot_ck = [0, 0, 0]
            P.wait("tensor", s_v)
            for cj in range(c.CJ):
                for b in range(c.RPC):
                    mm_transpose(
                        kvtok[:, qc * c.RPC + b, cj * 128:(cj + 1) * 128],
                        kvT[:, cj, T0 + b * 128:T0 + (b + 1) * 128],
                        slot_ck, cj * c.RPC + b)
            head_ck = []
            for h in range(c.NH):
                parh = h % 2
                if h == 0:
                    wstream_barrier()
                if h >= 2:
                    P.wait("sync", s_pe, head_ck[h - 1])
                P.dma("sync", s_wa[parh], wqh_b[:, parh], wqa_s[h])
                P.dma("sync", s_wg[parh], wkbk_b[:, parh], wkbk_s[h])
                P.dma("sync", s_wu[parh], wkbv_b[:, parh], wkbv_s[h])
                P.wait("tensor", s_wa[parh])
                P.wait("tensor", s_v)
                for kt in range(c.KT):
                    P.op("tensor", lambda e, parh=parh, kt=kt: e.matmul(
                        pD0[:, 0:c.TC], wqh_b[:, parh, kt, 0:c.NOPE],
                        xn[:, kt, :], start=(kt == 0),
                        stop=(kt == c.KT - 1)))
                for kt in range(c.KT):
                    P.op("tensor", lambda e, parh=parh, kt=kt: e.matmul(
                        pD1[0:c.ROPE, 0:c.TC],
                        wqh_b[:, parh, kt, c.NOPE:c.QK],
                        xn[:, kt, :], start=(kt == 0),
                        stop=(kt == c.KT - 1)),
                        sem=(s_pe if kt == c.KT - 1 else None))
                head_ck.append(s_pe.n)
                P.wait("vector", s_pe)
                P.op("vector", lambda e: e.tensor_copy(
                    out=qn_h[:], in_=pD0[:, 0:c.TC]))
                P.drain("vector", s_v)
                rope_apply(qpe_h[0:c.ROPE // 2, :],
                           qpe_h[c.ROPE // 2:c.ROPE, :], pD1, Tsl)
                P.wait("tensor", s_v)
                P.wait("tensor", s_wg[parh])
                for cj in range(c.CJ):
                    if cj >= 2:
                        P.wait("tensor", s_v)
                    P.op("tensor", lambda e, cj=cj, parh=parh: e.matmul(
                        pD[cj % 2][:, 0:c.TC], wkbk_b[
                            :, parh, cj * 128:(cj + 1) * 128], qn_h[:],
                        start=True, stop=True), sem=s_pe)
                    P.wait("vector", s_pe)
                    P.op("vector", lambda e, cj=cj: e.tensor_copy(
                        out=qlatT[:, cj, :], in_=pD[cj % 2][:, 0:c.TC]),
                        sem=s_v)
                def emit_row_tr(r, R, rp, div_ck):
                    P.wait("tensor", s_v, div_ck)
                    for j in range(R + 1):
                        mm_transpose(
                            pT[:, j, r * 128:(r + 1) * 128],
                            prow[:, rp, j * 128:(j + 1) * 128],
                            pt_slot_ck, j)

                pending = None
                for r in range(c.RPC):
                    R = qc * c.RPC + r
                    ncc = qc + 1
                    rsl = slice(r * 128, (r + 1) * 128)
                    rp = r % 2
                    P.wait("tensor", s_v)
                    P.wait("tensor", s_s)
                    for cc in range(ncc):
                        ksl = slice(cc * c.TC, (cc + 1) * c.TC)
                        for cj in range(c.CJ):
                            P.op("tensor", lambda e, cc=cc, cj=cj, rsl=rsl,
                                 ksl=ksl: e.matmul(
                                     pS[cc][:, 0:c.TC], qlatT[:, cj, rsl],
                                     kvT[:, cj, ksl], start=(cj == 0),
                                     stop=False))
                        P.op("tensor", lambda e, cc=cc, rsl=rsl, ksl=ksl:
                             e.matmul(pS[cc][:, 0:c.TC], qpe_h[:, rsl],
                                      kpeT[:, ksl], start=False, stop=True),
                             sem=s_pe)
                    if pending is not None:
                        emit_row_tr(*pending)
                    P.wait("vector", s_pe)
                    P.op("vector", lambda e, r=r, qc=qc: e.tensor_tensor(
                        out=pS[qc][:, r * 128:c.TC],
                        in0=pS[qc][:, r * 128:c.TC],
                        in1=maskb[:, 0:c.TC - r * 128], op=ALU.add))
                    P.drain("vector", s_v)
                    P.wait("scalar", s_v)
                    for cc in range(ncc):
                        P.op("scalar", lambda e, cc=cc, rp=rp: e.activation(
                            out=prow[:, rp, cc * c.TC:(cc + 1) * c.TC],
                            in_=pS[cc][:, 0:c.TC], func=AF.Exp,
                            scale=c.SCALE,
                            accum_out=psum_r[:, cc:cc + 1]),
                            sem=(s_s if cc == ncc - 1 else None))
                    P.wait("vector", s_s)
                    if ncc > 1:
                        P.op("vector", lambda e, ncc=ncc: e.reduce_sum(
                            out=rmax[:, 0:1], in_=psum_r[:, 0:ncc],
                            axis=AX.X))
                        P.drain("vector")
                    sum_ap = rmax[:, 0:1] if ncc > 1 else psum_r[:, 0:1]
                    P.op("vector", lambda e, sum_ap=sum_ap: e.reciprocal(
                        out=rmax[:, 4:5], in_=sum_ap))
                    P.drain("vector")
                    P.op("vector", lambda e, rp=rp, w=ncc * c.TC:
                         e.tensor_scalar(
                             out=prow[:, rp, 0:w], in0=prow[:, rp, 0:w],
                             scalar1=rmax[:, 4:5], scalar2=None,
                             op0=ALU.mult))
                    P.drain("vector", s_v)
                    pending = (r, R, rp, s_v.n)
                emit_row_tr(*pending)
                for r in range(c.RPC):
                    R = qc * c.RPC + r
                    for j in range(R + 1, (qc + 1) * c.RPC):
                        P.op("vector", lambda e, j=j, r=r: e.memset(
                            pT[:, j, r * 128:(r + 1) * 128], 0.0))
                P.drain("vector", s_v)
                P.wait("tensor", s_v)
                P.wait("tensor", s_s)
                NJT = (qc + 1) * c.RPC
                for cj in range(c.CJ):
                    for j in range(NJT):
                        P.op("tensor", lambda e, cj=cj, j=j, NJT=NJT:
                             e.matmul(
                                 pS[cj][:, 0:c.TC],
                                 kvtok[:, j, cj * 128:(cj + 1) * 128],
                                 pT[:, j, :], start=(j == 0),
                                 stop=(j == NJT - 1)),
                             sem=(s_pe if j == NJT - 1 else None))
                    P.wait("vector", s_pe)
                    P.op("vector", lambda e, cj=cj: e.tensor_copy(
                        out=outlatT[:, cj, :], in_=pS[cj][:, 0:c.TC]),
                        sem=s_v)
                P.wait("tensor", s_v)
                P.wait("tensor", s_wu[parh])
                for cj in range(c.CJ):
                    P.op("tensor", lambda e, cj=cj, parh=parh: e.matmul(
                        pR[0:c.VD, 0:c.TC],
                        wkbv_b[:, parh, cj * c.VD:(cj + 1) * c.VD],
                        outlatT[:, cj, :], start=(cj == 0),
                        stop=(cj == c.CJ - 1)),
                        sem=(s_pe if cj == c.CJ - 1 else None))
                P.wait("vector", s_pe)
                P.op("vector", lambda e, h=h: e.tensor_copy(
                    out=headso[:, h, :], in_=pR[0:c.VD, 0:c.TC]), sem=s_v)
            wo_ck = []
            for ft in range(c.HT):
                par = ft % 2
                if ft == 0:
                    wstream_barrier()
                if ft >= 2:
                    P.wait("sync", s_pe, wo_ck[ft - 1])
                P.dma("sync", s_wb[par], wo_b[:, par], wo_s[ft])
                P.wait("tensor", s_wb[par])
                P.wait("tensor", s_v)
                for ht in range(c.NHT):
                    P.op("tensor", lambda e, par=par, ht=ht: e.matmul(
                        pD[par][:, 0:c.TC], wo_b[:, par, ht, :],
                        headso[:, ht, :], start=(ht == 0),
                        stop=(ht == c.NHT - 1)),
                        sem=(s_pe if ht == c.NHT - 1 else None))
                wo_ck.append(s_pe.n)
                P.wait("vector", s_pe)
                P.op("vector", lambda e, par=par, ft=ft, T0=T0:
                     e.tensor_tensor(
                         out=stream[:, ft, T0:T0 + c.TC],
                         in0=stream[:, ft, T0:T0 + c.TC],
                         in1=pD[par][:, 0:c.TC], op=ALU.add), sem=s_v)

    def store_stream(sseq):
        for ch in range(c.NCH):
            T0 = ch * c.TC
            for ht in range(c.HT):
                rp = ht % 2
                P.wait("vector", s_pe)
                P.wait("vector", s_s)
                P.op("vector", lambda e, ht=ht, T0=T0, rp=rp: e.tensor_copy(
                    out=res_t[:, rp, :], in_=stream[:, ht, T0:T0 + c.TC]),
                    sem=s_v)
                for b in range(c.RPC):
                    tp = b % 4
                    P.wait("tensor", s_v)
                    P.wait("tensor", s_y[tp])
                    P.op("tensor", lambda e, rp=rp, b=b: e.transpose(
                        pR[:, 0:128], res_t[:, rp, b * 128:(b + 1) * 128],
                        ident32b[:]), sem=s_pe)
                    P.wait("vector", s_pe)
                    P.op("vector", lambda e, tp=tp: e.tensor_copy(
                        out=tstage[:, tp, :], in_=pR[:, 0:128]), sem=s_v)
                    P.wait("scalar", s_v)
                    P.dma("scalar", s_y[tp],
                          y_d[sseq, T0 + b * 128:T0 + (b + 1) * 128,
                              ht * 128:(ht + 1) * 128], tstage[:, tp, :])

    def load_x(sseq):
        slot_ck = [0, 0, 0]
        for tr in range(c.S // 128):
            P.wait("scalar", s_v)
            P.dma("scalar", s_c, xstage[:],
                  x_d[sseq, tr * 128:(tr + 1) * 128])
            P.wait("vector", s_c)
            P.wait("vector", s_pe)
            P.op("vector", lambda e: e.tensor_copy(
                out=xstage_b[:], in_=xstage[:]))
            P.drain("vector", s_v)
            P.wait("tensor", s_v)
            for ft in range(NFT):
                mm_transpose(
                    stream[:, ft, tr * 128:(tr + 1) * 128],
                    xstage_b[:, ft * 128:(ft + 1) * 128], slot_ck, ft)

    gq_wait("sync", decode_done)
    P.wait("scalar", s_c)
    for sseq in range(c.NSEQ):
        load_x(sseq)
        for pi_, ph in enumerate(phases):
            last = pi_ == len(phases) - 1
            if ph in ("ffn1", "ffn2"):
                ffn(sseq, final=last)
            elif ph == "attn":
                attention(sseq)
                if last:
                    store_stream(sseq)
            elif ph == "none":
                store_stream(sseq)
    for tp in range(4):
        P.wait("sync", s_y[tp])
    P.wait("sync", s_c)

    with nc.Block() as block:
        @block.sync
        def _(eng):
            for f in P.ops["sync"]:
                f(eng)

        @block.vector
        def _(eng):
            for f in P.ops["vector"]:
                f(eng)

        @block.scalar
        def _(eng):
            for f in P.ops["scalar"]:
                f(eng)

        @block.tensor
        def _(eng):
            for f in P.ops["tensor"]:
                f(eng)

        @block.gpsimd
        def _(eng):
            for f in P.ops["gpsimd"]:
                f(eng)

    return nc, st



def kernel(**inputs):
    cfg = Cfg()
    xp, shared = host_prepare(cfg, **inputs)
    nc, st = build_nc(cfg, drains=True)
    in_maps = []
    for core in range(cfg.NCORE):
        m = dict(x2=np.ascontiguousarray(
            xp[core * cfg.NSEQ:(core + 1) * cfg.NSEQ]))
        m.update(shared)
        in_maps.append(m)
    res = run_bass_kernel_spmd(nc, in_maps, core_ids=list(range(cfg.NCORE)))
    st.close()
    inv = np.argsort(_deint(cfg.D))
    y = np.concatenate([r["y"] for r in res.results], axis=0)
    return np.ascontiguousarray(y[:, :, inv])
